# version 01
# baseline (speedup 1.0000x reference)
import contextlib
import math
import numpy as np
import ml_dtypes
import concourse.bass as bass
import concourse.mybir as mybir
from concourse.bass_utils import run_bass_kernel_spmd

F32 = mybir.dt.float32
BF16 = mybir.dt.bfloat16
I32 = mybir.dt.int32
AF = mybir.ActivationFunctionType
ALU = mybir.AluOpType
AX = mybir.AxisListType
BFNP = ml_dtypes.bfloat16

NCORES = 8
B, T, C, H, D = 4, 4096, 1024, 16, 64
TPC = T // NCORES
ROWS = B * TPC
NTILE = ROWS // 128
FCH = C // 128
HID = 4096
HCH = HID // 128
EPS_LN = 1e-5
SEM_ROTATE = 24000



class ASB:

    ENGS = ("tensor", "vector", "scalar", "gpsimd", "sync")

    def __init__(self, name="prog"):
        self.nc = bass.Bass("TRN2", target_bir_lowering=False, debug=False)
        self.st = contextlib.ExitStack()
        self.name = name
        self.ops = []
        self._uid = 0

    def uid(self):
        self._uid += 1
        return self._uid

    def sb(self, name, shape, dtype):
        return self.st.enter_context(self.nc.sbuf_tensor(name, shape, dtype))

    def ps(self, name, shape, dtype=F32):
        return self.st.enter_context(self.nc.psum_tensor(name, shape, dtype))

    def dram_in(self, name, shape, dtype=F32):
        return self.nc.dram_tensor(name, list(shape), dtype, kind="ExternalInput").ap()

    def dram_out(self, name, shape, dtype=F32):
        return self.nc.dram_tensor(name, list(shape), dtype, kind="ExternalOutput").ap()

    def dram_tmp(self, name, shape, dtype=F32):
        return self.nc.dram_tensor(name, list(shape), dtype, kind="Internal").ap()

    def op(self, eng, fn, reads=(), writes=(), dma=False, tag=""):
        self.ops.append(dict(eng=eng, fn=fn, reads=tuple(reads),
                             writes=tuple(writes), dma=dma, tag=tag))
        return len(self.ops) - 1

    def dma(self, queue, out, in_, reads=(), writes=(), tag="", **kw):
        return self.op(queue, lambda e: e.dma_start(out=out, in_=in_, **kw),
                       reads=reads, writes=writes, dma=True, tag=tag)

    def build(self):
        n = len(self.ops)
        last_w = {}
        readers = {}
        deps = [set() for _ in range(n)]
        for i, o in enumerate(self.ops):
            for k in o["reads"]:
                if k in last_w:
                    deps[i].add(last_w[k])
            for k in o["writes"]:
                if k in last_w:
                    deps[i].add(last_w[k])
                for r in readers.get(k, ()):
                    deps[i].add(r)
            for k in o["writes"]:
                last_w[k] = i
                readers[k] = []
            for k in o["reads"]:
                readers.setdefault(k, []).append(i)
        need_inc = set()
        kept = [dict() for _ in range(n)]
        for i in range(n):
            for d in deps[i]:
                if self.ops[d]["dma"]:
                    kept[i][("dma", d)] = d
                else:
                    pe = self.ops[d]["eng"]
                    if kept[i].get(pe, -1) < d:
                        kept[i][pe] = d
        for i in range(n):
            for d in kept[i].values():
                need_inc.add(d)
        for i, o in enumerate(self.ops):
            if o["dma"]:
                need_inc.add(i)
        K = 12
        sems = {}
        sem_pos = {}
        inc_of = {}
        self_waits = {}
        dma_cnt = {}
        for i in range(n):
            if i in need_inc:
                o = self.ops[i]
                e = o["eng"]
                if o["dma"]:
                    cnt = dma_cnt.get(e, 0)
                    dma_cnt[e] = cnt + 1
                    slot = cnt % K
                    name = e + "q"
                    prev = sem_pos.get((name, slot), 0)
                    if prev:
                        self_waits[i] = (name, slot, prev)
                    val = prev + 16
                    sem_pos[(name, slot)] = val
                    inc_of[i] = (name, slot, val)
                else:
                    slot, val = sem_pos.get(e, (0, 0))
                    if val + 1 > SEM_ROTATE:
                        slot, val = slot + 1, 0
                    val += 1
                    sem_pos[e] = (slot, val)
                    inc_of[i] = (e, slot, val)
        waited = {}
        waits = [[] for _ in range(n)]
        for i in range(n):
            we = self.ops[i]["eng"]
            if i in self_waits:
                name, slot, val = self_waits[i]
                key = (we, (name, slot))
                if waited.get(key, 0) < val:
                    waits[i].append((name, slot, val))
                    waited[key] = val
            for d in sorted(kept[i].values()):
                if d not in inc_of:
                    pe = self.ops[d]["eng"]
                    cover = None
                    for j in range(d, n):
                        if j in inc_of and self.ops[j]["eng"] == pe:
                            cover = j
                            break
                    if cover is None:
                        raise RuntimeError(f"op {d} has no coverable inc")
                    raise RuntimeError(
                        f"internal: op {d} ({self.ops[d]['tag']}) should have "
                        f"been marked need_inc")
                pe, slot, val = inc_of[d]
                key = (we, (pe, slot))
                if waited.get(key, 0) >= val:
                    continue
                waits[i].append((pe, slot, val))
                waited[key] = val
        block = self.st.enter_context(self.nc.Block())
        i = 0
        nc = self.nc
        while i < n:
            e = self.ops[i]["eng"]
            j = i
            while j < n and self.ops[j]["eng"] == e:
                j += 1
            lo, hi = i, j

            def body(eng, lo=lo, hi=hi):
                for k in range(lo, hi):
                    for (pe, slot, val) in waits[k]:
                        h = sems.get((pe, slot))
                        if h is None:
                            h = self.st.enter_context(
                                nc.semaphore(f"s_{pe}_{slot}"))
                            sems[(pe, slot)] = h
                        eng.wait_ge(h, val)
                    inst = self.ops[k]["fn"](eng)
                    if k in inc_of:
                        pe, slot, val = inc_of[k]
                        h = sems.get((pe, slot))
                        if h is None:
                            h = self.st.enter_context(
                                nc.semaphore(f"s_{pe}_{slot}"))
                            sems[(pe, slot)] = h
                        inst.then_inc(h, 16 if self.ops[k]["dma"] else 1)
            getattr(block, e)(body)
            i = j
        return self

    def finish(self, keys):
        self.op("sync", lambda e: e.nop(), reads=tuple(keys), tag="finish")

    def close(self):
        self.st.close()



def _shard_rows(x_full):
    out = []
    for c in range(NCORES):
        sl = x_full[:, c * TPC:(c + 1) * TPC, :]
        out.append(np.ascontiguousarray(sl.reshape(ROWS, C), dtype=np.float32))
    return out


def _unshard_rows(shards):
    x = np.empty((B, T, C), np.float32)
    for c, s in enumerate(shards):
        x[:, c * TPC:(c + 1) * TPC, :] = np.asarray(s).reshape(B, TPC, C)
    return x



def emit_load_x(a, xd, xsb, key="x"):
    a.dma("sync", xsb[:, :, :], xd.rearrange("(n p) f -> p n f", p=128),
          writes=[key], tag="load_x")


def emit_ln(a, xsb, scale_col, bias_col, fold=1.0, n_tiles=NTILE, width=C,
            xkey="x", skey="lnsb"):
    u = a.uid()
    nseg = width // 512
    bn6 = a.sb(f"bn6_{u}", [128, n_tiles * nseg * 6], F32)
    mv = a.sb(f"mv_{u}", [128, n_tiles * 2], F32)
    for t in range(n_tiles):
        for s in range(nseg):
            a.op("vector", lambda e, t=t, s=s: e.bn_stats(
                bn6[:, (t * nseg + s) * 6:(t * nseg + s) * 6 + 6],
                xsb[:, t, s * 512:(s + 1) * 512]),
                reads=[xkey], writes=[f"bn6_{u}/{t}"], tag="bn_stats")
        a.op("vector", lambda e, t=t: e.bn_aggr(
            mv[:, 2 * t:2 * t + 2], bn6[:, t * nseg * 6:(t + 1) * nseg * 6]),
            reads=[f"bn6_{u}/{t}"], writes=[f"mv_{u}/{t}"], tag="bn_aggr")
    mvkeys = [f"mv_{u}/{t}" for t in range(n_tiles)]
    a.op("vector", lambda e: e.tensor_scalar(
        scale_col[:, 0:n_tiles], mv[:, 1:2 * n_tiles:2], EPS_LN, None, ALU.add),
        reads=mvkeys, writes=[skey + "/tmp"], tag="ln_eps")
    a.op("scalar", lambda e: e.activation(
        scale_col[:, 0:n_tiles], scale_col[:, 0:n_tiles], AF.Ln,
        bias=np.float32(0.0), scale=np.float32(1.0)),
        reads=[skey + "/tmp"], writes=[skey + "/ln"], tag="ln_ln")
    a.op("scalar", lambda e: e.activation(
        scale_col[:, 0:n_tiles], scale_col[:, 0:n_tiles], AF.Exp,
        bias=np.float32(math.log(fold)), scale=np.float32(-0.5)),
        reads=[skey + "/ln"], writes=[skey + "/scale"], tag="ln_exp")
    a.op("vector", lambda e: e.tensor_tensor(
        bias_col[:, 0:n_tiles], mv[:, 0:2 * n_tiles:2], scale_col[:, 0:n_tiles],
        ALU.mult),
        reads=mvkeys + [skey + "/scale"], writes=[skey + "/b0"], tag="ln_b0")
    a.op("vector", lambda e: e.tensor_scalar(
        bias_col[:, 0:n_tiles], bias_col[:, 0:n_tiles], -1.0, None, ALU.mult),
        reads=[skey + "/b0"], writes=[skey + "/bias"], tag="ln_b1")
    return skey + "/scale", skey + "/bias"


def emit_normalize(a, xsb, out_sb, scale_col, bias_col, skey, t, dst_key,
                   xkey="x", width=C, tag="ln_apply"):
    a.op("scalar", lambda e: e.activation(
        out_sb[:, t, 0:width], xsb[:, t, 0:width], AF.Identity,
        bias=bias_col[:, t:t + 1], scale=scale_col[:, t:t + 1]),
        reads=[xkey, skey + "/scale", skey + "/bias"], writes=[dst_key],
        tag=tag)


def emit_transposes(a, src_bf, dst_fm, n_tiles, ident, srckeys, dstkey_fn,
                    width=C, tag="tr", banks=None, bankkeys=None):
    u = a.uid()
    fch = width // 128
    if banks is None:
        banks = [a.ps(f"tp{u}_0", [128, 512], BF16),
                 a.ps(f"tp{u}_1", [128, 512], BF16)]
    groups = [(t, c) for t in range(n_tiles) for c in range(fch)]
    for g in range(0, len(groups), 4):
        bank = banks[(g // 4) % 2]
        bkey = (bankkeys[(g // 4) % 2] if bankkeys
                else f"tpbank{u}_{(g // 4) % 2}")
        items = groups[g:g + 4]
        for i, (t, c) in enumerate(items):
            a.op("tensor", lambda e, i=i, t=t, c=c, bank=bank: e.transpose(
                bank[:, i * 128:(i + 1) * 128],
                src_bf[:, t, c * 128:(c + 1) * 128], ident[:]),
                reads=[srckeys(t), "ident"], writes=[bkey], tag=tag)
        for i, (t, c) in enumerate(items):
            a.op("vector", lambda e, i=i, t=t, c=c, bank=bank: e.tensor_copy(
                dst_fm[:, c, t * 128:(t + 1) * 128],
                bank[:, i * 128:(i + 1) * 128]),
                reads=[bkey], writes=[dstkey_fn(t, c)], tag=tag + "_ev")


def emit_round_clip(a, eng, src_ap, i32_ap, dst_ap, lo, hi, rkeys, wkey, tag,
                    ikey=None):
    ikey = ikey or (wkey + "/i")
    a.op(eng, lambda e: e.tensor_copy(i32_ap, src_ap),
         reads=rkeys, writes=[ikey], tag=tag + "_rnd")
    a.op(eng, lambda e: e.tensor_scalar(dst_ap, i32_ap, float(hi), float(lo),
                                        ALU.min, ALU.max),
         reads=[ikey], writes=[wkey], tag=tag + "_clip")



def build_w0():
    a = ASB("w0")
    w1t = a.dram_in("w1t", [C, HID])
    w2t = a.dram_in("w2t", [HID, C])
    b1 = a.dram_in("b1", [1, HID])
    b2 = a.dram_in("b2", [1, C])
    v1t = a.dram_in("v1t", [C, HID])
    v2t = a.dram_in("v2t", [HID, C])
    m1t_o = a.dram_out("m1t", [C, HID], BF16)
    m2t_o = a.dram_out("m2t", [HID, C], BF16)
    mb1_o = a.dram_out("mb1", [1, HID])
    b2q_o = a.dram_out("b2q1024", [1, C])
    t1t_o = a.dram_out("t1t", [C, HID], BF16)
    t2t_o = a.dram_out("t2t", [HID, C], BF16)
    mw_o = a.dram_out("mw", [1, 2])

    nb = 3
    xs = [a.sb(f"x{i}", [128, HID], F32) for i in range(nb)]
    xis = [a.sb(f"xi{i}", [128, HID], I32) for i in range(nb)]
    xbs = [a.sb(f"xb{i}", [128, HID], BF16) for i in range(nb)]

    def quant_stream(src, dst, nrt, scale_imm, scale_col, name, width):
        lo, hi = (-128.0, 127.0) if scale_col is None else (-1.0, 1.0)
        for i in range(nrt):
            b = i % nb
            a.dma("sync", xs[b][:, 0:width], src[i * 128:(i + 1) * 128, :],
                  writes=[f"x{b}"], tag=f"{name}_ld")
            if scale_col is None:
                a.op("vector", lambda e, b=b: e.tensor_scalar(
                    xs[b][:, 0:width], xs[b][:, 0:width], float(scale_imm),
                    None, ALU.mult),
                    reads=[f"x{b}"], writes=[f"x{b}"], tag=f"{name}_s")
            else:
                col, ckey = scale_col
                a.op("vector", lambda e, b=b: e.tensor_scalar(
                    xs[b][:, 0:width], xs[b][:, 0:width], col, None, ALU.mult),
                    reads=[f"x{b}", ckey], writes=[f"x{b}"],
                    tag=f"{name}_s")
            a.op("vector", lambda e, b=b: e.tensor_copy(
                xis[b][:, 0:width], xs[b][:, 0:width]),
                 reads=[f"x{b}"], writes=[f"xi{b}"], tag=f"{name}_r")
            a.op("vector", lambda e, b=b: e.tensor_scalar(
                xbs[b][:, 0:width], xis[b][:, 0:width], hi, lo, ALU.min, ALU.max),
                reads=[f"xi{b}"], writes=[f"xb{b}"], tag=f"{name}_c")
            a.dma("gpsimd", dst[i * 128:(i + 1) * 128, :], xbs[b][:, 0:width],
                  reads=[f"xb{b}"], writes=[f"{name}st"], tag=f"{name}_st")

    quant_stream(w1t, m1t_o, C // 128, 64.0, None, "q1", HID)
    quant_stream(w2t, m2t_o, HID // 128, 64.0, None, "q2", C)

    br = a.sb("br", [1, HID], F32)
    bri = a.sb("bri", [1, HID], I32)
    a.dma("sync", br[0:1, 0:HID], b1[0:1, :], writes=["br"], tag="b1_ld")
    a.op("vector", lambda e: e.tensor_scalar(br[0:1, 0:HID], br[0:1, 0:HID],
                                             64.0, None, ALU.mult),
         reads=["br"], writes=["br"], tag="b1_s")
    a.op("vector", lambda e: e.tensor_copy(bri[0:1, 0:HID], br[0:1, 0:HID]),
         reads=["br"], writes=["bri"], tag="b1_r")
    a.op("vector", lambda e: e.tensor_copy(br[0:1, 0:HID], bri[0:1, 0:HID]),
         reads=["bri"], writes=["br"], tag="b1_c")
    a.dma("gpsimd", mb1_o[0:1, :], br[0:1, 0:HID], reads=["br"],
          writes=["mb1st"], tag="b1_st")
    a.dma("sync", br[0:1, 0:C], b2[0:1, :], reads=["br"], writes=["br"],
          tag="b2_ld")
    a.op("vector", lambda e: e.tensor_scalar(br[0:1, 0:C], br[0:1, 0:C],
                                             64.0, None, ALU.mult),
         reads=["br"], writes=["br"], tag="b2_s")
    a.op("vector", lambda e: e.tensor_copy(bri[0:1, 0:C], br[0:1, 0:C]),
         reads=["br"], writes=["bri"], tag="b2_r")
    a.op("vector", lambda e: e.tensor_scalar(br[0:1, 0:C], bri[0:1, 0:C],
                                             16.0, None, ALU.mult),
         reads=["bri"], writes=["br"], tag="b2_c")
    a.dma("gpsimd", b2q_o[0:1, :], br[0:1, 0:C], reads=["br"],
          writes=["b2st"], tag="b2_st")

    acc = a.sb("acc", [128, 2], F32)
    red = a.sb("red", [128, 1], F32)
    ones1f = a.sb("ones1f", [128, 1], F32)
    onesr = a.sb("onesr", [1, 128], BF16)
    rps = a.ps("rps", [1, 2], F32)
    bps = a.ps("bps", [128, 2], F32)
    mwsb = a.sb("mwsb", [1, 2], F32)
    mwb = a.sb("mwb", [1, 2], BF16)
    recs = a.sb("recs", [128, 2], F32)

    def mean_abs(src, nrt, col, name, width):
        for i in range(nrt):
            b = i % nb
            a.dma("sync", xs[b][:, 0:width], src[i * 128:(i + 1) * 128, :],
                  writes=[f"x{b}"], tag=f"{name}_ld")
            a.op("vector", lambda e, b=b: e.tensor_reduce(
                red[:], xs[b][:, 0:width], AX.X, ALU.add,
                apply_absolute_value=True),
                reads=[f"x{b}"], writes=["red"], tag=f"{name}_red")
            if i == 0:
                a.op("vector", lambda e: e.tensor_copy(acc[:, col:col + 1], red[:]),
                     reads=["red"], writes=[f"acc{col}"], tag=f"{name}_a0")
            else:
                a.op("vector", lambda e: e.tensor_add(
                    acc[:, col:col + 1], acc[:, col:col + 1], red[:]),
                    reads=["red", f"acc{col}"], writes=[f"acc{col}"],
                    tag=f"{name}_acc")

    mean_abs(v1t, C // 128, 0, "ma1", HID)
    mean_abs(v2t, HID // 128, 1, "ma2", C)
    a.op("vector", lambda e: e.memset(ones1f[:], 1.0), writes=["ones1"])
    a.op("vector", lambda e: e.memset(onesr[0:1, :], 1.0), writes=["onesr"])
    a.op("tensor", lambda e: e.matmul(rps[0:1, 0:2], ones1f[:, 0:1], acc[:, 0:2],
                                      start=True, stop=True),
         reads=["ones1", "acc0", "acc1"], writes=["rps"], tag="ma_mm")
    a.op("vector", lambda e: e.tensor_scalar(
        mwsb[0:1, 0:2], rps[0:1, 0:2], 1.0 / (C * HID), 1e-5, ALU.mult, ALU.max),
        reads=["rps"], writes=["mwsb"], tag="ma_mw")
    a.dma("gpsimd", mw_o[0:1, :], mwsb[0:1, 0:2], reads=["mwsb"],
          writes=["mwst"], tag="mw_st")
    a.op("vector", lambda e: e.tensor_copy(mwb[0:1, 0:2], mwsb[0:1, 0:2]),
         reads=["mwsb"], writes=["mwb"], tag="mw_cast")
    a.op("tensor", lambda e: e.matmul(bps[:, 0:2], onesr[0:1, :], mwb[0:1, 0:2],
                                      start=True, stop=True),
         reads=["onesr", "mwb"], writes=["bps"], tag="mw_bcast")
    a.op("vector", lambda e: e.reciprocal(recs[:, 0:2], bps[:, 0:2]),
         reads=["bps"], writes=["recs"], tag="mw_rec")

    quant_stream(v1t, t1t_o, C // 128, None, (recs[:, 0:1], "recs"), "t1", HID)
    quant_stream(v2t, t2t_o, HID // 128, None, (recs[:, 1:2], "recs"), "t2", C)

    a.finish(["q1st", "q2st", "t1st", "t2st", "mb1st", "b2st", "mwst"])
    return a.build()


def run_w0(inp):
    a = build_w0()
    m = {
        "w1t": np.ascontiguousarray(np.asarray(inp["L0_ffn_w1"], np.float32).T),
        "w2t": np.ascontiguousarray(np.asarray(inp["L0_ffn_w2"], np.float32).T),
        "b1": np.asarray(inp["L0_ffn_b1"], np.float32).reshape(1, -1),
        "b2": np.asarray(inp["L0_ffn_b2"], np.float32).reshape(1, -1),
        "v1t": np.ascontiguousarray(np.asarray(inp["L1_ffn_w1"], np.float32).T),
        "v2t": np.ascontiguousarray(np.asarray(inp["L1_ffn_w2"], np.float32).T),
    }
    res = run_bass_kernel_spmd(a.nc, [m], core_ids=[0])
    a.close()
    return res.results[0]



def build_f0(debug=False):
    a = ASB("f0")
    xd = a.dram_in("xs", [ROWS, C])
    m1d = a.dram_in("m1", [HCH, 128, C], BF16)
    m2d = a.dram_in("m2", [HID, C], BF16)
    mb1d = a.dram_in("mb1c", [128, HCH])
    b2qd = a.dram_in("b2q", [1, C])
    identd = a.dram_in("ident", [128, 128], BF16)
    xo = a.dram_out("xo", [ROWS, C])

    xsb = a.sb("xsb", [128, NTILE, C], F32)
    ident = a.sb("identsb", [128, 128], BF16)
    lns = a.sb("lns", [128, NTILE], F32)
    lnb = a.sb("lnb", [128, NTILE], F32)
    y16 = [a.sb(f"y16_{i}", [128, C], F32) for i in range(2)]
    yi = [a.sb(f"yi_{i}", [128, C], I32) for i in range(2)]
    nsb = a.sb("nsb", [128, NTILE, C], BF16)
    nT = a.sb("nT", [128, FCH, ROWS], BF16)
    m1b = [a.sb(f"m1b_{i}", [128, C], BF16) for i in range(2)]
    h16 = [a.sb(f"h16_{i}", [128, 512], F32) for i in range(2)]
    hi = [a.sb(f"hi_{i}", [128, 512], I32) for i in range(2)]
    n2 = a.sb("n2", [128, HCH, 512], BF16)
    m2b = [a.sb(f"m2b_{i}", [128, 512], BF16) for i in range(2)]
    onesr = a.sb("onesr", [1, 128], BF16)
    b2qbf = a.sb("b2qbf", [1, C], BF16)
    b2qf = a.sb("b2qf", [1, C], F32)
    mb1q = a.sb("mb1q", [128, HCH], F32)
    hps = [a.ps(f"hps_{i}", [128, 512], F32) for i in range(2)]
    ops = [a.ps(f"ops_{i}", [128, 512], F32) for i in range(4)]

    emit_load_x(a, xd, xsb, key="x")
    a.dma("sync", ident[:], identd[:, :], writes=["ident"], tag="ld_ident")
    a.dma("sync", mb1q[:], mb1d[:, :], writes=["mb1"], tag="ld_mb1")
    a.dma("sync", b2qf[0:1, :], b2qd[0:1, :], writes=["b2qf"], tag="ld_b2q")
    a.op("vector", lambda e: e.tensor_scalar(mb1q[:], mb1q[:], 0.25, None,
                                             ALU.mult),
         reads=["mb1"], writes=["mb1"], tag="mb1_q4")
    a.op("vector", lambda e: e.tensor_copy(b2qbf[0:1, :], b2qf[0:1, :]),
         reads=["b2qf"], writes=["b2qbf"], tag="b2q_cast")
    a.op("vector", lambda e: e.memset(onesr[0:1, :], 1.0), writes=["onesr"])

    emit_ln(a, xsb, lns, lnb, fold=16.0, xkey="x", skey="ln")

    for t in range(NTILE):
        b = t % 2
        a.op("scalar", lambda e, t=t, b=b: e.activation(
            y16[b][:, :], xsb[:, t, :], AF.Identity,
            bias=lnb[:, t:t + 1], scale=lns[:, t:t + 1]),
            reads=["x", "ln/scale", "ln/bias"], writes=[f"y16/{b}"],
            tag="ln_apply")
        a.op("vector", lambda e, b=b: e.tensor_copy(yi[b][:], y16[b][:]),
             reads=[f"y16/{b}"], writes=[f"yi/{b}"], tag="rnd")
        a.op("vector", lambda e, t=t, b=b: e.tensor_scalar(
            nsb[:, t, :], yi[b][:], 127.0, -128.0, ALU.min, ALU.max),
            reads=[f"yi/{b}"], writes=[f"n/{t}"], tag="clip")

    emit_transposes(a, nsb, nT, NTILE, ident, lambda t: f"n/{t}",
                    lambda t, c: f"nT/{t}/{c}")

    for tg in range(4):
        tg_tiles = [tg * 4 + i for i in range(4)]
        for hc in range(HCH):
            mb = (tg * HCH + hc) % 2
            a.dma("sync", m1b[mb][:, :], m1d[hc, :, :],
                  writes=[f"m1b/{mb}"], tag="m1_ld")
            pb_ = hc % 2
            for dc in range(FCH):
                nt_keys = [f"nT/{t}/{dc}" for t in tg_tiles]
                a.op("tensor", lambda e, hc=hc, dc=dc, tg=tg, mb=mb, pb_=pb_:
                     e.matmul(hps[pb_][:, :], m1b[mb][:, dc * 128:(dc + 1) * 128],
                              nT[:, dc, tg * 512:(tg + 1) * 512],
                              start=(dc == 0), stop=(dc == FCH - 1)),
                     reads=[f"m1b/{mb}"] + nt_keys, writes=[f"hps/{pb_}"],
                     tag="mm1")
            a.op("scalar", lambda e, hc=hc, pb_=pb_: e.activation(
                h16[pb_][:, :], hps[pb_][:, :], AF.Identity,
                bias=mb1q[:, hc:hc + 1], scale=np.float32(1.0 / 64)),
                reads=[f"hps/{pb_}", "mb1"], writes=[f"h16/{pb_}"], tag="h_evac")
            a.op("vector", lambda e, pb_=pb_: e.tensor_copy(hi[pb_][:], h16[pb_][:]),
                 reads=[f"h16/{pb_}"], writes=[f"hi/{pb_}"], tag="h_rnd")
            a.op("vector", lambda e, hc=hc, pb_=pb_: e.tensor_scalar(
                n2[:, hc, :], hi[pb_][:], 127.0, 0.0, ALU.min, ALU.max),
                reads=[f"hi/{pb_}"], writes=[f"n2/{hc}"], tag="h_clip")
        for dh in range(2):
            for hc in range(HCH):
                mb = (tg * 2 + dh + hc) % 2
                a.dma("scalar", m2b[mb][:, :],
                      m2d[hc * 128:(hc + 1) * 128, dh * 512:(dh + 1) * 512],
                      writes=[f"m2b/{mb}"], tag="m2_ld")
                for i, t in enumerate(tg_tiles):
                    a.op("tensor", lambda e, hc=hc, i=i, t=t, mb=mb: e.matmul(
                        ops[i][:, :], n2[:, hc, t % 4 * 128:(t % 4 + 1) * 128],
                        m2b[mb][:, :], start=(hc == 0), stop=False),
                        reads=[f"m2b/{mb}", f"n2/{hc}"], writes=[f"ops/{i}"],
                        tag="mm2")
            for i, t in enumerate(tg_tiles):
                a.op("tensor", lambda e, i=i, dh=dh: e.matmul(
                    ops[i][:, :], onesr[0:1, :], b2qbf[0:1, dh * 512:(dh + 1) * 512],
                    start=False, stop=True),
                    reads=["onesr", "b2qbf"], writes=[f"ops/{i}"], tag="mm2_bias")
            for i, t in enumerate(tg_tiles):
                a.op("vector", lambda e, i=i, t=t, dh=dh: e.scalar_tensor_tensor(
                    xsb[:, t, dh * 512:(dh + 1) * 512], ops[i][:, :],
                    1.0 / 1024, xsb[:, t, dh * 512:(dh + 1) * 512],
                    ALU.mult, ALU.add),
                    reads=[f"ops/{i}", "x"], writes=["x", f"xn/{t}/{dh}"],
                    tag="resid")

    a.dma("gpsimd", xo.rearrange("(n p) f -> p n f", p=128), xsb[:, :, :],
          reads=["x"] + [f"xn/{t}/{dh}" for t in range(NTILE) for dh in range(2)],
          writes=["xo_st"], tag="store_x")
    fin = ["xo_st"]
    if debug:
        nd = a.dram_out("dbg_n", [ROWS, C], BF16)
        n2d = a.dram_out("dbg_n2", [HID, 512], BF16)
        lnsd = a.dram_out("dbg_lns", [128, NTILE])
        a.dma("gpsimd", nd.rearrange("(n p) f -> p n f", p=128), nsb[:, :, :],
              reads=[f"n/{t}" for t in range(NTILE)], writes=["dbg1"], tag="dbg")
        a.dma("gpsimd", n2d.rearrange("(hc p) f -> p hc f", p=128), n2[:, :, :],
              reads=[f"n2/{hc}" for hc in range(HCH)], writes=["dbg2"], tag="dbg")
        a.dma("gpsimd", lnsd[:, :], lns[:, :], reads=["ln/scale"],
              writes=["dbg3"], tag="dbg")
        fin += ["dbg1", "dbg2", "dbg3"]
    a.finish(fin)
    return a.build()


def _f0_host_inputs(m1t, m2t, mb1, b2q1024):
    m1 = np.ascontiguousarray(
        np.asarray(m1t).reshape(FCH, 128, HCH, 128).transpose(2, 1, 0, 3)
    ).reshape(HCH, 128, C)
    m2 = np.ascontiguousarray(np.asarray(m2t))
    mb1c = np.ascontiguousarray(np.asarray(mb1).reshape(HCH, 128).T)
    return m1, m2, mb1c, np.asarray(b2q1024).reshape(1, C)


def run_f0(x_full, w0out, prog=None, debug=False):
    m1, m2, mb1c, b2q = _f0_host_inputs(w0out["m1t"], w0out["m2t"],
                                        w0out["mb1"], w0out["b2q1024"])
    a = prog or build_f0(debug=debug)
    shards = _shard_rows(x_full)
    ident = np.eye(128, dtype=BFNP)
    maps = [{"xs": s, "m1": m1, "m2": m2, "mb1c": mb1c, "b2q": b2q,
             "ident": ident} for s in shards]
    res = run_bass_kernel_spmd(a.nc, maps, core_ids=list(range(NCORES)))
    if prog is None:
        a.close()
    if debug:
        return _unshard_rows([r["xo"] for r in res.results]), res.results
    return _unshard_rows([r["xo"] for r in res.results])


def emit_weight_cast(a, wd, wsb, nchunks, wkey, width=C, tag="wld"):
    for dc in range(nchunks):
        a.dma("gpsimd", wsb[:, dc, :], wd[dc * 128:(dc + 1) * 128, :],
              writes=[f"{wkey}/{dc}"], tag=tag)


def emit_ln_apply_bf16(a, xsb, xln, t, tag="lnap"):
    a.op("scalar", lambda e, t=t: e.activation(
        xln[:, t, :], xsb[:, t, :], AF.Identity,
        bias=a._lnb[:, t:t + 1], scale=a._lns[:, t:t + 1]),
        reads=["x", "ln/scale", "ln/bias"], writes=[f"xln/{t}"], tag=tag)


def emit_std_prologue(a, xd, identd, fold=1.0):
    xsb = a.sb("xsb", [128, NTILE, C], F32)
    ident = a.sb("identsb", [128, 128], BF16)
    lns = a.sb("lns", [128, NTILE], F32)
    lnb = a.sb("lnb", [128, NTILE], F32)
    xln = a.sb("xln", [128, NTILE, C], BF16)
    xT = a.sb("xT", [128, FCH, ROWS], BF16)
    a._lns, a._lnb = lns, lnb
    emit_load_x(a, xd, xsb, key="x")
    a.dma("sync", ident[:], identd[:, :], writes=["ident"], tag="ld_ident")
    emit_ln(a, xsb, lns, lnb, fold=fold, xkey="x", skey="ln")
    for t in range(NTILE):
        emit_ln_apply_bf16(a, xsb, xln, t)
    emit_transposes(a, xln, xT, NTILE, ident, lambda t: f"xln/{t}",
                    lambda t, c: f"xT/{t}/{c}")
    return xsb, xT


def emit_stream_prologue(a, xd, identd, fold=1.0, nbuf=2):
    ident = a.sb("identsb", [128, 128], BF16)
    xt_ = [a.sb(f"xt_{i}", [128, C], F32) for i in range(nbuf)]
    bn6 = a.sb("bn6p", [128, 12], F32)
    mv = a.sb("mvp", [128, 2], F32)
    lns = a.sb("lns", [128, NTILE], F32)
    lnb = a.sb("lnb", [128, NTILE], F32)
    xln = a.sb("xln", [128, NTILE, C], BF16)
    xT = a.sb("xT", [128, FCH, ROWS], BF16)
    a._lns, a._lnb = lns, lnb
    a._xln_tensor = xln
    a.dma("sync", ident[:], identd[:, :], writes=["ident"], tag="ld_ident")
    for t in range(NTILE):
        b = t % nbuf
        a.dma("sync", xt_[b][:, :], xd[t * 128:(t + 1) * 128, :],
              reads=[f"xt/{b}"], writes=[f"xt/{b}"], tag="x_ld")
        for s in range(2):
            a.op("vector", lambda e, s=s, b=b: e.bn_stats(
                bn6[:, s * 6:(s + 1) * 6], xt_[b][:, s * 512:(s + 1) * 512]),
                reads=[f"xt/{b}"], writes=[f"bn6/{s}"], tag="bn_stats")
        a.op("vector", lambda e: e.bn_aggr(mv[:, 0:2], bn6[:, 0:12]),
             reads=["bn6/0", "bn6/1"], writes=["mv"], tag="bn_aggr")
        a.op("vector", lambda e, t=t: e.tensor_scalar(
            lns[:, t:t + 1], mv[:, 1:2], EPS_LN, None, ALU.add),
            reads=["mv"], writes=[f"lntmp/{t}"], tag="ln_eps")
        a.op("scalar", lambda e, t=t: e.activation(
            lns[:, t:t + 1], lns[:, t:t + 1], AF.Ln,
            bias=np.float32(0.0), scale=np.float32(1.0)),
            reads=[f"lntmp/{t}"], writes=[f"lnl/{t}"], tag="ln_ln")
        a.op("scalar", lambda e, t=t: e.activation(
            lns[:, t:t + 1], lns[:, t:t + 1], AF.Exp,
            bias=np.float32(math.log(fold)), scale=np.float32(-0.5)),
            reads=[f"lnl/{t}"], writes=[f"lnsc/{t}"], tag="ln_exp")
        a.op("vector", lambda e, t=t: e.tensor_tensor(
            lnb[:, t:t + 1], mv[:, 0:1], lns[:, t:t + 1], ALU.mult),
            reads=["mv", f"lnsc/{t}"], writes=[f"lnb0/{t}"], tag="ln_b0")
        a.op("vector", lambda e, t=t: e.tensor_scalar(
            lnb[:, t:t + 1], lnb[:, t:t + 1], -1.0, None, ALU.mult),
            reads=[f"lnb0/{t}"], writes=[f"lnbv/{t}"], tag="ln_b1")
        a.op("scalar", lambda e, t=t, b=b: e.activation(
            xln[:, t, :], xt_[b][:, :], AF.Identity,
            bias=lnb[:, t:t + 1], scale=lns[:, t:t + 1]),
            reads=[f"xt/{b}", f"lnsc/{t}", f"lnbv/{t}"], writes=[f"xln/{t}"],
            tag="ln_apply")
    emit_transposes(a, xln, xT, NTILE, ident, lambda t: f"xln/{t}",
                    lambda t, c: f"xT/{t}/{c}")
    return xT


def xt_keys(tg):
    return [f"xT/{t}/{c}" for t in range(tg * 4, tg * 4 + 4) for c in range(FCH)]



def build_c0():
    a = ASB("c0")
    xd = a.dram_in("xs", [ROWS, C])
    wqd = a.dram_in("wq", [C, C])
    wkd = a.dram_in("wk", [C, C])
    wvd = a.dram_in("wv", [C, C])
    bqd = a.dram_in("bqc", [128, FCH])
    bkd = a.dram_in("bkc", [128, FCH])
    bvd = a.dram_in("bvr", [1, C])
    ehd = a.dram_in("eh", [128, FCH, 16])
    identd = a.dram_in("ident", [128, 128], BF16)
    qt_o = a.dram_out("qt", [C, ROWS], BF16)
    kt_o = a.dram_out("kt", [C, ROWS], BF16)
    v_o = a.dram_out("vv", [ROWS, 16 * 65], BF16)
    qn2_o = a.dram_out("qn2", [16, ROWS])
    kn2_o = a.dram_out("kn2", [16, ROWS])

    xT = emit_stream_prologue(a, xd, identd)
    wsb = a.sb("wsb", [128, FCH, C], BF16)
    bcol = a.sb("bcol", [128, FCH], F32)
    ehf = a.sb("ehf", [128, FCH, 16], F32)
    bvr = a.sb("bvr_sb", [1, C], F32)
    bvb = a.sb("bvb", [1, C], BF16)
    onesr = a.sb("onesr", [1, 128], BF16)
    qf = [a.sb(f"qf_{i}", [128, 512], F32) for i in range(2)]
    sq = [a.sb(f"sq_{i}", [128, 512], F32) for i in range(2)]
    n2sb = a.sb("n2sb", [16, ROWS], F32)
    outT = a.sb("outT", [128, FCH, ROWS], BF16)
    vsb = a.sb("vsb", [128, NTILE, 16 * 65], BF16)
    pps = [a.ps(f"pps_{i}", [128, 512], F32) for i in range(2)]
    nps = a.ps("nps", [16, 512], F32)

    a.dma("sync", ehf[:, :, :], ehd[:, :, :], writes=["ehf"], tag="ld_eh")
    a.dma("sync", bvr[0:1, :], bvd[0:1, :], writes=["bvr"], tag="ld_bv")
    a.op("vector", lambda e: e.tensor_copy(bvb[0:1, :], bvr[0:1, :]),
         reads=["bvr"], writes=["bvb"], tag="bv_cast")
    a.op("vector", lambda e: e.memset(onesr[0:1, :], 1.0), writes=["onesr"])

    def proj_fm(wd_, bd_, out_dram, n2_dram, pname):
        emit_weight_cast(a, wd_, wsb, FCH, "wsb")
        a.dma("sync", bcol[:, :], bd_[:, :], reads=["bcol"], writes=["bcol"],
              tag=f"{pname}_bld")
        for tg in range(4):
            for oc in range(FCH):
                pb_ = oc % 2
                for dc in range(FCH):
                    a.op("tensor", lambda e, oc=oc, dc=dc, tg=tg, pb_=pb_:
                         e.matmul(pps[pb_][:, :],
                                  wsb[:, dc, oc * 128:(oc + 1) * 128],
                                  xT[:, dc, tg * 512:(tg + 1) * 512],
                                  start=(dc == 0), stop=(dc == FCH - 1)),
                         reads=[f"wsb/{dc}"] + xt_keys(tg),
                         writes=[f"pps/{pb_}"], tag=f"{pname}_mm")
                a.op("scalar", lambda e, oc=oc, pb_=pb_: e.activation(
                    qf[pb_][:, :], pps[pb_][:, :], AF.Identity,
                    bias=bcol[:, oc:oc + 1], scale=np.float32(1.0)),
                    reads=[f"pps/{pb_}", "bcol"], writes=[f"qf/{pb_}"],
                    tag=f"{pname}_ev")
                a.op("vector", lambda e, oc=oc, tg=tg, pb_=pb_: e.tensor_copy(
                    outT[:, oc, tg * 512:(tg + 1) * 512], qf[pb_][:, :]),
                    reads=[f"qf/{pb_}"], writes=[f"oT/{oc}/{tg}"],
                    tag=f"{pname}_cast")
                a.op("scalar", lambda e, pb_=pb_: e.activation(
                    sq[pb_][:, :], qf[pb_][:, :], AF.Square,
                    bias=np.float32(0.0), scale=np.float32(1.0)),
                    reads=[f"qf/{pb_}"], writes=[f"sqb/{pb_}"],
                    tag=f"{pname}_sq")
                a.op("tensor", lambda e, oc=oc, pb_=pb_: e.matmul(
                    nps[:, :], ehf[:, oc, :], sq[pb_][:, :],
                    start=(oc == 0), stop=(oc == FCH - 1)),
                    reads=["ehf", f"sqb/{pb_}"], writes=["nps"],
                    tag=f"{pname}_emm")
            a.op("vector", lambda e, tg=tg: e.tensor_copy(
                n2sb[:, tg * 512:(tg + 1) * 512], nps[:, :]),
                reads=["nps"], writes=[f"n2sb/{tg}"], tag=f"{pname}_n2ev")
        a.dma("gpsimd", out_dram.rearrange("(c p) r -> p c r", p=128),
              outT[:, :, :],
              reads=[f"oT/{oc}/{tg}" for oc in range(FCH) for tg in range(4)],
              writes=[f"{pname}_st", "outTbuf"], tag=f"{pname}_st")
        a.dma("gpsimd", n2_dram[:, :], n2sb[:, :],
              reads=[f"n2sb/{tg}" for tg in range(4)],
              writes=[f"{pname}_st2", "n2buf"], tag=f"{pname}_st2")

    proj_fm(wqd, bqd, qt_o, qn2_o, "q")
    proj_fm(wkd, bkd, kt_o, kn2_o, "k")

    emit_weight_cast(a, wvd, wsb, FCH, "wsb")
    a.op("vector", lambda e: e.memset(vsb[:, :, :].rearrange(
        "p t (h o) -> p t h o", o=65)[:, :, :, 64:65], 1.0),
        writes=["vones"], tag="v_ones")
    for t in range(NTILE):
        for vh in range(2):
            pb_ = (t * 2 + vh) % 2
            for dc in range(FCH):
                a.op("tensor", lambda e, t=t, vh=vh, dc=dc, pb_=pb_: e.matmul(
                    pps[pb_][:, :], xT[:, dc, t * 128:(t + 1) * 128],
                    wsb[:, dc, vh * 512:(vh + 1) * 512],
                    start=(dc == 0), stop=False),
                    reads=[f"wsb/{dc}", f"xT/{t}/{dc}"], writes=[f"pps/{pb_}"],
                    tag="v_mm")
            a.op("tensor", lambda e, vh=vh, pb_=pb_: e.matmul(
                pps[pb_][:, :], onesr[0:1, :], bvb[0:1, vh * 512:(vh + 1) * 512],
                start=False, stop=True),
                reads=["onesr", "bvb"], writes=[f"pps/{pb_}"], tag="v_bias")
            a.op("vector", lambda e, t=t, vh=vh, pb_=pb_: e.tensor_copy(
                vsb[:, t, :].rearrange("p (h o) -> p h o", o=65)
                [:, vh * 8:(vh + 1) * 8, 0:64],
                pps[pb_][:, :].rearrange("p (h d) -> p h d", d=64)),
                reads=[f"pps/{pb_}", "vones"], writes=[f"vev/{t}/{vh}"],
                tag="v_ev")
    a.dma("gpsimd", v_o.rearrange("(t p) v -> p t v", p=128), vsb[:, :, :],
          reads=[f"vev/{t}/{vh}" for t in range(NTILE) for vh in range(2)],
          writes=["v_st"], tag="v_st")

    a.finish(["q_st", "q_st2", "k_st", "k_st2", "v_st"])
    return a.build()



def build_c1(scale=float(D) ** -0.5, causal=False, name="c1"):
    a = ASB(name)
    xd = a.dram_in("xs", [ROWS, C])
    qtd = a.dram_in("qt", [C, ROWS], BF16)
    qn2d = a.dram_in("qn2", [16, ROWS])
    ktd = a.dram_in("ktg", [16, 64, B, T], BF16)
    kn2d = a.dram_in("kn2g", [16, B * T])
    vgd = a.dram_in("vg", [B, T, 16 * 65], BF16)
    wod = a.dram_in("wo", [C, C])
    bod = a.dram_in("bor", [1, C])
    maskd = a.dram_in("masks", [T // 128, 128, 512], BF16) if causal else None
    xo = a.dram_out("xo", [ROWS, C])

    qt = a.sb("qtsb", [128, FCH, ROWS], BF16)
    qn2 = a.sb("qn2sb", [16, ROWS], F32)
    kn2 = a.sb("kn2sb", [16, T], F32)
    vsb = a.sb("vsb", [128, T // 128, 16 * 65], BF16)
    kbuf = a.sb("kbuf", [128, T], BF16)
    wosb = a.sb("wosb", [128, FCH, C], BF16)
    borow = a.sb("borow", [1, C], F32)
    bob = a.sb("bob", [1, C], BF16)
    onesr = a.sb("onesr", [1, 128], BF16)
    onesrf = a.sb("onesrf", [1, 128], F32)
    ones64 = a.sb("ones64", [1, 64], F32)
    yT = a.sb("yT", [128, FCH, ROWS], BF16)
    pbuf = [a.sb(f"pbuf_{i}", [128, 512], BF16) for i in range(2)]
    masks = a.sb("masksb", [128, T // 128, 512], BF16) if causal else None
    mm_sc = a.sb("mm_sc", [16, 4], F32)
    kmx = a.sb("kmx", [16, 4], F32)
    crow = a.sb("crow", [1, 64], F32)
    cb = a.sb("cbsb", [128, 64], F32)
    rrow = a.sb("rrow", [1, 512], F32)
    rbs = a.sb("rbs", [64, 512], F32)
    xbuf = [a.sb(f"xbuf_{i}", [128, C], F32) for i in range(2)]
    sps = [a.ps(f"sps_{i}", [128, 512], F32) for i in range(2)]
    avps = a.ps("avps", [128, 512], F32)
    rbps = a.ps("rbps", [128, 512], F32)
    cbounce = a.dram_tmp("cbounce", [16, 4])

    a.dma("sync", qt[:, :, :], qtd.rearrange("(c p) r -> p c r", p=128),
          writes=["qt"], tag="ld_qt")
    a.dma("sync", qn2[:, :], qn2d[:, :], writes=["qn2"], tag="ld_qn2")

    if causal:
        a.dma("sync", masks[:, :, :], maskd.rearrange("j p q -> p j q"),
              writes=["masks"], tag="ld_masks")
    a.op("vector", lambda e: e.memset(onesr[0:1, :], 1.0), writes=["onesr"])
    a.op("vector", lambda e: e.memset(onesrf[0:1, :], 1.0), writes=["onesrf"])
    a.op("vector", lambda e: e.memset(ones64[0:1, :], 1.0), writes=["ones64"])
    emit_weight_cast(a, wod, wosb, FCH, "wo")
    a.dma("sync", borow[0:1, :], bod[0:1, :], writes=["bo"], tag="ld_bo")
    a.op("vector", lambda e: e.tensor_copy(bob[0:1, :], borow[0:1, :]),
         reads=["bo"], writes=["bob"], tag="bo_cast")

    for b in range(B):
        a.op("vector", lambda e, b=b: e.reduce_max(
            mm_sc[:, b:b + 1], qn2[:, b * TPC:(b + 1) * TPC], AX.X),
            reads=["qn2"], writes=[f"qm/{b}"], tag="qmax")
        a.dma("sync", kn2[:, :], kn2d[:, b * T:(b + 1) * T],
              reads=["kn2"], writes=["kn2"], tag="ld_kn2")
        a.op("vector", lambda e, b=b: e.reduce_max(
            kmx[:, b:b + 1], kn2[:, :], AX.X),
            reads=["kn2"], writes=[f"km/{b}"], tag="kmax")
    a.op("vector", lambda e: e.tensor_tensor(mm_sc[:, :], mm_sc[:, :],
                                             kmx[:, :], ALU.mult),
         reads=[f"qm/{b}" for b in range(B)] + [f"km/{b}" for b in range(B)],
         writes=["prod"], tag="cprod")
    a.op("scalar", lambda e: e.activation(mm_sc[:, :], mm_sc[:, :], AF.Sqrt,
                                          bias=np.float32(0.0),
                                          scale=np.float32(1.0)),
         reads=["prod"], writes=["csqrt"], tag="csqrt")
    a.op("vector", lambda e: e.tensor_scalar(
        mm_sc[:, :], mm_sc[:, :], float(-scale * 1.02),
        -0.01, ALU.mult, ALU.add),
        reads=["csqrt"], writes=["cneg"], tag="cneg")
    a.dma("gpsimd", cbounce[:, :], mm_sc[:, :], reads=["cneg"],
          writes=["cb1"], tag="c_bounce1")
    a.dma("sync", crow[0:1, :], cbounce.rearrange("h b -> () (h b)"),
          reads=["cb1"], writes=["cb2"], tag="c_bounce2")
    a.op("tensor", lambda e: e.matmul(rbps[:, 0:64], onesrf[0:1, :],
                                      crow[0:1, :], start=True, stop=True),
         reads=["cb2", "onesrf"], writes=["rbps"], tag="c_bcast")
    a.op("vector", lambda e: e.tensor_copy(cb[:, :], rbps[:, 0:64]),
         reads=["rbps"], writes=["cb"], tag="c_ev")

    NJ = T // 128
    for b in range(B):
        a.dma("sync", vsb[:, :, :],
              vgd[b, :, :].rearrange("(kt p) v -> p kt v", p=128),
              writes=["vsb"], tag="v_ld")
        for h in range(H):
            par = h % 2
            a.dma("sync", kbuf[par * 64:(par + 1) * 64, :], ktd[h, :, b, :],
                  writes=["kbuf"], tag="k_ld")
            for j in range(NJ):
                pj = j % 2
                a.op("tensor", lambda e, b=b, h=h, j=j, pj=pj, par=par:
                     e.matmul(sps[pj][:, :],
                              kbuf[par * 64:(par + 1) * 64,
                                   j * 128:(j + 1) * 128],
                              qt[par * 64:(par + 1) * 64, h // 2,
                                 b * 512:(b + 1) * 512],
                              start=True, stop=True),
                     reads=["kbuf", "qt"], writes=[f"sps/{pj}"], tag="s_mm")
                a.op("scalar", lambda e, b=b, h=h, pj=pj: e.activation(
                    pbuf[pj][:, :], sps[pj][:, :], AF.Exp,
                    bias=cb[:, h * B + b:h * B + b + 1],
                    scale=np.float32(scale)),
                    reads=[f"sps/{pj}", "cb"], writes=[f"pb/{pj}"], tag="exp")
                if causal:
                    a.op("gpsimd", lambda e, j=j, pj=pj: e.tensor_tensor(
                        pbuf[pj][:, :], pbuf[pj][:, :], masks[:, j, :],
                        ALU.mult),
                        reads=[f"pb/{pj}", "masks"], writes=[f"pb/{pj}"],
                        tag="mask")
                a.op("tensor", lambda e, h=h, j=j, pj=pj: e.matmul(
                    avps[0:65, :], vsb[:, j, h * 65:(h + 1) * 65],
                    pbuf[pj][:, :], start=(j == 0), stop=(j == NJ - 1)),
                    reads=["vsb", f"pb/{pj}"], writes=["avps"], tag="av_mm")
            a.op("vector", lambda e: e.tensor_scalar(
                rrow[0:1, :], avps[64:65, :], 1e-37, None,
                ALU.max),
                reads=["avps"], writes=["rrow"], tag="rho_max")
            a.op("vector", lambda e: e.reciprocal(rrow[0:1, :], rrow[0:1, :]),
                 reads=["rrow"], writes=["rrow"], tag="rho_rec")
            a.op("tensor", lambda e: e.matmul(rbps[0:64, :], ones64[0:1, :],
                                              rrow[0:1, :], start=True,
                                              stop=True),
                 reads=["rrow", "ones64"], writes=["rbps"], tag="rho_bcast")
            a.op("vector", lambda e: e.tensor_copy(rbs[:, :], rbps[0:64, :]),
                 reads=["rbps"], writes=["rbs"], tag="rho_ev")
            a.op("vector", lambda e, b=b, h=h: e.tensor_tensor(
                yT[(h % 2) * 64:(h % 2) * 64 + 64, h // 2,
                   b * 512:(b + 1) * 512],
                avps[0:64, :], rbs[:, :], ALU.mult),
                reads=["avps", "rbs"], writes=[f"y/{h}/{b}"], tag="y_ev")
    ykeys = [f"y/{h}/{b}" for h in range(H) for b in range(B)]
    for t in range(NTILE):
        xb_ = t % 2
        a.dma("sync", xbuf[xb_][:, :], xd[t * 128:(t + 1) * 128, :],
              reads=[f"xb/{xb_}"], writes=[f"xb/{xb_}"], tag="x_ld")
        for dh in range(2):
            pj = dh % 2
            for hc in range(FCH):
                a.op("tensor", lambda e, t=t, dh=dh, hc=hc, pj=pj: e.matmul(
                    sps[pj][:, :], yT[:, hc, t * 128:(t + 1) * 128],
                    wosb[:, hc, dh * 512:(dh + 1) * 512],
                    start=(hc == 0), stop=False),
                    reads=ykeys + [f"wo/{hc}"], writes=[f"sps/{pj}"],
                    tag="o_mm")
            a.op("tensor", lambda e, dh=dh, pj=pj: e.matmul(
                sps[pj][:, :], onesr[0:1, :], bob[0:1, dh * 512:(dh + 1) * 512],
                start=False, stop=True),
                reads=["onesr", "bob"], writes=[f"sps/{pj}"], tag="o_bias")
            a.op("vector", lambda e, dh=dh, pj=pj, xb_=xb_:
                 e.scalar_tensor_tensor(
                     xbuf[xb_][:, dh * 512:(dh + 1) * 512], sps[pj][:, :],
                     1.0, xbuf[xb_][:, dh * 512:(dh + 1) * 512],
                     ALU.mult, ALU.add),
                 reads=[f"sps/{pj}", f"xb/{xb_}"], writes=[f"xb/{xb_}"],
                 tag="o_resid")
        a.dma("gpsimd", xo[t * 128:(t + 1) * 128, :], xbuf[xb_][:, :],
              reads=[f"xb/{xb_}"], writes=[f"xb/{xb_}", "xo_st"],
              tag="x_st")
    a.finish(["xo_st"])
    return a.build()



def build_a0():
    a = ASB("a0")
    xd = a.dram_in("xs", [ROWS, C])
    wdd = a.dram_in("wd", [C, 128])
    wukd = a.dram_in("wukT", [128, FCH, 128])
    wuvd = a.dram_in("wuv", [128, C])
    wqd = a.dram_in("wq", [C, C])
    identd = a.dram_in("ident", [128, 128], BF16)
    lat_o = a.dram_out("lat", [128, ROWS], BF16)
    qat_o = a.dram_out("qat", [16 * 128, ROWS], BF16)
    v_o = a.dram_out("vv", [ROWS, 16 * 65], BF16)
    latmax_o = a.dram_out("latmax", [1, B])
    qamax_o = a.dram_out("qamax", [1, 64])

    xT = emit_stream_prologue(a, xd, identd)
    wsb = a.sb("wsb", [128, FCH, C], BF16)
    wdsb = a.sb("wdsb", [128, FCH, 128], BF16)
    wuk = a.sb("wuk", [128, FCH, 128], BF16)
    wuv = a.sb("wuv_sb", [128, C], BF16)
    onesf = a.sb("onesf", [128, 1], F32)
    latT = a.sb("latT", [128, ROWS], BF16)
    qtb = a.sb("qtb", [128, FCH, ROWS], BF16)
    qac = [a.sb(f"qac_{i}", [128, 512], BF16) for i in range(2)]
    vsb = a.sb("vsb", [128, NTILE, 16 * 65], BF16)
    qf = [a.sb(f"qf_{i}", [128, 512], F32) for i in range(2)]
    sq = [a.sb(f"sq_{i}", [128, 512], F32) for i in range(2)]
    nrow = a.sb("nrow", [1, 512], F32)
    latmax = a.sb("latmax_sb", [1, B], F32)
    qamax = a.sb("qamax_sb", [1, 64], F32)
    pps = [a.ps(f"pps_{i}", [128, 512], F32) for i in range(2)]
    nps = a.ps("nps", [1, 512], F32)

    emit_weight_cast(a, wdd, wdsb, FCH, "wdsb", width=128)
    wukf = a.sb("wukf", [128, FCH, 128], F32)
    a.dma("sync", wukf[:, :, :], wukd[:, :, :], writes=["wukf"], tag="wuk_ld")
    a.op("vector", lambda e: e.tensor_copy(wuk[:, :, :], wukf[:, :, :]),
         reads=["wukf"], writes=["wuk"], tag="wuk_cast")
    wuvf = a.sb("wuvf", [128, C], F32)
    a.dma("sync", wuvf[:, :], wuvd[:, :], writes=["wuvf"], tag="wuv_ld")
    a.op("vector", lambda e: e.tensor_copy(wuv[:, :], wuvf[:, :]),
         reads=["wuvf"], writes=["wuv"], tag="wuv_cast")
    a.op("vector", lambda e: e.memset(onesf[:, 0:1], 1.0), writes=["onesf"])

    for tg in range(4):
        pb_ = tg % 2
        for dc in range(FCH):
            a.op("tensor", lambda e, tg=tg, dc=dc, pb_=pb_: e.matmul(
                pps[pb_][:, :], wdsb[:, dc, :],
                xT[:, dc, tg * 512:(tg + 1) * 512],
                start=(dc == 0), stop=(dc == FCH - 1)),
                reads=[f"wdsb/{dc}"] + xt_keys(tg), writes=[f"pps/{pb_}"],
                tag="lat_mm")
        a.op("scalar", lambda e, pb_=pb_: e.activation(
            qf[pb_][:, :], pps[pb_][:, :], AF.Identity, bias=np.float32(0.0),
            scale=np.float32(1.0)),
            reads=[f"pps/{pb_}"], writes=[f"qf/{pb_}"], tag="lat_ev")
        a.op("vector", lambda e, tg=tg, pb_=pb_: e.tensor_copy(
            latT[:, tg * 512:(tg + 1) * 512], qf[pb_][:, :]),
            reads=[f"qf/{pb_}"], writes=[f"lat/{tg}"], tag="lat_cast")
        a.op("scalar", lambda e, pb_=pb_: e.activation(
            sq[pb_][:, :], qf[pb_][:, :], AF.Square, bias=np.float32(0.0),
            scale=np.float32(1.0)),
            reads=[f"qf/{pb_}"], writes=[f"sqb/{pb_}"], tag="lat_sq")
        a.op("tensor", lambda e, pb_=pb_: e.matmul(
            nps[0:1, :], onesf[:, 0:1], sq[pb_][:, :], start=True, stop=True),
            reads=["onesf", f"sqb/{pb_}"], writes=["nps"], tag="lat_nmm")
        a.op("vector", lambda e: e.tensor_copy(nrow[0:1, :], nps[0:1, :]),
             reads=["nps"], writes=["nrow"], tag="lat_nev")
        a.op("vector", lambda e, tg=tg: e.reduce_max(
            latmax[0:1, tg:tg + 1], nrow[0:1, :], AX.X),
            reads=["nrow"], writes=[f"lm/{tg}", "latmax"], tag="lat_max")
    a.dma("gpsimd", lat_o[:, :], latT[:, :],
          reads=[f"lat/{tg}" for tg in range(4)], writes=["lat_st"],
          tag="lat_st")
    a.dma("gpsimd", latmax_o[0:1, :], latmax[0:1, :],
          reads=[f"lm/{tg}" for tg in range(4)], writes=["lm_st"],
          tag="lm_st")

    emit_weight_cast(a, wqd, wsb, FCH, "wsb")
    for tg in range(4):
        for oc in range(FCH):
            pb_ = oc % 2
            for dc in range(FCH):
                a.op("tensor", lambda e, oc=oc, dc=dc, tg=tg, pb_=pb_:
                     e.matmul(pps[pb_][:, :],
                              wsb[:, dc, oc * 128:(oc + 1) * 128],
                              xT[:, dc, tg * 512:(tg + 1) * 512],
                              start=(dc == 0), stop=(dc == FCH - 1)),
                     reads=[f"wsb/{dc}"] + xt_keys(tg),
                     writes=[f"pps/{pb_}"], tag="q_mm")
            a.op("vector", lambda e, oc=oc, tg=tg, pb_=pb_: e.tensor_copy(
                qtb[:, oc, tg * 512:(tg + 1) * 512], pps[pb_][:, :]),
                reads=[f"pps/{pb_}"], writes=[f"qtb/{oc}/{tg}"], tag="q_cast")

    for h in range(H):
        par = h % 2
        for tg in range(4):
            pb_ = (h * 4 + tg) % 2
            a.op("tensor", lambda e, h=h, tg=tg, pb_=pb_, par=par: e.matmul(
                pps[pb_][:, :], wuk[par * 64:(par + 1) * 64, h // 2, :],
                qtb[par * 64:(par + 1) * 64, h // 2,
                    tg * 512:(tg + 1) * 512],
                start=True, stop=True),
                reads=["wuk", f"qtb/{h // 2}/{tg}"], writes=[f"pps/{pb_}"],
                tag="qa_mm")
            a.op("scalar", lambda e, pb_=pb_: e.activation(
                qf[pb_][:, :], pps[pb_][:, :], AF.Identity,
                bias=np.float32(0.0), scale=np.float32(1.0)),
                reads=[f"pps/{pb_}"], writes=[f"qf/{pb_}"], tag="qa_ev")
            a.op("vector", lambda e, h=h, tg=tg, pb_=pb_: e.tensor_copy(
                qac[pb_][:, :], qf[pb_][:, :]),
                reads=[f"qf/{pb_}"], writes=[f"qac/{pb_}"], tag="qa_cast")
            a.dma("gpsimd",
                  qat_o[h * 128:(h + 1) * 128, tg * 512:(tg + 1) * 512],
                  qac[pb_][:, :], reads=[f"qac/{pb_}"],
                  writes=[f"qac/{pb_}", "qa_st"], tag="qa_st")
            a.op("scalar", lambda e, pb_=pb_: e.activation(
                sq[pb_][:, :], qf[pb_][:, :], AF.Square, bias=np.float32(0.0),
                scale=np.float32(1.0)),
                reads=[f"qf/{pb_}"], writes=[f"sqb/{pb_}"], tag="qa_sq")
            a.op("tensor", lambda e, pb_=pb_: e.matmul(
                nps[0:1, :], onesf[:, 0:1], sq[pb_][:, :], start=True,
                stop=True),
                reads=["onesf", f"sqb/{pb_}"], writes=["nps"], tag="qa_nmm")
            a.op("vector", lambda e: e.tensor_copy(nrow[0:1, :], nps[0:1, :]),
                 reads=["nps"], writes=["nrow"], tag="qa_nev")
            a.op("vector", lambda e, h=h, tg=tg: e.reduce_max(
                qamax[0:1, h * B + tg:h * B + tg + 1], nrow[0:1, :], AX.X),
                reads=["nrow"], writes=[f"qam/{h}/{tg}", "qamax"],
                tag="qa_max")
    a.dma("gpsimd", qamax_o[0:1, :], qamax[0:1, :],
          reads=[f"qam/{h}/{tg}" for h in range(H) for tg in range(4)],
          writes=["qam_st"], tag="qam_st")

    a.op("vector", lambda e: e.memset(vsb[:, :, :].rearrange(
        "p t (h o) -> p t h o", o=65)[:, :, :, 64:65], 1.0),
        writes=["vones"], tag="v_ones")
    for t in range(NTILE):
        for vh in range(2):
            pb_ = (t * 2 + vh) % 2
            a.op("tensor", lambda e, t=t, vh=vh, pb_=pb_: e.matmul(
                pps[pb_][:, :], latT[:, t * 128:(t + 1) * 128],
                wuv[:, vh * 512:(vh + 1) * 512], start=True, stop=True),
                reads=["wuv"] + [f"lat/{tg}" for tg in range(4)],
                writes=[f"pps/{pb_}"], tag="v_mm")
            a.op("vector", lambda e, t=t, vh=vh, pb_=pb_: e.tensor_copy(
                vsb[:, t, :].rearrange("p (h o) -> p h o", o=65)
                [:, vh * 8:(vh + 1) * 8, 0:64],
                pps[pb_][:, :].rearrange("p (h d) -> p h d", d=64)),
                reads=[f"pps/{pb_}", "vones"], writes=[f"vev/{t}/{vh}"],
                tag="v_ev")
    a.dma("gpsimd", v_o.rearrange("(t p) v -> p t v", p=128), vsb[:, :, :],
          reads=[f"vev/{t}/{vh}" for t in range(NTILE) for vh in range(2)],
          writes=["v_st"], tag="v_st")

    a.finish(["lat_st", "lm_st", "qa_st", "qam_st", "v_st"])
    return a.build()



def build_a1(scale=float(D) ** -0.5):
    a = ASB("a1")
    xd = a.dram_in("xs", [ROWS, C])
    qatd = a.dram_in("qat", [16 * 128, ROWS], BF16)
    latgd = a.dram_in("latg", [128, B * T], BF16)
    vgd = a.dram_in("vg", [B, T, 16 * 65], BF16)
    qamaxd = a.dram_in("qamax", [1, 64])
    latmax8 = a.dram_in("latmax8", [1, NCORES * B])
    wod = a.dram_in("wo", [C, C])
    maskd = a.dram_in("masks", [T // 128, 128, 512], BF16)
    xo = a.dram_out("xo", [ROWS, C])

    latsb = a.sb("latsb", [128, B * T], BF16)
    qab = a.sb("qab", [128, H, 512], BF16)
    vsb = a.sb("vsb", [128, T // 128, 16 * 65], BF16)
    masks = a.sb("masksb", [128, T // 128, 512], BF16)
    wosb = a.sb("wosb", [128, FCH, C], BF16)
    onesr = a.sb("onesr", [1, 128], BF16)
    onesrf = a.sb("onesrf", [1, 128], F32)
    ones64 = a.sb("ones64", [1, 64], F32)
    yT = a.sb("yTb", [128, FCH, 512], BF16)
    pbuf = [a.sb(f"pbuf_{i}", [128, 512], BF16) for i in range(2)]
    qamax = a.sb("qamax_sb", [1, 64], F32)
    lm8 = a.sb("lm8", [1, NCORES * B], F32)
    lmb = a.sb("lmb", [1, 64], F32)
    crow = a.sb("crow", [1, 64], F32)
    cb = a.sb("cbsb", [128, 64], F32)
    rrow = a.sb("rrow", [1, 512], F32)
    rbs = a.sb("rbs", [64, 512], F32)
    xbuf = [a.sb(f"xbuf_{i}", [128, C], F32) for i in range(2)]
    sps = [a.ps(f"sps_{i}", [128, 512], F32) for i in range(2)]
    avps = a.ps("avps", [128, 512], F32)
    rbps = a.ps("rbps", [128, 512], F32)

    a.dma("sync", latsb[:, :], latgd[:, :], writes=["latsb"], tag="ld_lat")
    a.dma("sync", masks[:, :, :], maskd.rearrange("j p q -> p j q"),
          writes=["masks"], tag="ld_masks")
    a.dma("sync", qamax[0:1, :], qamaxd[0:1, :], writes=["qamax"],
          tag="ld_qam")
    a.dma("sync", lm8[0:1, :], latmax8[0:1, :], writes=["lm8"], tag="ld_lm8")
    a.op("vector", lambda e: e.memset(onesr[0:1, :], 1.0), writes=["onesr"])
    a.op("vector", lambda e: e.memset(onesrf[0:1, :], 1.0), writes=["onesrf"])
    a.op("vector", lambda e: e.memset(ones64[0:1, :], 1.0), writes=["ones64"])
    emit_weight_cast(a, wod, wosb, FCH, "wo")

    for b_ in range(B):
        a.op("vector", lambda e, b_=b_: e.reduce_max(
            lmb[0:1, b_:b_ + 1],
            lm8[0:1, :].rearrange("p (c b) -> p c b", b=B)[:, :, b_],
            AX.X), reads=["lm8"], writes=[f"lmx/{b_}"], tag="lm_max")
    for b_ in range(B):
        a.op("vector", lambda e, b_=b_: e.tensor_scalar(
            crow[0:1, :].rearrange("p (h b) -> p h b", b=B)[:, :, b_],
            ones64[0:1, 0:16], lmb[0:1, b_:b_ + 1], None, ALU.mult),
            reads=[f"lmx/{b_}", "ones64"], writes=["lmbcast"], tag="lm_bc")
    a.op("vector", lambda e: e.tensor_tensor(crow[0:1, :], crow[0:1, :],
                                             qamax[0:1, :], ALU.mult),
         reads=["lmbcast", "qamax"], writes=["cprod"], tag="cprod")
    a.op("scalar", lambda e: e.activation(crow[0:1, :], crow[0:1, :], AF.Sqrt,
                                          bias=np.float32(0.0),
                                          scale=np.float32(1.0)),
         reads=["cprod"], writes=["csqrt"], tag="csqrt")
    a.op("vector", lambda e: e.tensor_scalar(
        crow[0:1, :], crow[0:1, :], float(-scale * 1.02), -0.01,
        ALU.mult, ALU.add),
        reads=["csqrt"], writes=["cneg"], tag="cneg")
    a.op("tensor", lambda e: e.matmul(rbps[:, 0:64], onesrf[0:1, :],
                                      crow[0:1, :], start=True, stop=True),
         reads=["cneg", "onesrf"], writes=["rbps"], tag="c_bcast")
    a.op("vector", lambda e: e.tensor_copy(cb[:, :], rbps[:, 0:64]),
         reads=["rbps"], writes=["cb"], tag="c_ev")

    NJ = T // 128
    for b in range(B):
        a.dma("sync", vsb[:, :, :],
              vgd[b, :, :].rearrange("(kt p) v -> p kt v", p=128),
              writes=["vsb"], tag="v_ld")
        a.dma("sync", qab[:, :, :],
              qatd.rearrange("(h p) r -> p h r", p=128)
              [:, :, b * 512:(b + 1) * 512],
              writes=["qab"], tag="qa_ld")
        for h in range(H):
            for j in range(NJ):
                pj = j % 2
                a.op("tensor", lambda e, b=b, h=h, j=j, pj=pj: e.matmul(
                    sps[pj][:, :],
                    latsb[:, b * T + j * 128:b * T + (j + 1) * 128],
                    qab[:, h, :], start=True, stop=True),
                    reads=["latsb", "qab"], writes=[f"sps/{pj}"], tag="s_mm")
                a.op("scalar", lambda e, b=b, h=h, pj=pj: e.activation(
                    pbuf[pj][:, :], sps[pj][:, :], AF.Exp,
                    bias=cb[:, h * B + b:h * B + b + 1],
                    scale=np.float32(scale)),
                    reads=[f"sps/{pj}", "cb"], writes=[f"pb/{pj}"], tag="exp")
                a.op("gpsimd", lambda e, j=j, pj=pj: e.tensor_tensor(
                    pbuf[pj][:, :], pbuf[pj][:, :], masks[:, j, :], ALU.mult),
                    reads=[f"pb/{pj}", "masks"], writes=[f"pb/{pj}"],
                    tag="mask")
                a.op("tensor", lambda e, h=h, j=j, pj=pj: e.matmul(
                    avps[0:65, :], vsb[:, j, h * 65:(h + 1) * 65],
                    pbuf[pj][:, :], start=(j == 0), stop=(j == NJ - 1)),
                    reads=["vsb", f"pb/{pj}"], writes=["avps"], tag="av_mm")
            a.op("vector", lambda e: e.tensor_scalar(
                rrow[0:1, :], avps[64:65, :], 1e-37, None, ALU.max),
                reads=["avps"], writes=["rrow"], tag="rho_max")
            a.op("vector", lambda e: e.reciprocal(rrow[0:1, :], rrow[0:1, :]),
                 reads=["rrow"], writes=["rrow"], tag="rho_rec")
            a.op("tensor", lambda e: e.matmul(rbps[0:64, :], ones64[0:1, :],
                                              rrow[0:1, :], start=True,
                                              stop=True),
                 reads=["rrow", "ones64"], writes=["rbps"], tag="rho_bcast")
            a.op("vector", lambda e: e.tensor_copy(rbs[:, :], rbps[0:64, :]),
                 reads=["rbps"], writes=["rbs"], tag="rho_ev")
            a.op("vector", lambda e, h=h: e.tensor_tensor(
                yT[(h % 2) * 64:(h % 2) * 64 + 64, h // 2, :],
                avps[0:64, :], rbs[:, :], ALU.mult),
                reads=["avps", "rbs"], writes=[f"y/{h}"], tag="y_ev")
        ykeys = [f"y/{h}" for h in range(H)]
        for i in range(4):
            t = b * 4 + i
            xb_ = i % 2
            a.dma("sync", xbuf[xb_][:, :], xd[t * 128:(t + 1) * 128, :],
                  reads=[f"xb/{xb_}"], writes=[f"xb/{xb_}"], tag="x_ld")
            for dh in range(2):
                pj = dh % 2
                for hc in range(FCH):
                    a.op("tensor", lambda e, i=i, dh=dh, hc=hc, pj=pj:
                         e.matmul(sps[pj][:, :],
                                  yT[:, hc, i * 128:(i + 1) * 128],
                                  wosb[:, hc, dh * 512:(dh + 1) * 512],
                                  start=(hc == 0), stop=(hc == FCH - 1)),
                         reads=ykeys + [f"wo/{hc}"], writes=[f"sps/{pj}"],
                         tag="o_mm")
                a.op("vector", lambda e, dh=dh, pj=pj, xb_=xb_:
                     e.scalar_tensor_tensor(
                         xbuf[xb_][:, dh * 512:(dh + 1) * 512], sps[pj][:, :],
                         1.0, xbuf[xb_][:, dh * 512:(dh + 1) * 512],
                         ALU.mult, ALU.add),
                     reads=[f"sps/{pj}", f"xb/{xb_}"], writes=[f"xb/{xb_}"],
                     tag="o_resid")
            a.dma("gpsimd", xo[t * 128:(t + 1) * 128, :], xbuf[xb_][:, :],
                  reads=[f"xb/{xb_}"], writes=[f"xb/{xb_}", "xo_st"],
                  tag="x_st")
    a.finish(["xo_st"])
    return a.build()



EPS_B = 1e-6
SQEPS2 = 2.0 * math.sqrt(EPS_B)


def build_b0():
    a = ASB("b0")
    xd = a.dram_in("xs", [ROWS, C])
    wqkvd = a.dram_in("wqkv", [C, 3 * C])
    bqc_d = a.dram_in("bqc", [128, FCH])
    bkc_d = a.dram_in("bkc", [128, FCH])
    bvr_d = a.dram_in("bvr", [1, C])
    ehd = a.dram_in("eh", [128, FCH, 16])
    identd = a.dram_in("ident", [128, 128], BF16)
    qaug_o = a.dram_out("qaug", [16, 68, ROWS], BF16)
    kaug_o = a.dram_out("kaug", [16, 68, ROWS], BF16)
    v_o = a.dram_out("vv", [ROWS, 16 * 65], BF16)

    xT = emit_stream_prologue(a, xd, identd)
    wsb = a.sb("wsb", [128, FCH, C], BF16)
    bcol = a.sb("bcol", [128, FCH], F32)
    ehf = a.sb("ehf", [128, FCH, 16], F32)
    bvr = a.sb("bvr_sb", [1, C], F32)
    bvb = a.sb("bvb", [1, C], BF16)
    onesr = a.sb("onesr", [1, 128], BF16)
    qf = [a.sb(f"qf_{i}", [128, 512], F32) for i in range(2)]
    sq = [a.sb(f"sq_{i}", [128, 512], F32) for i in range(2)]
    qb = [a.sb(f"qb_{i}", [128, 512], BF16) for i in range(2)]
    n2b = a.sb("n2b", [16, ROWS], BF16)
    vsb = a.sb("vsb", [128, NTILE, 16 * 65], BF16)
    consts = a.sb("consts", [1, ROWS], BF16)
    pps = [a.ps(f"pps_{i}", [128, 512], F32) for i in range(2)]
    nps = a.ps("nps", [16, 512], F32)

    a.dma("sync", ehf[:, :, :], ehd[:, :, :], writes=["ehf"], tag="ld_eh")
    a.dma("sync", bvr[0:1, :], bvr_d[0:1, :], writes=["bvr"], tag="ld_bv")
    a.op("vector", lambda e: e.tensor_copy(bvb[0:1, :], bvr[0:1, :]),
         reads=["bvr"], writes=["bvb"], tag="bv_cast")
    a.op("vector", lambda e: e.memset(onesr[0:1, :], 1.0), writes=["onesr"])

    def proj_aug(col_off, bd_, out_dram, scale_cast, nrm_scale, c64,
                 pname):
        emit_weight_cast(a, wqkvd[:, col_off:col_off + C], wsb, FCH, "wsb")
        a.dma("sync", bcol[:, :], bd_[:, :], reads=["bcol"], writes=["bcol"],
              tag=f"{pname}_bld")
        for tg in range(4):
            for oc in range(FCH):
                pb_ = oc % 2
                for dc in range(FCH):
                    a.op("tensor", lambda e, oc=oc, dc=dc, tg=tg, pb_=pb_:
                         e.matmul(pps[pb_][:, :],
                                  wsb[:, dc, oc * 128:(oc + 1) * 128],
                                  xT[:, dc, tg * 512:(tg + 1) * 512],
                                  start=(dc == 0), stop=(dc == FCH - 1)),
                         reads=[f"wsb/{dc}"] + xt_keys(tg),
                         writes=[f"pps/{pb_}"], tag=f"{pname}_mm")
                a.op("scalar", lambda e, oc=oc, pb_=pb_: e.activation(
                    qf[pb_][:, :], pps[pb_][:, :], AF.Identity,
                    bias=bcol[:, oc:oc + 1], scale=np.float32(1.0)),
                    reads=[f"pps/{pb_}", "bcol"], writes=[f"qf/{pb_}"],
                    tag=f"{pname}_ev")
                a.op("vector", lambda e, pb_=pb_: e.tensor_scalar(
                    qb[pb_][:, :], qf[pb_][:, :], float(scale_cast), None,
                    ALU.mult),
                    reads=[f"qf/{pb_}"], writes=[f"qb/{pb_}"],
                    tag=f"{pname}_cast")
                for hh in range(2):
                    h = 2 * oc + hh
                    a.dma("gpsimd",
                          out_dram[h, 0:64, tg * 512:(tg + 1) * 512],
                          qb[pb_][hh * 64:(hh + 1) * 64, :],
                          reads=[f"qb/{pb_}"], writes=[f"qb/{pb_}", "aug_st"],
                          tag=f"{pname}_st")
                a.op("scalar", lambda e, pb_=pb_: e.activation(
                    sq[pb_][:, :], qf[pb_][:, :], AF.Square,
                    bias=np.float32(0.0), scale=np.float32(1.0)),
                    reads=[f"qf/{pb_}"], writes=[f"sqb/{pb_}"],
                    tag=f"{pname}_sq")
                a.op("tensor", lambda e, oc=oc, pb_=pb_: e.matmul(
                    nps[:, :], ehf[:, oc, :], sq[pb_][:, :],
                    start=(oc == 0), stop=(oc == FCH - 1)),
                    reads=["ehf", f"sqb/{pb_}"], writes=["nps"],
                    tag=f"{pname}_emm")
            a.op("vector", lambda e, tg=tg: e.tensor_scalar(
                n2b[:, tg * 512:(tg + 1) * 512], nps[:, :],
                float(nrm_scale), None, ALU.mult),
                reads=["nps"], writes=[f"n2b/{tg}"], tag=f"{pname}_n2")
        nrow_idx = 64 if c64 is None else 65
        for h in range(H):
            a.dma("gpsimd", out_dram[h, nrow_idx:nrow_idx + 1, :],
                  n2b[h:h + 1, :],
                  reads=[f"n2b/{tg}" for tg in range(4)],
                  writes=["aug_st"], tag=f"{pname}_nst")
        crow_idx = 65 if c64 is None else 64
        cval = 1.0 if c64 is None else c64
        for val, row in ((cval, crow_idx), (SQEPS2, 66), (0.0, 67)):
            a.op("vector", lambda e, val=val: e.memset(consts[0:1, :], val),
                 writes=["consts"], tag="c_set")
            for h in range(H):
                a.dma("gpsimd", out_dram[h, row:row + 1, :], consts[0:1, :],
                      reads=["consts"], writes=["aug_st"], tag=f"{pname}_cst")

    proj_aug(0, bqc_d, qaug_o, 4.0, 1.0, None, "q")
    proj_aug(C, bkc_d, kaug_o, -2.0, 4.0, 4.0, "k")

    emit_weight_cast(a, wqkvd[:, 2 * C:3 * C], wsb, FCH, "wsb")
    a.op("vector", lambda e: e.memset(vsb[:, :, :].rearrange(
        "p t (h o) -> p t h o", o=65)[:, :, :, 64:65], 1.0),
        writes=["vones"], tag="v_ones")
    for t in range(NTILE):
        for vh in range(2):
            pb_ = (t * 2 + vh) % 2
            for dc in range(FCH):
                a.op("tensor", lambda e, t=t, vh=vh, dc=dc, pb_=pb_: e.matmul(
                    pps[pb_][:, :], xT[:, dc, t * 128:(t + 1) * 128],
                    wsb[:, dc, vh * 512:(vh + 1) * 512],
                    start=(dc == 0), stop=False),
                    reads=[f"wsb/{dc}", f"xT/{t}/{dc}"], writes=[f"pps/{pb_}"],
                    tag="v_mm")
            a.op("tensor", lambda e, vh=vh, pb_=pb_: e.matmul(
                pps[pb_][:, :], onesr[0:1, :], bvb[0:1, vh * 512:(vh + 1) * 512],
                start=False, stop=True),
                reads=["onesr", "bvb"], writes=[f"pps/{pb_}"], tag="v_bias")
            a.op("vector", lambda e, t=t, vh=vh, pb_=pb_: e.tensor_copy(
                vsb[:, t, :].rearrange("p (h o) -> p h o", o=65)
                [:, vh * 8:(vh + 1) * 8, 0:64],
                pps[pb_][:, :].rearrange("p (h d) -> p h d", d=64)),
                reads=[f"pps/{pb_}", "vones"], writes=[f"vev/{t}/{vh}"],
                tag="v_ev")
    a.dma("gpsimd", v_o.rearrange("(t p) v -> p t v", p=128), vsb[:, :, :],
          reads=[f"vev/{t}/{vh}" for t in range(NTILE) for vh in range(2)],
          writes=["v_st"], tag="v_st")

    a.finish(["aug_st", "v_st"])
    return a.build()



def build_b1(debug=False):
    a = ASB("b1")
    xd = a.dram_in("xs", [ROWS, C])
    qaugd = a.dram_in("qaug", [16, 68, ROWS], BF16)
    kaugd = a.dram_in("kaugg", [16, 68, B, T], BF16)
    vgd = a.dram_in("vg", [B, T, 16 * 65], BF16)
    wod = a.dram_in("wo", [C, C])
    bod = a.dram_in("bor", [1, C])
    maskd = a.dram_in("masks", [T // 128, 128, 512], BF16)
    identd = a.dram_in("ident", [128, 128], BF16)
    xo = a.dram_out("xo", [ROWS, C])

    qaug = a.sb("qaugsb", [68, 16, 512], BF16)
    vsb = a.sb("vsb", [128, T // 128, 16 * 65], BF16)
    kbuf = a.sb("kbuf", [68, T], BF16)
    ident = a.sb("identsb", [128, 128], BF16)
    wosb = a.sb("wosb", [128, FCH, C], BF16)
    borow = a.sb("borow", [1, C], F32)
    bob = a.sb("bob", [1, C], BF16)
    onesr = a.sb("onesr", [1, 128], BF16)
    onesrf = a.sb("onesrf", [1, 128], F32)
    ones64 = a.sb("ones64", [1, 64], F32)
    yT = a.sb("yTb", [128, FCH, 512], BF16)
    masks = a.sb("masksb", [128, T // 128, 512], BF16)
    sbig = a.sb("sbig", [128, T // 128, 256], F32)
    d2 = [a.sb(f"d2_{i}", [128, 256], F32) for i in range(2)]
    rc = [a.sb(f"rc_{i}", [128, 256], F32) for i in range(2)]
    pbuf = [a.sb(f"pbuf_{i}", [128, 256], BF16) for i in range(2)]
    colmax = a.sb("colmax", [128, 1], F32)
    cmrow = a.sb("cmrow", [1, 128], F32)
    cneg = a.sb("cneg1", [1, 1], F32)
    cb = a.sb("cbsb", [128, 1], F32)
    rrow = a.sb("rrow", [1, 256], F32)
    rbs = a.sb("rbs", [64, 256], F32)
    xbuf = [a.sb(f"xbuf_{i}", [128, C], F32) for i in range(2)]
    sps = [a.ps(f"sps_{i}", [128, 256], F32) for i in range(2)]
    dps = [a.ps(f"dps_{i}", [128, 256], F32) for i in range(2)]
    avps = a.ps("avps", [128, 256], F32)
    rbps = a.ps("rbps", [128, 256], F32)
    tps = a.ps("tps", [128, 128], BF16)
    ops2 = [a.ps("ops2_0", [128, 512], F32)]
    cmb = a.sb("cmb", [128, 128], BF16)

    a.dma("sync", masks[:, :, :], maskd.rearrange("j p q -> p j q"),
          writes=["masks"], tag="ld_masks")
    a.dma("sync", ident[:], identd[:, :], writes=["ident"], tag="ld_ident")
    a.op("vector", lambda e: e.memset(onesr[0:1, :], 1.0), writes=["onesr"])
    a.op("vector", lambda e: e.memset(onesrf[0:1, :], 1.0), writes=["onesrf"])
    a.op("vector", lambda e: e.memset(ones64[0:1, :], 1.0), writes=["ones64"])
    a.op("vector", lambda e: e.memset(cmb[:, :], 0.0), writes=["cmb"],
         tag="init")
    a.op("vector", lambda e: e.memset(yT[:, :, :], 0.0),
         writes=[f"y/{h}" for h in range(H)], tag="init")
    emit_weight_cast(a, wod, wosb, FCH, "wo")
    a.dma("sync", borow[0:1, :], bod[0:1, :], writes=["bo"], tag="ld_bo")
    a.op("vector", lambda e: e.tensor_copy(bob[0:1, :], borow[0:1, :]),
         reads=["bo"], writes=["bob"], tag="bo_cast")

    NJ = T // 128
    if debug:
        a._cdbg = a.dram_out("dbg_cd", [B * H * 2, 1])
    for b in range(B):
        a.dma("sync", vsb[:, :, :],
              vgd[b, :, :].rearrange("(kt p) v -> p kt v", p=128),
              writes=["vsb"], tag="v_ld")
        a.dma("sync", qaug[:, :, :],
              qaugd.rearrange("h a r -> a h r")
              [:, :, b * 512:(b + 1) * 512],
              writes=["qaug"], tag="qa_ld")
        for h in range(H):
            a.dma("sync", kbuf[:, :], kaugd[h, :, b, :],
                  writes=["kbuf"], tag="k_ld")
            for qh in range(2):
                qsl = slice(qh * 256, (qh + 1) * 256)
                for j in range(NJ):
                    pj = j % 2
                    a.op("tensor", lambda e, j=j, pj=pj, h=h, qh=qh: e.matmul(
                        sps[pj][:, :], kbuf[0:64, j * 128:(j + 1) * 128],
                        qaug[0:64, h, qh * 256:(qh + 1) * 256],
                        start=True, stop=True),
                        reads=["kbuf", "qaug"], writes=[f"sps/{pj}"],
                        tag="dot_mm")
                    a.op("tensor", lambda e, j=j, pj=pj, h=h, qh=qh: e.matmul(
                        dps[pj][:, :], kbuf[0:68, j * 128:(j + 1) * 128],
                        qaug[0:68, h, qh * 256:(qh + 1) * 256],
                        start=True, stop=True),
                        reads=["kbuf", "qaug"], writes=[f"dps/{pj}"],
                        tag="den_mm")
                    a.op("scalar", lambda e, pj=pj: e.activation(
                        d2[pj][:, :], sps[pj][:, :], AF.Square,
                        bias=np.float32(0.0), scale=np.float32(0.25)),
                        reads=[f"sps/{pj}"], writes=[f"d2/{pj}"], tag="sq")
                    a.op("vector", lambda e, pj=pj: e.reciprocal(
                        rc[pj][:, :], dps[pj][:, :]),
                        reads=[f"dps/{pj}"], writes=[f"rc/{pj}"], tag="recip")
                    a.op("gpsimd", lambda e, j=j, pj=pj: e.tensor_tensor(
                        sbig[:, j, :], d2[pj][:, :], rc[pj][:, :], ALU.mult),
                        reads=[f"d2/{pj}", f"rc/{pj}"], writes=[f"sb/{j}"],
                        tag="smul")
                a.op("vector", lambda e: e.reduce_max(
                    colmax[:, 0:1],
                    sbig[:, :, :].rearrange("p j q -> p (j q)"), AX.X),
                    reads=[f"sb/{j}" for j in range(NJ)], writes=["colmax"],
                    tag="cmax")
                a.op("vector", lambda e: e.tensor_copy(cmb[:, 0:1],
                                                       colmax[:, 0:1]),
                     reads=["colmax"], writes=["cmb"], tag="cm_cast")
                a.op("tensor", lambda e: e.transpose(
                    tps[:, 0:128], cmb[:, 0:128], ident[:]),
                    reads=["cmb", "ident"], writes=["tps"], tag="cm_tr")
                a.op("vector", lambda e: e.tensor_copy(cmrow[0:1, :],
                                                       tps[0:1, 0:128]),
                     reads=["tps"], writes=["cmrow"], tag="cm_ev")
                a.op("vector", lambda e: e.reduce_max(
                    cneg[0:1, 0:1], cmrow[0:1, :], AX.X),
                    reads=["cmrow"], writes=["cneg"], tag="cm_red")
                a.op("vector", lambda e: e.tensor_scalar(
                    cneg[0:1, 0:1], cneg[0:1, 0:1], -1.0, None, ALU.mult),
                    reads=["cneg"], writes=["cneg"], tag="cm_neg")
                a.op("tensor", lambda e: e.matmul(
                    rbps[:, 0:1], onesrf[0:1, :], cneg[0:1, 0:1],
                    start=True, stop=True),
                    reads=["cneg", "onesrf"], writes=["rbps"], tag="c_bc")
                a.op("vector", lambda e: e.tensor_copy(cb[:, 0:1],
                                                       rbps[:, 0:1]),
                     reads=["rbps"], writes=["cb"], tag="c_ev")
                if debug:
                    idx = (b * H + h) * 2 + qh
                    a.dma("gpsimd", a._cdbg[idx:idx + 1, 0:1], cneg[0:1, 0:1],
                          reads=["cneg"], writes=["cdbg_st"], tag="cdbg")
                for j in range(NJ):
                    pj = j % 2
                    a.op("scalar", lambda e, j=j, pj=pj: e.activation(
                        pbuf[pj][:, :], sbig[:, j, :], AF.Exp,
                        bias=cb[:, 0:1], scale=np.float32(1.0)),
                        reads=[f"sb/{j}", "cb"], writes=[f"pb/{pj}"],
                        tag="exp")
                    a.op("gpsimd", lambda e, j=j, pj=pj, qh=qh:
                         e.tensor_tensor(
                             pbuf[pj][:, :], pbuf[pj][:, :],
                             masks[:, j, qh * 256:(qh + 1) * 256], ALU.mult),
                         reads=[f"pb/{pj}", "masks"], writes=[f"pb/{pj}"],
                         tag="mask")
                    a.op("tensor", lambda e, h=h, j=j, pj=pj: e.matmul(
                        avps[0:65, :], vsb[:, j, h * 65:(h + 1) * 65],
                        pbuf[pj][:, :], start=(j == 0), stop=(j == NJ - 1)),
                        reads=["vsb", f"pb/{pj}"], writes=["avps"],
                        tag="av_mm")
                a.op("vector", lambda e: e.tensor_scalar(
                    rrow[0:1, :], avps[64:65, :], 1e-37, None, ALU.max),
                    reads=["avps"], writes=["rrow"], tag="rho_max")
                a.op("vector", lambda e: e.reciprocal(rrow[0:1, :],
                                                      rrow[0:1, :]),
                     reads=["rrow"], writes=["rrow"], tag="rho_rec")
                a.op("tensor", lambda e: e.matmul(
                    rbps[0:64, 0:256], ones64[0:1, :], rrow[0:1, :],
                    start=True, stop=True),
                    reads=["rrow", "ones64"], writes=["rbps"], tag="rho_bc")
                a.op("vector", lambda e: e.tensor_copy(rbs[:, :],
                                                       rbps[0:64, 0:256]),
                     reads=["rbps"], writes=["rbs"], tag="rho_ev")
                a.op("vector", lambda e, h=h, qsl=qsl: e.tensor_tensor(
                    yT[(h % 2) * 64:(h % 2) * 64 + 64, h // 2, qsl],
                    avps[0:64, :], rbs[:, :], ALU.mult),
                    reads=["avps", "rbs"], writes=[f"y/{h}"], tag="y_ev")
        ykeys = [f"y/{h}" for h in range(H)]
        for i in range(4):
            t = b * 4 + i
            xb_ = i % 2
            a.dma("sync", xbuf[xb_][:, :], xd[t * 128:(t + 1) * 128, :],
                  reads=[f"xb/{xb_}"], writes=[f"xb/{xb_}"], tag="x_ld")
            for dh in range(2):
                for hc in range(FCH):
                    a.op("tensor", lambda e, i=i, dh=dh, hc=hc:
                         e.matmul(ops2[0][:, :],
                                  yT[:, hc, i * 128:(i + 1) * 128],
                                  wosb[:, hc, dh * 512:(dh + 1) * 512],
                                  start=(hc == 0), stop=False),
                         reads=ykeys + [f"wo/{hc}"], writes=["ops2/0"],
                         tag="o_mm")
                a.op("tensor", lambda e, dh=dh: e.matmul(
                    ops2[0][:, :], onesr[0:1, :],
                    bob[0:1, dh * 512:(dh + 1) * 512],
                    start=False, stop=True),
                    reads=["onesr", "bob"], writes=["ops2/0"],
                    tag="o_bias")
                a.op("vector", lambda e, dh=dh, xb_=xb_:
                     e.scalar_tensor_tensor(
                         xbuf[xb_][:, dh * 512:(dh + 1) * 512],
                         ops2[0][:, :], 1.0,
                         xbuf[xb_][:, dh * 512:(dh + 1) * 512],
                         ALU.mult, ALU.add),
                     reads=["ops2/0", f"xb/{xb_}"], writes=[f"xb/{xb_}"],
                     tag="o_resid")
            a.dma("gpsimd", xo[t * 128:(t + 1) * 128, :], xbuf[xb_][:, :],
                  reads=[f"xb/{xb_}"], writes=[f"xb/{xb_}", "xo_st"],
                  tag="x_st")
    if debug:
        sdbg = a.dram_out("dbg_s", [128, T // 128, 256])
        ydbg = a.dram_out("dbg_y", [128, FCH, 512], BF16)
        a.dma("gpsimd", sdbg[:, :, :], sbig[:, :, :],
              reads=[f"sb/{j}" for j in range(T // 128)], writes=["dbg_st"],
              tag="dbg")
        a.dma("gpsimd", ydbg[:, :, :], yT[:, :, :],
              reads=[f"y/{h}" for h in range(H)], writes=["dbg_st"], tag="dbg")
        a.finish(["xo_st", "dbg_st", "cdbg_st"])
    else:
        a.finish(["xo_st"])
    return a.build()



def build_g():
    a = ASB("g")
    xd = a.dram_in("xs", [ROWS, C])
    wqd = a.dram_in("wqT", [C, C])
    wkd = a.dram_in("wkT", [C, C])
    wvd = a.dram_in("wvT", [C, C])
    wod = a.dram_in("woT", [C, C])
    wwd = a.dram_in("wwT", [C, 16])
    wrd = a.dram_in("wrT", [C, 16])
    bqd = a.dram_in("bqc", [128, FCH])
    bkd = a.dram_in("bkc", [128, FCH])
    bvd = a.dram_in("bvc", [128, FCH])
    bod = a.dram_in("bor", [1, C])
    bwd = a.dram_in("bwc", [16, 1])
    brd = a.dram_in("brc", [16, 1])
    ehd = a.dram_in("eh", [128, FCH, 16])
    seld = a.dram_in("sel", [16, FCH, 128])
    identd = a.dram_in("ident", [128, 128], BF16)
    xo = a.dram_out("xo", [ROWS, C])

    xT = emit_stream_prologue(a, xd, identd, nbuf=1)
    xln_t = a._xln_tensor
    qn = xln_t[:, :, :].rearrange("p a b -> p (a b)").rearrange(
        "p (c r) -> p c r", c=FCH)
    wsb = a.sb("wsb", [128, FCH, C], BF16)
    wsmall = a.sb("wsmall", [128, FCH, 16], BF16)
    bcol = a.sb("bcol", [128, FCH], F32)
    bcol16 = a.sb("bcol16", [16, 1], F32)
    ehb = a.sb("ehb", [128, FCH, 16], BF16)
    ehf = a.sb("ehf", [128, FCH, 16], F32)
    self_ = a.sb("self_", [16, FCH, 128], F32)
    selb = a.sb("selb", [16, FCH, 128], BF16)
    ee = a.sb("ee", [128, 128], BF16)
    onesr = a.sb("onesr", [1, 128], BF16)
    borow = a.sb("borow", [1, C], F32)
    bob = a.sb("bob", [1, C], BF16)
    kn = a.sb("kn", [128, FCH, ROWS], BF16)
    vw = a.sb("vw", [128, FCH, ROWS], BF16)
    wrows = a.sb("wrows", [16, ROWS], BF16)
    rrows = a.sb("rrows", [16, ROWS], BF16)
    grows = a.sb("grows", [16, 512], BF16)
    qf = [a.sb("qf_0", [128, 512], F32)]
    sqb = [a.sb("sqb_0", [128, 512], BF16)]
    rn = [a.sb("rn_0", [128, 512], F32)]
    prodb = a.sb("prodb", [128, 512], BF16)
    pr2 = a.sb("pr2", [128, 512], BF16)
    outacc = a.sb("outacc", [128, FCH, 512], BF16)
    yTb = a.sb("yTb", [128, FCH, 512], BF16)
    xbuf = [a.sb("xbuf_0", [128, C], F32)]
    pps = [a.ps(f"pps_{i}", [128, 512], F32) for i in range(2)]
    bps = a.ps("bps", [128, 512], F32)
    nsqps = a.ps("nsqps", [128, 512], F32)
    gps = a.ps("gps", [16, 512], F32)
    ops2 = a.ps("ops2", [128, 512], F32)

    a.op("sync", lambda e: e.nop(),
         reads=[f"xT/{t}/{c}" for t in range(NTILE) for c in range(FCH)],
         writes=["xlnfree"], tag="xln_fence")

    a.dma("sync", ehf[:, :, :], ehd[:, :, :], writes=["ehf"], tag="ld_eh")
    a.op("vector", lambda e: e.tensor_copy(ehb[:, :, :], ehf[:, :, :]),
         reads=["ehf"], writes=["ehb"], tag="eh_cast")
    a.dma("sync", self_[:, :, :], seld[:, :, :], writes=["self"], tag="ld_sel")
    a.op("vector", lambda e: e.tensor_copy(selb[:, :, :], self_[:, :, :]),
         reads=["self"], writes=["selb"], tag="sel_cast")
    a.op("vector", lambda e: e.memset(ee[:, :], 0.0), writes=["ee"], tag="ee0")
    a.op("vector", lambda e: e.memset(ee[0:64, 0:64], 1.0),
         reads=["ee"], writes=["ee"], tag="ee1")
    a.op("vector", lambda e: e.memset(ee[64:128, 64:128], 1.0),
         reads=["ee"], writes=["ee"], tag="ee2")
    a.op("vector", lambda e: e.memset(onesr[0:1, :], 1.0), writes=["onesr"])
    a.dma("sync", borow[0:1, :], bod[0:1, :], writes=["bo"], tag="ld_bo")
    a.op("vector", lambda e: e.tensor_copy(bob[0:1, :], borow[0:1, :]),
         reads=["bo"], writes=["bob"], tag="bo_cast")

    for name, wdram, bdram, dst in (("w", wwd, bwd, wrows),
                                    ("r", wrd, brd, rrows)):
        emit_weight_cast(a, wdram, wsmall, FCH, "wsmall", width=16)
        a.dma("sync", bcol16[:, 0:1], bdram[:, :], reads=["bcol16"],
              writes=["bcol16"], tag=f"{name}_bld")
        for tg in range(4):
            for dc in range(FCH):
                a.op("tensor", lambda e, tg=tg, dc=dc: e.matmul(
                    gps[:, :], wsmall[:, dc, :],
                    xT[:, dc, tg * 512:(tg + 1) * 512],
                    start=(dc == 0), stop=(dc == FCH - 1)),
                    reads=[f"wsmall/{dc}"] + xt_keys(tg), writes=["gps"],
                    tag=f"{name}_mm")
            a.op("scalar", lambda e, tg=tg, dst=dst: e.activation(
                dst[:, tg * 512:(tg + 1) * 512], gps[:, :], AF.Sigmoid,
                bias=bcol16[:, 0:1], scale=np.float32(1.0)),
                reads=["gps", "bcol16"], writes=[f"{name}rows/{tg}"],
                tag=f"{name}_ev")
    wrkeys = [f"wrows/{tg}" for tg in range(4)]
    rrkeys = [f"rrows/{tg}" for tg in range(4)]

    def proj_norm(wdram, bdram, dst, dkey, pname):
        emit_weight_cast(a, wdram, wsb, FCH, "wsb")
        a.dma("sync", bcol[:, :], bdram[:, :], reads=["bcol"],
              writes=["bcol"], tag=f"{pname}_bld")
        for tg in range(4):
            for oc in range(FCH):
                pb_ = 0
                for dc in range(FCH):
                    a.op("tensor", lambda e, oc=oc, dc=dc, tg=tg, pb_=pb_:
                         e.matmul(pps[pb_][:, :],
                                  wsb[:, dc, oc * 128:(oc + 1) * 128],
                                  xT[:, dc, tg * 512:(tg + 1) * 512],
                                  start=(dc == 0), stop=(dc == FCH - 1)),
                         reads=[f"wsb/{dc}"] + xt_keys(tg),
                         writes=[f"pps/{pb_}"], tag=f"{pname}_mm")
                a.op("scalar", lambda e, oc=oc, pb_=pb_: e.activation(
                    qf[pb_][:, :], pps[pb_][:, :], AF.Identity,
                    bias=bcol[:, oc:oc + 1], scale=np.float32(1.0)),
                    reads=[f"pps/{pb_}", "bcol"], writes=[f"qf/{pb_}"],
                    tag=f"{pname}_ev")
                a.op("scalar", lambda e, pb_=pb_: e.activation(
                    sqb[pb_][:, :], qf[pb_][:, :], AF.Square,
                    bias=np.float32(0.0), scale=np.float32(1.0)),
                    reads=[f"qf/{pb_}"], writes=[f"sqb/{pb_}"],
                    tag=f"{pname}_sq")
                a.op("tensor", lambda e, pb_=pb_: e.matmul(
                    nsqps[:, :], ee[:, :], sqb[pb_][:, :],
                    start=True, stop=True),
                    reads=["ee", f"sqb/{pb_}"], writes=["nsqps"],
                    tag=f"{pname}_eemm")
                a.op("vector", lambda e, pb_=pb_: e.tensor_scalar(
                    rn[pb_][:, :], nsqps[:, :], 1e-24, None, ALU.max),
                    reads=["nsqps"], writes=[f"rn/{pb_}"], tag=f"{pname}_clip")
                a.op("scalar", lambda e, pb_=pb_: e.activation(
                    rn[pb_][:, :], rn[pb_][:, :], AF.Ln,
                    bias=np.float32(0.0), scale=np.float32(1.0)),
                    reads=[f"rn/{pb_}"], writes=[f"rn/{pb_}"],
                    tag=f"{pname}_ln")
                a.op("scalar", lambda e, pb_=pb_: e.activation(
                    rn[pb_][:, :], rn[pb_][:, :], AF.Exp,
                    bias=np.float32(0.0), scale=np.float32(-0.5)),
                    reads=[f"rn/{pb_}"], writes=[f"rn/{pb_}"],
                    tag=f"{pname}_exp")
                a.op("vector", lambda e, oc=oc, tg=tg, pb_=pb_, dst=dst:
                     e.tensor_tensor(dst[:, oc, tg * 512:(tg + 1) * 512],
                                     qf[pb_][:, :], rn[pb_][:, :], ALU.mult),
                     reads=[f"qf/{pb_}", f"rn/{pb_}", "xlnfree"],
                     writes=[f"{dkey}/{oc}/{tg}"], tag=f"{pname}_n")

    proj_norm(wqd, bqd, qn, "qn", "q")
    proj_norm(wkd, bkd, kn, "kn", "k")

    emit_weight_cast(a, wvd, wsb, FCH, "wsb")
    a.dma("sync", bcol[:, :], bvd[:, :], reads=["bcol"], writes=["bcol"],
          tag="v_bld")
    for tg in range(4):
        for oc in range(FCH):
            pb_ = 0
            for dc in range(FCH):
                a.op("tensor", lambda e, oc=oc, dc=dc, tg=tg, pb_=pb_:
                     e.matmul(pps[pb_][:, :],
                              wsb[:, dc, oc * 128:(oc + 1) * 128],
                              xT[:, dc, tg * 512:(tg + 1) * 512],
                              start=(dc == 0), stop=(dc == FCH - 1)),
                     reads=[f"wsb/{dc}"] + xt_keys(tg),
                     writes=[f"pps/{pb_}"], tag="v_mm")
            a.op("scalar", lambda e, oc=oc, pb_=pb_: e.activation(
                qf[pb_][:, :], pps[pb_][:, :], AF.Identity,
                bias=bcol[:, oc:oc + 1], scale=np.float32(1.0)),
                reads=[f"pps/{pb_}", "bcol"], writes=[f"qf/{pb_}"],
                tag="v_ev")
            a.op("tensor", lambda e, oc=oc, tg=tg: e.matmul(
                bps[:, :], selb[:, oc, :],
                wrows[:, tg * 512:(tg + 1) * 512],
                start=True, stop=True),
                reads=["selb"] + wrkeys, writes=["bps"], tag="v_selmm")
            a.op("vector", lambda e, oc=oc, tg=tg, pb_=pb_: e.tensor_tensor(
                vw[:, oc, tg * 512:(tg + 1) * 512], qf[pb_][:, :],
                bps[:, :], ALU.mult),
                reads=[f"qf/{pb_}", "bps"], writes=[f"vw/{oc}/{tg}"],
                tag="v_n")

    qnkeys = [f"qn/{oc}/{tg}" for oc in range(FCH) for tg in range(4)]
    knkeys = [f"kn/{oc}/{tg}" for oc in range(FCH) for tg in range(4)]
    vwkeys = [f"vw/{oc}/{tg}" for oc in range(FCH) for tg in range(4)]
    for s in range(B):
        scols = slice(s * 512, (s + 1) * 512)
        for s2 in range(B):
            s2cols = slice(s2 * 512, (s2 + 1) * 512)
            for oc in range(FCH):
                a.op("vector", lambda e, oc=oc, scols=scols, s2cols=s2cols:
                     e.tensor_tensor(prodb[:, :], qn[:, oc, scols],
                                     kn[:, oc, s2cols], ALU.mult),
                     reads=qnkeys + knkeys, writes=["prodb"], tag="g_tt")
                a.op("tensor", lambda e, oc=oc: e.matmul(
                    gps[:, :], ehb[:, oc, :], prodb[:, :],
                    start=(oc == 0), stop=(oc == FCH - 1)),
                    reads=["ehb", "prodb"], writes=["gps"], tag="g_mm")
            a.op("vector", lambda e: e.tensor_copy(grows[:, :], gps[:, :]),
                 reads=["gps"], writes=["grows"], tag="g_ev")
            for oc in range(FCH):
                a.op("tensor", lambda e, oc=oc: e.matmul(
                    bps[:, :], selb[:, oc, :], grows[:, :],
                    start=True, stop=True),
                    reads=["selb", "grows"], writes=["bps"], tag="g_selmm")
                if s2 == 0:
                    a.op("vector", lambda e, oc=oc, s2cols=s2cols:
                         e.tensor_tensor(outacc[:, oc, :],
                                         vw[:, oc, s2cols], bps[:, :],
                                         ALU.mult),
                         reads=vwkeys + ["bps"], writes=[f"oa/{oc}"],
                         tag="g_out0")
                else:
                    a.op("vector", lambda e, oc=oc, s2cols=s2cols:
                         e.tensor_tensor(pr2[:, :], vw[:, oc, s2cols],
                                         bps[:, :], ALU.mult),
                         reads=vwkeys + ["bps"], writes=["pr2"], tag="g_outm")
                    a.op("vector", lambda e, oc=oc: e.tensor_add(
                        outacc[:, oc, :], outacc[:, oc, :], pr2[:, :]),
                        reads=["pr2", f"oa/{oc}"], writes=[f"oa/{oc}"],
                        tag="g_outa")
        for oc in range(FCH):
            a.op("tensor", lambda e, oc=oc, scols=scols: e.matmul(
                bps[:, :], selb[:, oc, :], rrows[:, scols],
                start=True, stop=True),
                reads=["selb"] + rrkeys, writes=["bps"], tag="r_selmm")
            a.op("vector", lambda e, oc=oc: e.tensor_tensor(
                yTb[:, oc, :], outacc[:, oc, :], bps[:, :], ALU.mult),
                reads=[f"oa/{oc}", "bps"], writes=[f"y/{oc}"], tag="r_gate")
        emit_weight_cast(a, wod, wsb, FCH, "wsb") if s == 0 else None
        ykeys = [f"y/{oc}" for oc in range(FCH)]
        for i in range(4):
            t = s * 4 + i
            xb_ = 0
            a.dma("sync", xbuf[xb_][:, :], xd[t * 128:(t + 1) * 128, :],
                  reads=[f"xb/{xb_}"], writes=[f"xb/{xb_}"], tag="x_ld")
            for dh in range(2):
                for hc in range(FCH):
                    a.op("tensor", lambda e, i=i, dh=dh, hc=hc: e.matmul(
                        ops2[:, :], yTb[:, hc, i * 128:(i + 1) * 128],
                        wsb[:, hc, dh * 512:(dh + 1) * 512],
                        start=(hc == 0), stop=False),
                        reads=ykeys + [f"wsb/{hc}"], writes=["ops2"],
                        tag="o_mm")
                a.op("tensor", lambda e, dh=dh: e.matmul(
                    ops2[:, :], onesr[0:1, :],
                    bob[0:1, dh * 512:(dh + 1) * 512],
                    start=False, stop=True),
                    reads=["onesr", "bob"], writes=["ops2"], tag="o_bias")
                a.op("vector", lambda e, dh=dh, xb_=xb_:
                     e.scalar_tensor_tensor(
                         xbuf[xb_][:, dh * 512:(dh + 1) * 512], ops2[:, :],
                         1.0, xbuf[xb_][:, dh * 512:(dh + 1) * 512],
                         ALU.mult, ALU.add),
                     reads=["ops2", f"xb/{xb_}"], writes=[f"xb/{xb_}"],
                     tag="o_resid")
            a.dma("gpsimd", xo[t * 128:(t + 1) * 128, :], xbuf[xb_][:, :],
                  reads=[f"xb/{xb_}"], writes=[f"xb/{xb_}", "xo_st"],
                  tag="x_st")
    a.finish(["xo_st"])
    return a.build()



def build_f1(debug=False):
    a = ASB("f1")
    xd = a.dram_in("xs", [ROWS, C])
    t1d = a.dram_in("t1r", [HCH, 128, C], BF16)
    t2d = a.dram_in("t2", [HID, C], BF16)
    mwd = a.dram_in("mw", [1, 2])
    cwd = a.dram_in("convw", [128, HCH, 3])
    cbd = a.dram_in("convb", [128, HCH])
    halod = a.dram_in("halox", [8, C])
    identd = a.dram_in("ident", [128, 128], BF16)
    xo = a.dram_out("xo", [ROWS, C])

    ident = a.sb("identsb", [128, 128], BF16)
    xt_ = [a.sb(f"xt_{i}", [128, C], F32) for i in range(2)]
    bn6 = a.sb("bn6p", [128, 12], F32)
    mv = a.sb("mvp", [128, 2], F32)
    lncol = a.sb("lncol", [128, 2], F32)
    ylt = a.sb("ylt", [128, C], F32)
    yi = a.sb("yi", [128, C], I32)
    nsb = a.sb("nsb", [128, NTILE, C], BF16)
    nT = a.sb("nT", [128, FCH, ROWS], BF16)
    mxc = a.sb("mxc", [128, NTILE], F32)
    scol = a.sb("scol", [128, NTILE], F32)
    hbcol = a.sb("hbcol", [128, NTILE], F32)
    mwcol = a.sb("mwcol", [128, 2], F32)
    mwsb = a.sb("mwsb", [1, 2], F32)
    onesrf = a.sb("onesrf", [1, 128], F32)
    hx = a.sb("hx", [8, C], F32)
    hbn = a.sb("hbn", [8, 12], F32)
    hmv = a.sb("hmv", [8, 2], F32)
    hln = a.sb("hln", [8, 2], F32)
    hy = a.sb("hy", [8, C], F32)
    hyi = a.sb("hyi", [8, C], I32)
    hnb = a.sb("hnb", [8, C], BF16)
    hmx = a.sb("hmx", [8, 1], F32)
    hsc = a.sb("hsc", [8, 1], F32)
    hhb = a.sb("hhb", [8, 1], F32)
    nTh = a.sb("nTh", [128, FCH, 8], BF16)
    hrow = a.sb("hrow", [1, 8], F32)
    srow = a.sb("srow", [1, ROWS], F32)
    s2row = a.sb("s2row", [1, 1024], F32)
    hh8 = a.sb("hh8", [128, 8], F32)
    sbb = a.sb("sbb", [128, 1024], F32)
    s2bb = a.sb("s2bb", [128, 1024], F32)
    shalo = a.sb("shalo", [128, 8], F32)
    t1b = [a.sb(f"t1b_{i}", [128, C], BF16) for i in range(2)]
    t2b = [a.sb(f"t2b_{i}", [128, 512], BF16) for i in range(2)]
    cw = a.sb("cwsb", [128, HCH, 3], F32)
    cbc = a.sb("cbsb2", [128, HCH], F32)
    hbuf = a.sb("hbuf", [128, 2, 514], F32)
    cvb = a.sb("cvb", [128, 512], F32)
    shbuf = a.sb("shbuf", [128, HCH, 1024], BF16)
    mh = a.sb("mh", [128, 1024], F32)
    mhb = a.sb("mhb", [128, 1024], BF16)
    mx2 = a.sb("mx2", [128, 8], F32)
    s2col = a.sb("s2col", [128, 8], F32)
    s2inv = a.sb("s2inv", [128, 8], F32)
    xbuf = a.sb("xbuf_0", [128, C], F32)
    bounce = a.dram_tmp("bounce", [128, NTILE])
    bounce2 = a.dram_tmp("bounce2", [128, 8])
    hps = [a.ps(f"hps_{i}", [128, 512], F32) for i in range(2)]
    hhps = a.ps("hhps", [128, 8], F32)
    bcps = a.ps("bcps", [128, 512], F32)
    ops = [a.ps(f"ops_{i}", [128, 512], F32) for i in range(2)]
    tpbanks = [a.ps("tpA", [128, 512], BF16), a.ps("tpB", [128, 512], BF16)]
    tps = tpbanks[0]

    a.dma("sync", ident[:], identd[:, :], writes=["ident"], tag="ld_ident")
    a.dma("sync", cw[:, :, :], cwd[:, :, :], writes=["cw"], tag="ld_cw")
    a.dma("sync", cbc[:, :], cbd[:, :], writes=["cb"], tag="ld_cb")
    a.dma("sync", mwsb[0:1, :], mwd[0:1, :], writes=["mw"], tag="ld_mw")
    a.op("vector", lambda e: e.memset(onesrf[0:1, :], 1.0), writes=["onesrf"])
    a.op("tensor", lambda e: e.matmul(bcps[:, 0:2], onesrf[0:1, :],
                                      mwsb[0:1, 0:2], start=True, stop=True),
         reads=["onesrf", "mw"], writes=["bcps"], tag="mw_bc")
    a.op("vector", lambda e: e.tensor_copy(mwcol[:, :], bcps[:, 0:2]),
         reads=["bcps"], writes=["mwcol"], tag="mw_ev")

    for t in range(NTILE):
        b = t % 2
        a.dma("sync", xt_[b][:, :], xd[t * 128:(t + 1) * 128, :],
              reads=[f"xt/{b}"], writes=[f"xt/{b}"], tag="x_ld")
        for s in range(2):
            a.op("vector", lambda e, s=s, b=b: e.bn_stats(
                bn6[:, s * 6:(s + 1) * 6], xt_[b][:, s * 512:(s + 1) * 512]),
                reads=[f"xt/{b}"], writes=[f"bn6/{s}"], tag="bn")
        a.op("vector", lambda e: e.bn_aggr(mv[:, 0:2], bn6[:, 0:12]),
             reads=["bn6/0", "bn6/1"], writes=["mv"], tag="bnag")
        a.op("vector", lambda e: e.tensor_scalar(
            lncol[:, 0:1], mv[:, 1:2], EPS_LN, None, ALU.add),
            reads=["mv"], writes=["lntmp"], tag="ln_eps")
        a.op("scalar", lambda e: e.activation(
            lncol[:, 0:1], lncol[:, 0:1], AF.Ln, bias=np.float32(0.0),
            scale=np.float32(1.0)),
            reads=["lntmp"], writes=["lnl"], tag="ln_ln")
        a.op("scalar", lambda e: e.activation(
            lncol[:, 0:1], lncol[:, 0:1], AF.Exp, bias=np.float32(0.0),
            scale=np.float32(-0.5)),
            reads=["lnl"], writes=["lnsc"], tag="ln_exp")
        a.op("vector", lambda e: e.tensor_tensor(
            lncol[:, 1:2], mv[:, 0:1], lncol[:, 0:1], ALU.mult),
            reads=["mv", "lnsc"], writes=["lnb0"], tag="ln_b0")
        a.op("vector", lambda e: e.tensor_scalar(
            lncol[:, 1:2], lncol[:, 1:2], -1.0, None, ALU.mult),
            reads=["lnb0"], writes=["lnbv"], tag="ln_b1")
        a.op("scalar", lambda e, t=t, b=b: e.activation(
            ylt[:, :], xt_[b][:, :], AF.Identity, bias=lncol[:, 1:2],
            scale=lncol[:, 0:1]),
            reads=[f"xt/{b}", "lnsc", "lnbv"], writes=["ylt"], tag="ln_ap")
        a.op("vector", lambda e, t=t: e.reduce_max(
            mxc[:, t:t + 1], ylt[:, :], AX.X, apply_absolute_value=True),
            reads=["ylt"], writes=[f"mx/{t}"], tag="q_max")
        a.op("vector", lambda e, t=t: e.tensor_scalar(
            scol[:, t:t + 1], mxc[:, t:t + 1], 1e-5, None, ALU.max),
            reads=[f"mx/{t}"], writes=[f"mxc/{t}"], tag="q_clip")
        a.op("vector", lambda e, t=t: e.tensor_scalar(
            hbcol[:, t:t + 1], scol[:, t:t + 1], mwcol[:, 0:1],
            float(1.0 / 127), ALU.mult, ALU.mult),
            reads=[f"mxc/{t}", "mwcol"], writes=[f"hb/{t}"], tag="q_hb")
        a.op("vector", lambda e, t=t: e.reciprocal(
            scol[:, t:t + 1], scol[:, t:t + 1]),
            reads=[f"mxc/{t}"], writes=[f"sc0/{t}"], tag="q_rec")
        a.op("vector", lambda e, t=t: e.tensor_scalar(
            scol[:, t:t + 1], scol[:, t:t + 1], 127.0, None, ALU.mult),
            reads=[f"sc0/{t}"], writes=[f"sc/{t}"], tag="q_s")
        a.op("vector", lambda e, t=t: e.tensor_scalar(
            ylt[:, :], ylt[:, :], scol[:, t:t + 1], None, ALU.mult),
            reads=["ylt", f"sc/{t}"], writes=["ylt"], tag="q_mul")
        a.op("vector", lambda e: e.tensor_copy(yi[:, :], ylt[:, :]),
             reads=["ylt"], writes=["yi"], tag="q_rnd")
        a.op("vector", lambda e, t=t: e.tensor_scalar(
            nsb[:, t, :], yi[:, :], 127.0, -128.0, ALU.min, ALU.max),
            reads=["yi"], writes=[f"n/{t}"], tag="q_clip2")

    emit_transposes(a, nsb, nT, NTILE, ident, lambda t: f"n/{t}",
                    lambda t, c: f"nT/{t}/{c}", banks=tpbanks,
                    bankkeys=["tps", "tpsB"])

    a.dma("sync", hx[:, :], halod[:, :], writes=["hx"], tag="h_ld")
    for s in range(2):
        a.op("vector", lambda e, s=s: e.bn_stats(
            hbn[:, s * 6:(s + 1) * 6], hx[:, s * 512:(s + 1) * 512]),
            reads=["hx"], writes=[f"hbn/{s}"], tag="h_bn")
    a.op("vector", lambda e: e.bn_aggr(hmv[:, 0:2], hbn[:, 0:12]),
         reads=["hbn/0", "hbn/1"], writes=["hmv"], tag="h_bnag")
    a.op("vector", lambda e: e.tensor_scalar(
        hln[:, 0:1], hmv[:, 1:2], EPS_LN, None, ALU.add),
        reads=["hmv"], writes=["hlnt"], tag="h_eps")
    a.op("scalar", lambda e: e.activation(hln[:, 0:1], hln[:, 0:1], AF.Ln,
                                          bias=np.float32(0.0),
                                          scale=np.float32(1.0)),
         reads=["hlnt"], writes=["hlnl"], tag="h_ln")
    a.op("scalar", lambda e: e.activation(hln[:, 0:1], hln[:, 0:1], AF.Exp,
                                          bias=np.float32(0.0),
                                          scale=np.float32(-0.5)),
         reads=["hlnl"], writes=["hlns"], tag="h_exp")
    a.op("vector", lambda e: e.tensor_tensor(hln[:, 1:2], hmv[:, 0:1],
                                             hln[:, 0:1], ALU.mult),
         reads=["hmv", "hlns"], writes=["hlb0"], tag="h_b0")
    a.op("vector", lambda e: e.tensor_scalar(hln[:, 1:2], hln[:, 1:2], -1.0,
                                             None, ALU.mult),
         reads=["hlb0"], writes=["hlbv"], tag="h_b1")
    a.op("scalar", lambda e: e.activation(hy[:, :], hx[:, :], AF.Identity,
                                          bias=hln[:, 1:2],
                                          scale=hln[:, 0:1]),
         reads=["hx", "hlns", "hlbv"], writes=["hy"], tag="h_ap")
    a.op("vector", lambda e: e.reduce_max(hmx[:, 0:1], hy[:, :], AX.X,
                                          apply_absolute_value=True),
         reads=["hy"], writes=["hmx"], tag="h_max")
    a.op("vector", lambda e: e.tensor_scalar(hmx[:, 0:1], hmx[:, 0:1], 1e-5,
                                             None, ALU.max),
         reads=["hmx"], writes=["hmx"], tag="h_clip")
    a.op("vector", lambda e: e.tensor_scalar(
        hhb[:, 0:1], hmx[:, 0:1], mwcol[0:8, 0:1], float(1.0 / 127),
        ALU.mult, ALU.mult),
        reads=["hmx", "mwcol"], writes=["hhb"], tag="h_hb")
    a.op("vector", lambda e: e.reciprocal(hsc[:, 0:1], hmx[:, 0:1]),
         reads=["hmx"], writes=["hsc"], tag="h_rec")
    a.op("vector", lambda e: e.tensor_scalar(hsc[:, 0:1], hsc[:, 0:1], 127.0,
                                             None, ALU.mult),
         reads=["hsc"], writes=["hsc"], tag="h_s")
    a.op("vector", lambda e: e.tensor_scalar(hy[:, :], hy[:, :],
                                             hsc[:, 0:1], None, ALU.mult),
         reads=["hy", "hsc"], writes=["hy"], tag="h_mul")
    a.op("vector", lambda e: e.tensor_copy(hyi[:, :], hy[:, :]),
         reads=["hy"], writes=["hyi"], tag="h_rnd")
    a.op("vector", lambda e: e.tensor_scalar(hnb[:, :], hyi[:, :], 127.0,
                                             -128.0, ALU.min, ALU.max),
         reads=["hyi"], writes=["hnb"], tag="h_q")
    for c in range(FCH):
        a.op("tensor", lambda e, c=c: e.transpose(
            tps[:, 0:8], hnb[:, c * 128:(c + 1) * 128], ident[0:8, 0:8]),
            reads=["hnb", "ident"], writes=["tps"], tag="h_tr")
        a.op("vector", lambda e, c=c: e.tensor_copy(nTh[:, c, :],
                                                    tps[:, 0:8]),
            reads=["tps"], writes=[f"nTh/{c}"], tag="h_trev")
    a.dma("gpsimd", bounce2[0:8, 0:1], hhb[:, 0:1], reads=["hhb"],
          writes=["hbb1"], tag="h_bounce1",
          allow_slow_non_contiguous=True)
    a.dma("sync", hrow[0:1, 0:8], bounce2[0:8, 0:1].rearrange("a () -> () a"),
          reads=["hbb1"], writes=["hrow"], tag="h_bounce2",
          allow_slow_non_contiguous=True)
    a.op("tensor", lambda e: e.matmul(bcps[:, 0:8], onesrf[0:1, :],
                                      hrow[0:1, 0:8], start=True, stop=True),
         reads=["hrow", "onesrf"], writes=["bcps"], tag="h_bc")
    a.op("vector", lambda e: e.tensor_copy(shalo[:, 0:8], bcps[:, 0:8]),
         reads=["bcps"], writes=["shalo"], tag="h_bcev")

    a.dma("gpsimd", bounce[:, :], hbcol[:, :],
          reads=[f"hb/{t}" for t in range(NTILE)], writes=["bb1"],
          tag="s_bounce1")
    a.dma("sync", srow[0:1, :].rearrange("a (t p) -> a t p", p=128),
          bounce.rearrange("p t -> () t p"),
          reads=["bb1"], writes=["srow"], tag="s_bounce2",
          allow_slow_non_contiguous=True)

    for bh in range(2):
        base = bh * 1024
        for g in range(2):
            a.op("tensor", lambda e, g=g: e.matmul(
                bcps[:, :], onesrf[0:1, :],
                srow[0:1, base + g * 512:base + (g + 1) * 512],
                start=True, stop=True),
                reads=["srow", "onesrf"], writes=["bcps"], tag="sb_bc")
            a.op("vector", lambda e, g=g: e.tensor_copy(
                sbb[:, g * 512:(g + 1) * 512], bcps[:, :]),
                reads=["bcps"], writes=["sbb"], tag="sb_ev")
        for hc in range(HCH):
            tb = hc % 2
            a.dma("sync", t1b[tb][:, :], t1d[hc, :, :],
                  reads=[f"t1b/{tb}"], writes=[f"t1b/{tb}"], tag="t1_ld")
            for g in range(2):
                pg = g % 2
                for dc in range(FCH):
                    a.op("tensor", lambda e, hc=hc, dc=dc, g=g, tb=tb, pg=pg,
                         base=base: e.matmul(
                             hps[pg][:, :], t1b[tb][:, dc * 128:(dc + 1) * 128],
                             nT[:, dc, base + g * 512:base + (g + 1) * 512],
                             start=(dc == 0), stop=(dc == FCH - 1)),
                         reads=[f"t1b/{tb}"] +
                         [f"nT/{t}/{dc}" for t in range(bh * 8 + g * 4,
                                                        bh * 8 + g * 4 + 4)],
                         writes=[f"hps/{pg}"], tag="mm1")
                bb = g
                a.op("vector", lambda e, g=g, pg=pg: e.tensor_tensor(
                    hbuf[:, g, 1:513], hps[pg][:, :],
                    sbb[:, g * 512:(g + 1) * 512], ALU.mult),
                    reads=[f"hps/{pg}", "sbb"], writes=[f"hbuf/{g}"],
                    tag="h_evac")
            for dc in range(FCH):
                a.op("tensor", lambda e, hc=hc, dc=dc, tb=tb: e.matmul(
                    hhps[:, 0:8], t1b[tb][:, dc * 128:(dc + 1) * 128],
                    nTh[:, dc, :], start=(dc == 0), stop=(dc == FCH - 1)),
                    reads=[f"t1b/{tb}", f"nTh/{dc}"], writes=["hhps"],
                    tag="mm1h")
            a.op("vector", lambda e: e.tensor_tensor(
                hh8[:, 0:8], hhps[:, 0:8], shalo[:, 0:8], ALU.mult),
                reads=["hhps", "shalo"], writes=["hh"], tag="h_hevac")
            for g in range(2):
                b_glob = bh * 2 + g
                a.op("vector", lambda e, g=g, b_glob=b_glob: e.tensor_copy(
                    hbuf[:, g, 0:1], hh8[:, 2 * b_glob:2 * b_glob + 1]),
                    reads=["hh"], writes=[f"hbuf/{g}"], tag="h_hl")
                a.op("vector", lambda e, g=g, b_glob=b_glob: e.tensor_copy(
                    hbuf[:, g, 513:514], hh8[:, 2 * b_glob + 1:2 * b_glob + 2]),
                    reads=["hh"], writes=[f"hbuf/{g}"], tag="h_hr")
            for g in range(2):
                a.op("vector", lambda e, hc=hc, g=g: e.tensor_scalar(
                    cvb[:, 0:512], hbuf[:, g, 0:512], cw[:, hc, 0:1],
                    cbc[:, hc:hc + 1], ALU.mult, ALU.add),
                    reads=[f"hbuf/{g}", "cw", "cb"], writes=["cvb"],
                    tag="conv0")
                a.op("vector", lambda e, hc=hc, g=g: e.scalar_tensor_tensor(
                    cvb[:, 0:512], hbuf[:, g, 1:513], cw[:, hc, 1:2],
                    cvb[:, 0:512], ALU.mult, ALU.add),
                    reads=[f"hbuf/{g}", "cw", "cvb"], writes=["cvb"],
                    tag="conv1")
                a.op("vector", lambda e, hc=hc, g=g: e.scalar_tensor_tensor(
                    cvb[:, 0:512], hbuf[:, g, 2:514], cw[:, hc, 2:3],
                    cvb[:, 0:512], ALU.mult, ALU.add),
                    reads=[f"hbuf/{g}", "cw", "cvb"], writes=["cvb"],
                    tag="conv2")
                a.op("scalar", lambda e, hc=hc, g=g: e.activation(
                    shbuf[:, hc, g * 512:(g + 1) * 512], cvb[:, 0:512],
                    AF.Silu, bias=np.float32(0.0), scale=np.float32(1.0)),
                    reads=["cvb"], writes=[f"sh/{hc}"], tag="silu")
            if hc == 0:
                a.op("vector", lambda e: e.memset(mh[:, :], 0.0),
                     reads=["mhk"], writes=["mhk"], tag="mh0")
            a.op("vector", lambda e, hc=hc: e.scalar_tensor_tensor(
                ylt[:, :], shbuf[:, hc, :], -1.0, shbuf[:, hc, :],
                ALU.mult, ALU.max),
                reads=[f"sh/{hc}"], writes=["ylt"], tag="mh_abs")
            a.op("vector", lambda e: e.tensor_max(mh[:, :], mh[:, :],
                                                  ylt[:, :]),
                reads=["ylt", "mhk"], writes=["mhk"], tag="mh_acc")
        a.op("vector", lambda e: e.tensor_copy(mhb[:, :], mh[:, :]),
             reads=["mhk"], writes=["mhb"], tag="mh_cast")
        for i in range(8):
            a.op("tensor", lambda e, i=i: e.transpose(
                tps[:, 0:128], mhb[:, i * 128:(i + 1) * 128], ident[:]),
                reads=["mhb", "ident"], writes=["tps"], tag="mh_tr")
            a.op("vector", lambda e, i=i: e.reduce_max(
                mx2[:, i:i + 1], tps[:, 0:128], AX.X),
                reads=["tps"], writes=[f"mx2/{i}"], tag="mh_red")
        mx2keys = [f"mx2/{i}" for i in range(8)]
        a.op("vector", lambda e: e.tensor_scalar(
            mx2[:, :], mx2[:, :], 1e-5, None, ALU.max),
            reads=mx2keys, writes=["mx2c"], tag="s2_clip")
        a.op("vector", lambda e: e.tensor_scalar(
            s2inv[:, :], mx2[:, :], mwcol[:, 1:2], float(1.0 / 127),
            ALU.mult, ALU.mult),
            reads=["mx2c", "mwcol"], writes=["s2inv"], tag="s2_inv")
        a.op("vector", lambda e: e.reciprocal(s2col[:, :], mx2[:, :]),
             reads=["mx2c"], writes=["s2r"], tag="s2_rec")
        a.op("vector", lambda e: e.tensor_scalar(
            s2col[:, :], s2col[:, :], 127.0, None, ALU.mult),
            reads=["s2r"], writes=["s2col"], tag="s2_s")
        a.dma("gpsimd", bounce2[:, 0:8], s2col[:, :], reads=["s2col"],
              writes=["bb2"], tag="s2_bounce1")
        a.dma("sync", s2row[0:1, 0:1024].rearrange("a (t p) -> a t p", p=128),
              bounce2[:, 0:8].rearrange("p t -> () t p"),
              reads=["bb2", "s2row"], writes=["s2row"], tag="s2_bounce2",
              allow_slow_non_contiguous=True)
        for g in range(2):
            a.op("tensor", lambda e, g=g: e.matmul(
                bcps[:, :], onesrf[0:1, :],
                s2row[0:1, g * 512:(g + 1) * 512], start=True, stop=True),
                reads=["s2row", "onesrf"], writes=["bcps"], tag="s2_bc")
            a.op("vector", lambda e, g=g: e.tensor_copy(
                s2bb[:, g * 512:(g + 1) * 512], bcps[:, :]),
                reads=["bcps"], writes=["s2bb"], tag="s2_ev")
        for hc in range(HCH):
            a.op("vector", lambda e, hc=hc: e.tensor_tensor(
                ylt[:, :], shbuf[:, hc, :], s2bb[:, :], ALU.mult),
                reads=[f"sh/{hc}", "s2bb"], writes=["ylt"], tag="q2_mul")
            a.op("vector", lambda e: e.tensor_copy(yi[:, :], ylt[:, :]),
                 reads=["ylt"], writes=["yi"], tag="q2_rnd")
            a.op("vector", lambda e, hc=hc: e.tensor_scalar(
                shbuf[:, hc, :], yi[:, :], 127.0, -128.0, ALU.min, ALU.max),
                reads=["yi"], writes=[f"sh/{hc}"], tag="q2_clip")
        for tg2 in range(4):
            for dh in range(2):
                for hc in range(HCH):
                    tb = hc % 2
                    a.dma("scalar", t2b[tb][:, :],
                          t2d[hc * 128:(hc + 1) * 128,
                              dh * 512:(dh + 1) * 512],
                          reads=[f"t2b/{tb}"], writes=[f"t2b/{tb}"],
                          tag="t2_ld")
                    for i in range(2):
                        tt = tg2 * 2 + i
                        a.op("tensor", lambda e, hc=hc, i=i, tt=tt, tb=tb:
                             e.matmul(ops[i][:, :],
                                      shbuf[:, hc, tt * 128:(tt + 1) * 128],
                                      t2b[tb][:, :],
                                      start=(hc == 0), stop=(hc == HCH - 1)),
                             reads=[f"t2b/{tb}"] +
                             [f"sh/{hcx}" for hcx in range(HCH)],
                             writes=[f"ops/{i}"], tag="mm2")
                for i in range(2):
                    tt = tg2 * 2 + i
                    t_glob = bh * 8 + tt
                    a.dma("sync", xbuf[:, :],
                          xd[t_glob * 128:(t_glob + 1) * 128, :],
                          reads=["xbuf"], writes=["xbuf"], tag="x2_ld")
                    a.op("vector", lambda e, i=i, tt=tt, dh=dh:
                         e.scalar_tensor_tensor(
                             xbuf[:, dh * 512:(dh + 1) * 512], ops[i][:, :],
                             s2inv[:, tt:tt + 1],
                             xbuf[:, dh * 512:(dh + 1) * 512],
                             ALU.mult, ALU.add),
                         reads=[f"ops/{i}", "xbuf", "s2inv"],
                         writes=["xbuf"], tag="o_resid")
                    a.dma("gpsimd",
                          xo[t_glob * 128:(t_glob + 1) * 128,
                             dh * 512:(dh + 1) * 512],
                          xbuf[:, dh * 512:(dh + 1) * 512],
                          reads=["xbuf"], writes=["xbuf", "xo_st"],
                          tag="x_st")
    if debug:
        nd = a.dram_out("dbg_n", [ROWS, C], BF16)
        hd = a.dram_out("dbg_h", [128, 2, 514])
        shd = a.dram_out("dbg_sh", [HID, 1024], BF16)
        s2d = a.dram_out("dbg_s2inv", [128, 8])
        sbd = a.dram_out("dbg_sbb", [128, 1024])
        a.dma("gpsimd", nd.rearrange("(n p) f -> p n f", p=128), nsb[:, :, :],
              reads=[f"n/{t}" for t in range(NTILE)], writes=["dbg"],
              tag="dbg")
        a.dma("gpsimd", hd[:, :, :], hbuf[:, :, :],
              reads=["hbuf/0", "hbuf/1"], writes=["dbg"], tag="dbg")
        a.dma("gpsimd", shd.rearrange("(hc p) f -> p hc f", p=128),
              shbuf[:, :, :],
              reads=[f"sh/{hc}" for hc in range(HCH)], writes=["dbg"],
              tag="dbg")
        a.dma("gpsimd", s2d[:, :], s2inv[:, :], reads=["s2inv"],
              writes=["dbg"], tag="dbg")
        a.dma("gpsimd", sbd[:, :], sbb[:, :], reads=["sbb"], writes=["dbg"],
              tag="dbg")
        a.finish(["xo_st", "dbg"])
    else:
        a.finish(["xo_st"])
    return a.build()



_PROGS = {}


def _prog(name, builder):
    if name not in _PROGS:
        _PROGS[name] = builder()
    return _PROGS[name]


def _consts():
    ident = np.eye(128, dtype=BFNP)
    eh = np.zeros((128, FCH, 16), np.float32)
    for f in range(C):
        eh[f % 128, f // 128, f // 64] = 1.0
    sel = np.zeros((16, FCH, 128), np.float32)
    for oc in range(FCH):
        sel[2 * oc, oc, 0:64] = 1.0
        sel[2 * oc + 1, oc, 64:128] = 1.0
    masks = []
    for c in range(NCORES):
        kk = np.arange(T)[:, None]
        qq = TPC * c + np.arange(TPC)[None, :]
        m = (kk <= qq).astype(BFNP).reshape(T // 128, 128, TPC)
        masks.append(np.ascontiguousarray(m))
    return ident, eh, sel, masks


def _run(prog, maps):
    res = run_bass_kernel_spmd(prog.nc, maps, core_ids=list(range(NCORES)))
    return res.results


def _colsplit(v, n=128):
    v = np.asarray(v, np.float32)
    return np.ascontiguousarray(v.reshape(-1, n).T)


def kernel(**inputs):
    inp = {k: np.asarray(v, np.float32) for k, v in inputs.items()}
    ident, eh, sel, masks = _consts()
    x = np.ascontiguousarray(inp["L0_attn_a_x"], dtype=np.float32)

    w0p = _prog("w0", build_w0)
    m = {
        "w1t": np.ascontiguousarray(inp["L0_ffn_w1"].T),
        "w2t": np.ascontiguousarray(inp["L0_ffn_w2"].T),
        "b1": inp["L0_ffn_b1"].reshape(1, -1),
        "b2": inp["L0_ffn_b2"].reshape(1, -1),
        "v1t": np.ascontiguousarray(inp["L1_ffn_w1"].T),
        "v2t": np.ascontiguousarray(inp["L1_ffn_w2"].T),
    }
    w0 = run_bass_kernel_spmd(w0p.nc, [m], core_ids=[0]).results[0]
    m1, m2, mb1c, b2q = _f0_host_inputs(w0["m1t"], w0["m2t"], w0["mb1"],
                                        w0["b2q1024"])
    t1r = np.ascontiguousarray(
        np.asarray(w0["t1t"]).reshape(FCH, 128, HCH, 128)
        .transpose(2, 1, 0, 3)).reshape(HCH, 128, C)
    t2 = np.ascontiguousarray(np.asarray(w0["t2t"]))
    mw = np.asarray(w0["mw"])
    convw = np.ascontiguousarray(
        inp["L1_ffn_conv_w"].reshape(HID, 3).reshape(HCH, 128, 3)
        .transpose(1, 0, 2))
    convb = _colsplit(inp["L1_ffn_conv_b"])

    def run_f0(x):
        shards = _shard_rows(x)
        maps = [{"xs": s, "m1": m1, "m2": m2, "mb1c": mb1c, "b2q": b2q,
                 "ident": ident} for s in shards]
        res = _run(_prog("f0", build_f0), maps)
        return _unshard_rows([r["xo"] for r in res])

    def run_f1(x):
        shards = _shard_rows(x)
        halos = []
        for c in range(NCORES):
            hh = np.zeros((8, C), np.float32)
            for b in range(B):
                tl = c * TPC - 1
                tr = (c + 1) * TPC
                if tl >= 0:
                    hh[b * 2 + 0] = x[b, tl]
                if tr < T:
                    hh[b * 2 + 1] = x[b, tr]
            halos.append(hh)
        maps = [{"xs": shards[c], "t1r": t1r, "t2": t2, "mw": mw,
                 "convw": convw, "convb": convb, "halox": halos[c],
                 "ident": ident} for c in range(NCORES)]
        res = _run(_prog("f1", build_f1), maps)
        return _unshard_rows([r["xo"] for r in res])

    shards = _shard_rows(x)
    wukc = np.ascontiguousarray(
        inp["L0_attn_a_w_kv_up"][:, :1024].T.reshape(FCH, 128, 128)
        .transpose(1, 0, 2))
    maps = [{"xs": s, "wd": inp["L0_attn_a_w_kv_down"], "wukT": wukc,
             "wuv": np.ascontiguousarray(inp["L0_attn_a_w_kv_up"][:, 1024:]),
             "wq": inp["L0_attn_a_w_q"], "ident": ident} for s in shards]
    res0 = _run(_prog("a0", build_a0), maps)
    latg = np.concatenate([r["lat"].reshape(128, B, TPC) for r in res0],
                          axis=2).reshape(128, B * T)
    vg = np.concatenate([r["vv"].reshape(B, TPC, 16 * 65) for r in res0],
                        axis=1)
    latmax8 = np.concatenate([r["latmax"] for r in res0], axis=1)
    maps = [{"xs": shards[c], "qat": res0[c]["qat"], "latg": latg, "vg": vg,
             "qamax": res0[c]["qamax"], "latmax8": latmax8,
             "wo": inp["L0_attn_a_w_out"], "masks": masks[c]}
            for c in range(NCORES)]
    res = _run(_prog("a1", build_a1), maps)
    x = _unshard_rows([r["xo"] for r in res])

    x = run_f0(x)

    shards = _shard_rows(x)
    bqkv = inp["L0_attn_b_b_qkv"]
    maps = [{"xs": s, "wqkv": inp["L0_attn_b_w_qkv"],
             "bqc": _colsplit(bqkv[:C]), "bkc": _colsplit(bqkv[C:2 * C]),
             "bvr": bqkv[2 * C:].reshape(1, C), "eh": eh, "ident": ident}
            for s in shards]
    res0 = _run(_prog("b0", build_b0), maps)
    kaugg = np.ascontiguousarray(np.concatenate(
        [r["kaug"].reshape(16, 68, B, TPC) for r in res0], axis=3))
    vg = np.concatenate([r["vv"].reshape(B, TPC, 16 * 65) for r in res0],
                        axis=1)
    maps = [{"xs": shards[c], "qaug": res0[c]["qaug"], "kaugg": kaugg,
             "vg": vg, "wo": inp["L0_attn_b_w_out"],
             "bor": inp["L0_attn_b_b_out"].reshape(1, C),
             "masks": masks[c], "ident": ident} for c in range(NCORES)]
    res = _run(_prog("b1", build_b1), maps)
    x = _unshard_rows([r["xo"] for r in res])

    x = run_f0(x)

    shards = _shard_rows(x)
    maps = [{"xs": s,
             "wqT": np.ascontiguousarray(inp["L1_attn_a_Wq_w"].T),
             "wkT": np.ascontiguousarray(inp["L1_attn_a_Wk_w"].T),
             "wvT": np.ascontiguousarray(inp["L1_attn_a_Wv_w"].T),
             "woT": np.ascontiguousarray(inp["L1_attn_a_Wo_w"].T),
             "wwT": np.ascontiguousarray(inp["L1_attn_a_Ww_w"].T),
             "wrT": np.ascontiguousarray(inp["L1_attn_a_Wr_w"].T),
             "bqc": _colsplit(inp["L1_attn_a_Wq_b"]),
             "bkc": _colsplit(inp["L1_attn_a_Wk_b"]),
             "bvc": _colsplit(inp["L1_attn_a_Wv_b"]),
             "bor": inp["L1_attn_a_Wo_b"].reshape(1, C),
             "bwc": inp["L1_attn_a_Ww_b"].reshape(16, 1),
             "brc": inp["L1_attn_a_Wr_b"].reshape(16, 1),
             "eh": eh, "sel": sel, "ident": ident} for s in shards]
    res = _run(_prog("g", build_g), maps)
    x = _unshard_rows([r["xo"] for r in res])

    x = run_f1(x)

    shards = _shard_rows(x)
    maps = [{"xs": s, "wq": inp["L1_attn_b_Wq"], "wk": inp["L1_attn_b_Wk"],
             "wv": inp["L1_attn_b_Wv"],
             "bqc": _colsplit(inp["L1_attn_b_bq"]),
             "bkc": _colsplit(inp["L1_attn_b_bk"]),
             "bvr": inp["L1_attn_b_bv"].reshape(1, C),
             "eh": eh, "ident": ident} for s in shards]
    res0 = _run(_prog("c0", build_c0), maps)
    ktg = np.ascontiguousarray(np.concatenate(
        [r["kt"].reshape(16, 64, B, TPC) for r in res0], axis=3))
    kn2g = np.ascontiguousarray(np.concatenate(
        [r["kn2"].reshape(16, B, TPC) for r in res0], axis=2)
        .reshape(16, B * T))
    vg = np.concatenate([r["vv"].reshape(B, TPC, 16 * 65) for r in res0],
                        axis=1)
    maps = [{"xs": shards[c], "qt": res0[c]["qt"], "qn2": res0[c]["qn2"],
             "ktg": ktg, "kn2g": kn2g, "vg": vg, "wo": inp["L1_attn_b_Wo"],
             "bor": inp["L1_attn_b_bo"].reshape(1, C)}
            for c in range(NCORES)]
    res = _run(_prog("c1", build_c1), maps)
    x = _unshard_rows([r["xo"] for r in res])

    x = run_f1(x)

    return x


# version 03 (final)
# speedup vs baseline: 1.1812x; 1.1812x over previous
import contextlib
import math
import numpy as np
import ml_dtypes
import concourse.bass as bass
import concourse.mybir as mybir
from concourse.bass_utils import run_bass_kernel_spmd

F32 = mybir.dt.float32
BF16 = mybir.dt.bfloat16
I32 = mybir.dt.int32
AF = mybir.ActivationFunctionType
ALU = mybir.AluOpType
AX = mybir.AxisListType
BFNP = ml_dtypes.bfloat16

NCORES = 8
B, T, C, H, D = 4, 4096, 1024, 16, 64
TPC = T // NCORES
ROWS = B * TPC
NTILE = ROWS // 128
FCH = C // 128
HID = 4096
HCH = HID // 128
EPS_LN = 1e-5
SEM_ROTATE = 24000



class ASB:

    ENGS = ("tensor", "vector", "scalar", "gpsimd", "sync")

    def __init__(self, name="prog"):
        self.nc = bass.Bass("TRN2", target_bir_lowering=False, debug=False)
        self.st = contextlib.ExitStack()
        self.name = name
        self.ops = []
        self._uid = 0

    def uid(self):
        self._uid += 1
        return self._uid

    def sb(self, name, shape, dtype):
        return self.st.enter_context(self.nc.sbuf_tensor(name, shape, dtype))

    def ps(self, name, shape, dtype=F32):
        return self.st.enter_context(self.nc.psum_tensor(name, shape, dtype))

    def dram_in(self, name, shape, dtype=F32):
        return self.nc.dram_tensor(name, list(shape), dtype, kind="ExternalInput").ap()

    def dram_out(self, name, shape, dtype=F32):
        return self.nc.dram_tensor(name, list(shape), dtype, kind="ExternalOutput").ap()

    def dram_tmp(self, name, shape, dtype=F32):
        return self.nc.dram_tensor(name, list(shape), dtype, kind="Internal").ap()

    def op(self, eng, fn, reads=(), writes=(), dma=False, tag=""):
        self.ops.append(dict(eng=eng, fn=fn, reads=tuple(reads),
                             writes=tuple(writes), dma=dma, tag=tag))
        return len(self.ops) - 1

    def dma(self, queue, out, in_, reads=(), writes=(), tag="", **kw):
        return self.op(queue, lambda e: e.dma_start(out=out, in_=in_, **kw),
                       reads=reads, writes=writes, dma=True, tag=tag)

    def build(self):
        n = len(self.ops)
        last_w = {}
        readers = {}
        deps = [set() for _ in range(n)]
        for i, o in enumerate(self.ops):
            for k in o["reads"]:
                if k in last_w:
                    deps[i].add(last_w[k])
            for k in o["writes"]:
                if k in last_w:
                    deps[i].add(last_w[k])
                for r in readers.get(k, ()):
                    deps[i].add(r)
            for k in o["writes"]:
                last_w[k] = i
                readers[k] = []
            for k in o["reads"]:
                readers.setdefault(k, []).append(i)
        need_inc = set()
        kept = [dict() for _ in range(n)]
        for i in range(n):
            for d in deps[i]:
                if self.ops[d]["dma"]:
                    kept[i][("dma", d)] = d
                else:
                    pe = self.ops[d]["eng"]
                    if kept[i].get(pe, -1) < d:
                        kept[i][pe] = d
        for i in range(n):
            for d in kept[i].values():
                need_inc.add(d)
        for i, o in enumerate(self.ops):
            if o["dma"]:
                need_inc.add(i)
        K = 12
        sems = {}
        sem_pos = {}
        inc_of = {}
        self_waits = {}
        dma_cnt = {}
        for i in range(n):
            if i in need_inc:
                o = self.ops[i]
                e = o["eng"]
                if o["dma"]:
                    cnt = dma_cnt.get(e, 0)
                    dma_cnt[e] = cnt + 1
                    slot = cnt % K
                    name = e + "q"
                    prev = sem_pos.get((name, slot), 0)
                    if prev:
                        self_waits[i] = (name, slot, prev)
                    val = prev + 16
                    sem_pos[(name, slot)] = val
                    inc_of[i] = (name, slot, val)
                else:
                    slot, val = sem_pos.get(e, (0, 0))
                    if val + 1 > SEM_ROTATE:
                        slot, val = slot + 1, 0
                    val += 1
                    sem_pos[e] = (slot, val)
                    inc_of[i] = (e, slot, val)
        waited = {}
        waits = [[] for _ in range(n)]
        for i in range(n):
            we = self.ops[i]["eng"]
            if i in self_waits:
                name, slot, val = self_waits[i]
                key = (we, (name, slot))
                if waited.get(key, 0) < val:
                    waits[i].append((name, slot, val))
                    waited[key] = val
            for d in sorted(kept[i].values()):
                if d not in inc_of:
                    pe = self.ops[d]["eng"]
                    cover = None
                    for j in range(d, n):
                        if j in inc_of and self.ops[j]["eng"] == pe:
                            cover = j
                            break
                    if cover is None:
                        raise RuntimeError(f"op {d} has no coverable inc")
                    raise RuntimeError(
                        f"internal: op {d} ({self.ops[d]['tag']}) should have "
                        f"been marked need_inc")
                pe, slot, val = inc_of[d]
                key = (we, (pe, slot))
                if waited.get(key, 0) >= val:
                    continue
                waits[i].append((pe, slot, val))
                waited[key] = val
        block = self.st.enter_context(self.nc.Block())
        i = 0
        nc = self.nc
        while i < n:
            e = self.ops[i]["eng"]
            j = i
            while j < n and self.ops[j]["eng"] == e:
                j += 1
            lo, hi = i, j

            def body(eng, lo=lo, hi=hi):
                for k in range(lo, hi):
                    for (pe, slot, val) in waits[k]:
                        h = sems.get((pe, slot))
                        if h is None:
                            h = self.st.enter_context(
                                nc.semaphore(f"s_{pe}_{slot}"))
                            sems[(pe, slot)] = h
                        eng.wait_ge(h, val)
                    inst = self.ops[k]["fn"](eng)
                    if k in inc_of:
                        pe, slot, val = inc_of[k]
                        h = sems.get((pe, slot))
                        if h is None:
                            h = self.st.enter_context(
                                nc.semaphore(f"s_{pe}_{slot}"))
                            sems[(pe, slot)] = h
                        inst.then_inc(h, 16 if self.ops[k]["dma"] else 1)
            getattr(block, e)(body)
            i = j
        return self

    def finish(self, keys):
        self.op("sync", lambda e: e.nop(), reads=tuple(keys), tag="finish")

    def close(self):
        self.st.close()



def _shard_rows(x_full):
    out = []
    for c in range(NCORES):
        sl = x_full[:, c * TPC:(c + 1) * TPC, :]
        out.append(np.ascontiguousarray(sl.reshape(ROWS, C), dtype=np.float32))
    return out


def _unshard_rows(shards):
    x = np.empty((B, T, C), np.float32)
    for c, s in enumerate(shards):
        x[:, c * TPC:(c + 1) * TPC, :] = np.asarray(s).reshape(B, TPC, C)
    return x



def emit_load_x(a, xd, xsb, key="x"):
    a.dma("sync", xsb[:, :, :], xd.rearrange("(n p) f -> p n f", p=128),
          writes=[key], tag="load_x")


def emit_ln(a, xsb, scale_col, bias_col, fold=1.0, n_tiles=NTILE, width=C,
            xkey="x", skey="lnsb"):
    u = a.uid()
    nseg = width // 512
    bn6 = a.sb(f"bn6_{u}", [128, n_tiles * nseg * 6], F32)
    mv = a.sb(f"mv_{u}", [128, n_tiles * 2], F32)
    for t in range(n_tiles):
        for s in range(nseg):
            a.op("vector", lambda e, t=t, s=s: e.bn_stats(
                bn6[:, (t * nseg + s) * 6:(t * nseg + s) * 6 + 6],
                xsb[:, t, s * 512:(s + 1) * 512]),
                reads=[xkey], writes=[f"bn6_{u}/{t}"], tag="bn_stats")
        a.op("vector", lambda e, t=t: e.bn_aggr(
            mv[:, 2 * t:2 * t + 2], bn6[:, t * nseg * 6:(t + 1) * nseg * 6]),
            reads=[f"bn6_{u}/{t}"], writes=[f"mv_{u}/{t}"], tag="bn_aggr")
    mvkeys = [f"mv_{u}/{t}" for t in range(n_tiles)]
    a.op("vector", lambda e: e.tensor_scalar(
        scale_col[:, 0:n_tiles], mv[:, 1:2 * n_tiles:2], EPS_LN, None, ALU.add),
        reads=mvkeys, writes=[skey + "/tmp"], tag="ln_eps")
    a.op("scalar", lambda e: e.activation(
        scale_col[:, 0:n_tiles], scale_col[:, 0:n_tiles], AF.Ln,
        bias=np.float32(0.0), scale=np.float32(1.0)),
        reads=[skey + "/tmp"], writes=[skey + "/ln"], tag="ln_ln")
    a.op("scalar", lambda e: e.activation(
        scale_col[:, 0:n_tiles], scale_col[:, 0:n_tiles], AF.Exp,
        bias=np.float32(math.log(fold)), scale=np.float32(-0.5)),
        reads=[skey + "/ln"], writes=[skey + "/scale"], tag="ln_exp")
    a.op("vector", lambda e: e.tensor_tensor(
        bias_col[:, 0:n_tiles], mv[:, 0:2 * n_tiles:2], scale_col[:, 0:n_tiles],
        ALU.mult),
        reads=mvkeys + [skey + "/scale"], writes=[skey + "/b0"], tag="ln_b0")
    a.op("vector", lambda e: e.tensor_scalar(
        bias_col[:, 0:n_tiles], bias_col[:, 0:n_tiles], -1.0, None, ALU.mult),
        reads=[skey + "/b0"], writes=[skey + "/bias"], tag="ln_b1")
    return skey + "/scale", skey + "/bias"


def emit_normalize(a, xsb, out_sb, scale_col, bias_col, skey, t, dst_key,
                   xkey="x", width=C, tag="ln_apply"):
    a.op("scalar", lambda e: e.activation(
        out_sb[:, t, 0:width], xsb[:, t, 0:width], AF.Identity,
        bias=bias_col[:, t:t + 1], scale=scale_col[:, t:t + 1]),
        reads=[xkey, skey + "/scale", skey + "/bias"], writes=[dst_key],
        tag=tag)


def emit_transposes(a, src_bf, dst_fm, n_tiles, ident, srckeys, dstkey_fn,
                    width=C, tag="tr", banks=None, bankkeys=None):
    u = a.uid()
    fch = width // 128
    if banks is None:
        banks = [a.ps(f"tp{u}_0", [128, 512], BF16),
                 a.ps(f"tp{u}_1", [128, 512], BF16)]
    groups = [(t, c) for t in range(n_tiles) for c in range(fch)]
    for g in range(0, len(groups), 4):
        bank = banks[(g // 4) % 2]
        bkey = (bankkeys[(g // 4) % 2] if bankkeys
                else f"tpbank{u}_{(g // 4) % 2}")
        items = groups[g:g + 4]
        for i, (t, c) in enumerate(items):
            a.op("tensor", lambda e, i=i, t=t, c=c, bank=bank: e.transpose(
                bank[:, i * 128:(i + 1) * 128],
                src_bf[:, t, c * 128:(c + 1) * 128], ident[:]),
                reads=[srckeys(t), "ident"], writes=[bkey], tag=tag)
        for i, (t, c) in enumerate(items):
            a.op("vector", lambda e, i=i, t=t, c=c, bank=bank: e.tensor_copy(
                dst_fm[:, c, t * 128:(t + 1) * 128],
                bank[:, i * 128:(i + 1) * 128]),
                reads=[bkey], writes=[dstkey_fn(t, c)], tag=tag + "_ev")


def emit_round_clip(a, eng, src_ap, i32_ap, dst_ap, lo, hi, rkeys, wkey, tag,
                    ikey=None):
    ikey = ikey or (wkey + "/i")
    a.op(eng, lambda e: e.tensor_copy(i32_ap, src_ap),
         reads=rkeys, writes=[ikey], tag=tag + "_rnd")
    a.op(eng, lambda e: e.tensor_scalar(dst_ap, i32_ap, float(hi), float(lo),
                                        ALU.min, ALU.max),
         reads=[ikey], writes=[wkey], tag=tag + "_clip")



def build_w0():
    a = ASB("w0")
    w1t = a.dram_in("w1t", [C, HID])
    w2t = a.dram_in("w2t", [HID, C])
    b1 = a.dram_in("b1", [1, HID])
    b2 = a.dram_in("b2", [1, C])
    v1t = a.dram_in("v1t", [C, HID])
    v2t = a.dram_in("v2t", [HID, C])
    m1t_o = a.dram_out("m1t", [C, HID], BF16)
    m2t_o = a.dram_out("m2t", [HID, C], BF16)
    mb1_o = a.dram_out("mb1", [1, HID])
    b2q_o = a.dram_out("b2q1024", [1, C])
    t1t_o = a.dram_out("t1t", [C, HID], BF16)
    t2t_o = a.dram_out("t2t", [HID, C], BF16)
    mw_o = a.dram_out("mw", [1, 2])

    nb = 3
    xs = [a.sb(f"x{i}", [128, HID], F32) for i in range(nb)]
    xis = [a.sb(f"xi{i}", [128, HID], I32) for i in range(nb)]
    xbs = [a.sb(f"xb{i}", [128, HID], BF16) for i in range(nb)]

    def quant_stream(src, dst, nrt, scale_imm, scale_col, name, width):
        lo, hi = (-128.0, 127.0) if scale_col is None else (-1.0, 1.0)
        for i in range(nrt):
            b = i % nb
            a.dma("sync", xs[b][:, 0:width], src[i * 128:(i + 1) * 128, :],
                  writes=[f"x{b}"], tag=f"{name}_ld")
            if scale_col is None:
                a.op("vector", lambda e, b=b: e.tensor_scalar(
                    xs[b][:, 0:width], xs[b][:, 0:width], float(scale_imm),
                    None, ALU.mult),
                    reads=[f"x{b}"], writes=[f"x{b}"], tag=f"{name}_s")
            else:
                col, ckey = scale_col
                a.op("vector", lambda e, b=b: e.tensor_scalar(
                    xs[b][:, 0:width], xs[b][:, 0:width], col, None, ALU.mult),
                    reads=[f"x{b}", ckey], writes=[f"x{b}"],
                    tag=f"{name}_s")
            a.op("vector", lambda e, b=b: e.tensor_copy(
                xis[b][:, 0:width], xs[b][:, 0:width]),
                 reads=[f"x{b}"], writes=[f"xi{b}"], tag=f"{name}_r")
            a.op("vector", lambda e, b=b: e.tensor_scalar(
                xbs[b][:, 0:width], xis[b][:, 0:width], hi, lo, ALU.min, ALU.max),
                reads=[f"xi{b}"], writes=[f"xb{b}"], tag=f"{name}_c")
            a.dma("gpsimd", dst[i * 128:(i + 1) * 128, :], xbs[b][:, 0:width],
                  reads=[f"xb{b}"], writes=[f"{name}st"], tag=f"{name}_st")

    quant_stream(w1t, m1t_o, C // 128, 64.0, None, "q1", HID)
    quant_stream(w2t, m2t_o, HID // 128, 64.0, None, "q2", C)

    br = a.sb("br", [1, HID], F32)
    bri = a.sb("bri", [1, HID], I32)
    a.dma("sync", br[0:1, 0:HID], b1[0:1, :], writes=["br"], tag="b1_ld")
    a.op("vector", lambda e: e.tensor_scalar(br[0:1, 0:HID], br[0:1, 0:HID],
                                             64.0, None, ALU.mult),
         reads=["br"], writes=["br"], tag="b1_s")
    a.op("vector", lambda e: e.tensor_copy(bri[0:1, 0:HID], br[0:1, 0:HID]),
         reads=["br"], writes=["bri"], tag="b1_r")
    a.op("vector", lambda e: e.tensor_copy(br[0:1, 0:HID], bri[0:1, 0:HID]),
         reads=["bri"], writes=["br"], tag="b1_c")
    a.dma("gpsimd", mb1_o[0:1, :], br[0:1, 0:HID], reads=["br"],
          writes=["mb1st"], tag="b1_st")
    a.dma("sync", br[0:1, 0:C], b2[0:1, :], reads=["br"], writes=["br"],
          tag="b2_ld")
    a.op("vector", lambda e: e.tensor_scalar(br[0:1, 0:C], br[0:1, 0:C],
                                             64.0, None, ALU.mult),
         reads=["br"], writes=["br"], tag="b2_s")
    a.op("vector", lambda e: e.tensor_copy(bri[0:1, 0:C], br[0:1, 0:C]),
         reads=["br"], writes=["bri"], tag="b2_r")
    a.op("vector", lambda e: e.tensor_scalar(br[0:1, 0:C], bri[0:1, 0:C],
                                             16.0, None, ALU.mult),
         reads=["bri"], writes=["br"], tag="b2_c")
    a.dma("gpsimd", b2q_o[0:1, :], br[0:1, 0:C], reads=["br"],
          writes=["b2st"], tag="b2_st")

    acc = a.sb("acc", [128, 2], F32)
    red = a.sb("red", [128, 1], F32)
    ones1f = a.sb("ones1f", [128, 1], F32)
    onesr = a.sb("onesr", [1, 128], BF16)
    rps = a.ps("rps", [1, 2], F32)
    bps = a.ps("bps", [128, 2], F32)
    mwsb = a.sb("mwsb", [1, 2], F32)
    mwb = a.sb("mwb", [1, 2], BF16)
    recs = a.sb("recs", [128, 2], F32)

    def mean_abs(src, nrt, col, name, width):
        for i in range(nrt):
            b = i % nb
            a.dma("sync", xs[b][:, 0:width], src[i * 128:(i + 1) * 128, :],
                  writes=[f"x{b}"], tag=f"{name}_ld")
            a.op("vector", lambda e, b=b: e.tensor_reduce(
                red[:], xs[b][:, 0:width], AX.X, ALU.add,
                apply_absolute_value=True),
                reads=[f"x{b}"], writes=["red"], tag=f"{name}_red")
            if i == 0:
                a.op("vector", lambda e: e.tensor_copy(acc[:, col:col + 1], red[:]),
                     reads=["red"], writes=[f"acc{col}"], tag=f"{name}_a0")
            else:
                a.op("vector", lambda e: e.tensor_add(
                    acc[:, col:col + 1], acc[:, col:col + 1], red[:]),
                    reads=["red", f"acc{col}"], writes=[f"acc{col}"],
                    tag=f"{name}_acc")

    mean_abs(v1t, C // 128, 0, "ma1", HID)
    mean_abs(v2t, HID // 128, 1, "ma2", C)
    a.op("vector", lambda e: e.memset(ones1f[:], 1.0), writes=["ones1"])
    a.op("vector", lambda e: e.memset(onesr[0:1, :], 1.0), writes=["onesr"])
    a.op("tensor", lambda e: e.matmul(rps[0:1, 0:2], ones1f[:, 0:1], acc[:, 0:2],
                                      start=True, stop=True),
         reads=["ones1", "acc0", "acc1"], writes=["rps"], tag="ma_mm")
    a.op("vector", lambda e: e.tensor_scalar(
        mwsb[0:1, 0:2], rps[0:1, 0:2], 1.0 / (C * HID), 1e-5, ALU.mult, ALU.max),
        reads=["rps"], writes=["mwsb"], tag="ma_mw")
    a.dma("gpsimd", mw_o[0:1, :], mwsb[0:1, 0:2], reads=["mwsb"],
          writes=["mwst"], tag="mw_st")
    a.op("vector", lambda e: e.tensor_copy(mwb[0:1, 0:2], mwsb[0:1, 0:2]),
         reads=["mwsb"], writes=["mwb"], tag="mw_cast")
    a.op("tensor", lambda e: e.matmul(bps[:, 0:2], onesr[0:1, :], mwb[0:1, 0:2],
                                      start=True, stop=True),
         reads=["onesr", "mwb"], writes=["bps"], tag="mw_bcast")
    a.op("vector", lambda e: e.reciprocal(recs[:, 0:2], bps[:, 0:2]),
         reads=["bps"], writes=["recs"], tag="mw_rec")

    quant_stream(v1t, t1t_o, C // 128, None, (recs[:, 0:1], "recs"), "t1", HID)
    quant_stream(v2t, t2t_o, HID // 128, None, (recs[:, 1:2], "recs"), "t2", C)

    a.finish(["q1st", "q2st", "t1st", "t2st", "mb1st", "b2st", "mwst"])
    return a.build()


def run_w0(inp):
    a = build_w0()
    m = {
        "w1t": np.ascontiguousarray(np.asarray(inp["L0_ffn_w1"], np.float32).T),
        "w2t": np.ascontiguousarray(np.asarray(inp["L0_ffn_w2"], np.float32).T),
        "b1": np.asarray(inp["L0_ffn_b1"], np.float32).reshape(1, -1),
        "b2": np.asarray(inp["L0_ffn_b2"], np.float32).reshape(1, -1),
        "v1t": np.ascontiguousarray(np.asarray(inp["L1_ffn_w1"], np.float32).T),
        "v2t": np.ascontiguousarray(np.asarray(inp["L1_ffn_w2"], np.float32).T),
    }
    res = run_bass_kernel_spmd(a.nc, [m], core_ids=[0])
    a.close()
    return res.results[0]



def build_f0(debug=False):
    a = ASB("f0")
    xd = a.dram_in("xs", [ROWS, C])
    m1d = a.dram_in("m1", [HCH, 128, C], BF16)
    m2d = a.dram_in("m2", [HID, C], BF16)
    mb1d = a.dram_in("mb1c", [128, HCH])
    b2qd = a.dram_in("b2q", [1, C])
    identd = a.dram_in("ident", [128, 128], BF16)
    xo = a.dram_out("xo", [ROWS, C])

    xsb = a.sb("xsb", [128, NTILE, C], F32)
    ident = a.sb("identsb", [128, 128], BF16)
    lns = a.sb("lns", [128, NTILE], F32)
    lnb = a.sb("lnb", [128, NTILE], F32)
    y16 = [a.sb(f"y16_{i}", [128, C], F32) for i in range(2)]
    yi = [a.sb(f"yi_{i}", [128, C], I32) for i in range(2)]
    nsb = a.sb("nsb", [128, NTILE, C], BF16)
    nT = a.sb("nT", [128, FCH, ROWS], BF16)
    m1b = [a.sb(f"m1b_{i}", [128, C], BF16) for i in range(2)]
    h16 = [a.sb(f"h16_{i}", [128, 512], F32) for i in range(2)]
    hi = [a.sb(f"hi_{i}", [128, 512], I32) for i in range(2)]
    n2 = a.sb("n2", [128, HCH, 512], BF16)
    m2b = [a.sb(f"m2b_{i}", [128, 512], BF16) for i in range(2)]
    onesr = a.sb("onesr", [1, 128], BF16)
    b2qbf = a.sb("b2qbf", [1, C], BF16)
    b2qf = a.sb("b2qf", [1, C], F32)
    mb1q = a.sb("mb1q", [128, HCH], F32)
    hps = [a.ps(f"hps_{i}", [128, 512], F32) for i in range(2)]
    ops = [a.ps(f"ops_{i}", [128, 512], F32) for i in range(4)]

    emit_load_x(a, xd, xsb, key="x")
    a.dma("sync", ident[:], identd[:, :], writes=["ident"], tag="ld_ident")
    a.dma("sync", mb1q[:], mb1d[:, :], writes=["mb1"], tag="ld_mb1")
    a.dma("sync", b2qf[0:1, :], b2qd[0:1, :], writes=["b2qf"], tag="ld_b2q")
    a.op("vector", lambda e: e.tensor_scalar(mb1q[:], mb1q[:], 0.25, None,
                                             ALU.mult),
         reads=["mb1"], writes=["mb1"], tag="mb1_q4")
    a.op("vector", lambda e: e.tensor_copy(b2qbf[0:1, :], b2qf[0:1, :]),
         reads=["b2qf"], writes=["b2qbf"], tag="b2q_cast")
    a.op("vector", lambda e: e.memset(onesr[0:1, :], 1.0), writes=["onesr"])

    emit_ln(a, xsb, lns, lnb, fold=16.0, xkey="x", skey="ln")

    for t in range(NTILE):
        b = t % 2
        a.op("scalar", lambda e, t=t, b=b: e.activation(
            y16[b][:, :], xsb[:, t, :], AF.Identity,
            bias=lnb[:, t:t + 1], scale=lns[:, t:t + 1]),
            reads=["x", "ln/scale", "ln/bias"], writes=[f"y16/{b}"],
            tag="ln_apply")
        a.op("vector", lambda e, b=b: e.tensor_copy(yi[b][:], y16[b][:]),
             reads=[f"y16/{b}"], writes=[f"yi/{b}"], tag="rnd")
        a.op("vector", lambda e, t=t, b=b: e.tensor_scalar(
            nsb[:, t, :], yi[b][:], 127.0, -128.0, ALU.min, ALU.max),
            reads=[f"yi/{b}"], writes=[f"n/{t}"], tag="clip")

    emit_transposes(a, nsb, nT, NTILE, ident, lambda t: f"n/{t}",
                    lambda t, c: f"nT/{t}/{c}")

    for tg in range(4):
        tg_tiles = [tg * 4 + i for i in range(4)]
        for hc in range(HCH):
            mb = (tg * HCH + hc) % 2
            a.dma("sync", m1b[mb][:, :], m1d[hc, :, :],
                  writes=[f"m1b/{mb}"], tag="m1_ld")
            pb_ = hc % 2
            for dc in range(FCH):
                nt_keys = [f"nT/{t}/{dc}" for t in tg_tiles]
                a.op("tensor", lambda e, hc=hc, dc=dc, tg=tg, mb=mb, pb_=pb_:
                     e.matmul(hps[pb_][:, :], m1b[mb][:, dc * 128:(dc + 1) * 128],
                              nT[:, dc, tg * 512:(tg + 1) * 512],
                              start=(dc == 0), stop=(dc == FCH - 1)),
                     reads=[f"m1b/{mb}"] + nt_keys, writes=[f"hps/{pb_}"],
                     tag="mm1")
            a.op("scalar", lambda e, hc=hc, pb_=pb_: e.activation(
                h16[pb_][:, :], hps[pb_][:, :], AF.Identity,
                bias=mb1q[:, hc:hc + 1], scale=np.float32(1.0 / 64)),
                reads=[f"hps/{pb_}", "mb1"], writes=[f"h16/{pb_}"], tag="h_evac")
            a.op("vector", lambda e, pb_=pb_: e.tensor_copy(hi[pb_][:], h16[pb_][:]),
                 reads=[f"h16/{pb_}"], writes=[f"hi/{pb_}"], tag="h_rnd")
            a.op("vector", lambda e, hc=hc, pb_=pb_: e.tensor_scalar(
                n2[:, hc, :], hi[pb_][:], 127.0, 0.0, ALU.min, ALU.max),
                reads=[f"hi/{pb_}"], writes=[f"n2/{hc}"], tag="h_clip")
        for dh in range(2):
            for hc in range(HCH):
                mb = (tg * 2 + dh + hc) % 2
                a.dma("scalar", m2b[mb][:, :],
                      m2d[hc * 128:(hc + 1) * 128, dh * 512:(dh + 1) * 512],
                      writes=[f"m2b/{mb}"], tag="m2_ld")
                for i, t in enumerate(tg_tiles):
                    a.op("tensor", lambda e, hc=hc, i=i, t=t, mb=mb: e.matmul(
                        ops[i][:, :], n2[:, hc, t % 4 * 128:(t % 4 + 1) * 128],
                        m2b[mb][:, :], start=(hc == 0), stop=False),
                        reads=[f"m2b/{mb}", f"n2/{hc}"], writes=[f"ops/{i}"],
                        tag="mm2")
            for i, t in enumerate(tg_tiles):
                a.op("tensor", lambda e, i=i, dh=dh: e.matmul(
                    ops[i][:, :], onesr[0:1, :], b2qbf[0:1, dh * 512:(dh + 1) * 512],
                    start=False, stop=True),
                    reads=["onesr", "b2qbf"], writes=[f"ops/{i}"], tag="mm2_bias")
            for i, t in enumerate(tg_tiles):
                a.op("vector", lambda e, i=i, t=t, dh=dh: e.scalar_tensor_tensor(
                    xsb[:, t, dh * 512:(dh + 1) * 512], ops[i][:, :],
                    1.0 / 1024, xsb[:, t, dh * 512:(dh + 1) * 512],
                    ALU.mult, ALU.add),
                    reads=[f"ops/{i}", "x"], writes=["x", f"xn/{t}/{dh}"],
                    tag="resid")

    a.dma("gpsimd", xo.rearrange("(n p) f -> p n f", p=128), xsb[:, :, :],
          reads=["x"] + [f"xn/{t}/{dh}" for t in range(NTILE) for dh in range(2)],
          writes=["xo_st"], tag="store_x")
    fin = ["xo_st"]
    if debug:
        nd = a.dram_out("dbg_n", [ROWS, C], BF16)
        n2d = a.dram_out("dbg_n2", [HID, 512], BF16)
        lnsd = a.dram_out("dbg_lns", [128, NTILE])
        a.dma("gpsimd", nd.rearrange("(n p) f -> p n f", p=128), nsb[:, :, :],
              reads=[f"n/{t}" for t in range(NTILE)], writes=["dbg1"], tag="dbg")
        a.dma("gpsimd", n2d.rearrange("(hc p) f -> p hc f", p=128), n2[:, :, :],
              reads=[f"n2/{hc}" for hc in range(HCH)], writes=["dbg2"], tag="dbg")
        a.dma("gpsimd", lnsd[:, :], lns[:, :], reads=["ln/scale"],
              writes=["dbg3"], tag="dbg")
        fin += ["dbg1", "dbg2", "dbg3"]
    a.finish(fin)
    return a.build()


def _f0_host_inputs(m1t, m2t, mb1, b2q1024):
    m1 = np.ascontiguousarray(
        np.asarray(m1t).reshape(FCH, 128, HCH, 128).transpose(2, 1, 0, 3)
    ).reshape(HCH, 128, C)
    m2 = np.ascontiguousarray(np.asarray(m2t))
    mb1c = np.ascontiguousarray(np.asarray(mb1).reshape(HCH, 128).T)
    return m1, m2, mb1c, np.asarray(b2q1024).reshape(1, C)


def run_f0(x_full, w0out, prog=None, debug=False):
    m1, m2, mb1c, b2q = _f0_host_inputs(w0out["m1t"], w0out["m2t"],
                                        w0out["mb1"], w0out["b2q1024"])
    a = prog or build_f0(debug=debug)
    shards = _shard_rows(x_full)
    ident = np.eye(128, dtype=BFNP)
    maps = [{"xs": s, "m1": m1, "m2": m2, "mb1c": mb1c, "b2q": b2q,
             "ident": ident} for s in shards]
    res = run_bass_kernel_spmd(a.nc, maps, core_ids=list(range(NCORES)))
    if prog is None:
        a.close()
    if debug:
        return _unshard_rows([r["xo"] for r in res.results]), res.results
    return _unshard_rows([r["xo"] for r in res.results])


def emit_weight_cast(a, wd, wsb, nchunks, wkey, width=C, tag="wld"):
    for dc in range(nchunks):
        a.dma("gpsimd", wsb[:, dc, :], wd[dc * 128:(dc + 1) * 128, :],
              writes=[f"{wkey}/{dc}"], tag=tag)


def emit_ln_apply_bf16(a, xsb, xln, t, tag="lnap"):
    a.op("scalar", lambda e, t=t: e.activation(
        xln[:, t, :], xsb[:, t, :], AF.Identity,
        bias=a._lnb[:, t:t + 1], scale=a._lns[:, t:t + 1]),
        reads=["x", "ln/scale", "ln/bias"], writes=[f"xln/{t}"], tag=tag)


def emit_std_prologue(a, xd, identd, fold=1.0):
    xsb = a.sb("xsb", [128, NTILE, C], F32)
    ident = a.sb("identsb", [128, 128], BF16)
    lns = a.sb("lns", [128, NTILE], F32)
    lnb = a.sb("lnb", [128, NTILE], F32)
    xln = a.sb("xln", [128, NTILE, C], BF16)
    xT = a.sb("xT", [128, FCH, ROWS], BF16)
    a._lns, a._lnb = lns, lnb
    emit_load_x(a, xd, xsb, key="x")
    a.dma("sync", ident[:], identd[:, :], writes=["ident"], tag="ld_ident")
    emit_ln(a, xsb, lns, lnb, fold=fold, xkey="x", skey="ln")
    for t in range(NTILE):
        emit_ln_apply_bf16(a, xsb, xln, t)
    emit_transposes(a, xln, xT, NTILE, ident, lambda t: f"xln/{t}",
                    lambda t, c: f"xT/{t}/{c}")
    return xsb, xT


def emit_stream_prologue(a, xd, identd, fold=1.0, nbuf=2):
    ident = a.sb("identsb", [128, 128], BF16)
    xt_ = [a.sb(f"xt_{i}", [128, C], F32) for i in range(nbuf)]
    bn6 = a.sb("bn6p", [128, 12], F32)
    mv = a.sb("mvp", [128, 2], F32)
    lns = a.sb("lns", [128, NTILE], F32)
    lnb = a.sb("lnb", [128, NTILE], F32)
    xln = a.sb("xln", [128, NTILE, C], BF16)
    xT = a.sb("xT", [128, FCH, ROWS], BF16)
    a._lns, a._lnb = lns, lnb
    a._xln_tensor = xln
    a.dma("sync", ident[:], identd[:, :], writes=["ident"], tag="ld_ident")
    for t in range(NTILE):
        b = t % nbuf
        a.dma("sync", xt_[b][:, :], xd[t * 128:(t + 1) * 128, :],
              reads=[f"xt/{b}"], writes=[f"xt/{b}"], tag="x_ld")
        for s in range(2):
            a.op("vector", lambda e, s=s, b=b: e.bn_stats(
                bn6[:, s * 6:(s + 1) * 6], xt_[b][:, s * 512:(s + 1) * 512]),
                reads=[f"xt/{b}"], writes=[f"bn6/{s}"], tag="bn_stats")
        a.op("vector", lambda e: e.bn_aggr(mv[:, 0:2], bn6[:, 0:12]),
             reads=["bn6/0", "bn6/1"], writes=["mv"], tag="bn_aggr")
        a.op("vector", lambda e, t=t: e.tensor_scalar(
            lns[:, t:t + 1], mv[:, 1:2], EPS_LN, None, ALU.add),
            reads=["mv"], writes=[f"lntmp/{t}"], tag="ln_eps")
        a.op("scalar", lambda e, t=t: e.activation(
            lns[:, t:t + 1], lns[:, t:t + 1], AF.Ln,
            bias=np.float32(0.0), scale=np.float32(1.0)),
            reads=[f"lntmp/{t}"], writes=[f"lnl/{t}"], tag="ln_ln")
        a.op("scalar", lambda e, t=t: e.activation(
            lns[:, t:t + 1], lns[:, t:t + 1], AF.Exp,
            bias=np.float32(math.log(fold)), scale=np.float32(-0.5)),
            reads=[f"lnl/{t}"], writes=[f"lnsc/{t}"], tag="ln_exp")
        a.op("vector", lambda e, t=t: e.tensor_tensor(
            lnb[:, t:t + 1], mv[:, 0:1], lns[:, t:t + 1], ALU.mult),
            reads=["mv", f"lnsc/{t}"], writes=[f"lnb0/{t}"], tag="ln_b0")
        a.op("vector", lambda e, t=t: e.tensor_scalar(
            lnb[:, t:t + 1], lnb[:, t:t + 1], -1.0, None, ALU.mult),
            reads=[f"lnb0/{t}"], writes=[f"lnbv/{t}"], tag="ln_b1")
        a.op("scalar", lambda e, t=t, b=b: e.activation(
            xln[:, t, :], xt_[b][:, :], AF.Identity,
            bias=lnb[:, t:t + 1], scale=lns[:, t:t + 1]),
            reads=[f"xt/{b}", f"lnsc/{t}", f"lnbv/{t}"], writes=[f"xln/{t}"],
            tag="ln_apply")
    emit_transposes(a, xln, xT, NTILE, ident, lambda t: f"xln/{t}",
                    lambda t, c: f"xT/{t}/{c}")
    return xT


def xt_keys(tg):
    return [f"xT/{t}/{c}" for t in range(tg * 4, tg * 4 + 4) for c in range(FCH)]



def build_c0():
    a = ASB("c0")
    xd = a.dram_in("xs", [ROWS, C])
    wqd = a.dram_in("wq", [C, C])
    wkd = a.dram_in("wk", [C, C])
    wvd = a.dram_in("wv", [C, C])
    bqd = a.dram_in("bqc", [128, FCH])
    bkd = a.dram_in("bkc", [128, FCH])
    bvd = a.dram_in("bvr", [1, C])
    ehd = a.dram_in("eh", [128, FCH, 16])
    identd = a.dram_in("ident", [128, 128], BF16)
    qt_o = a.dram_out("qt", [C, ROWS], BF16)
    kt_o = a.dram_out("kt", [C, ROWS], BF16)
    v_o = a.dram_out("vv", [ROWS, 16 * 65], BF16)
    qn2_o = a.dram_out("qn2", [16, ROWS])
    kn2_o = a.dram_out("kn2", [16, ROWS])

    xT = emit_stream_prologue(a, xd, identd)
    wsb = a.sb("wsb", [128, FCH, C], BF16)
    bcol = a.sb("bcol", [128, FCH], F32)
    ehf = a.sb("ehf", [128, FCH, 16], F32)
    bvr = a.sb("bvr_sb", [1, C], F32)
    bvb = a.sb("bvb", [1, C], BF16)
    onesr = a.sb("onesr", [1, 128], BF16)
    qf = [a.sb(f"qf_{i}", [128, 512], F32) for i in range(2)]
    sq = [a.sb(f"sq_{i}", [128, 512], F32) for i in range(2)]
    n2sb = a.sb("n2sb", [16, ROWS], F32)
    outT = a.sb("outT", [128, FCH, ROWS], BF16)
    vsb = a.sb("vsb", [128, NTILE, 16 * 65], BF16)
    pps = [a.ps(f"pps_{i}", [128, 512], F32) for i in range(2)]
    nps = a.ps("nps", [16, 512], F32)

    a.dma("sync", ehf[:, :, :], ehd[:, :, :], writes=["ehf"], tag="ld_eh")
    a.dma("sync", bvr[0:1, :], bvd[0:1, :], writes=["bvr"], tag="ld_bv")
    a.op("vector", lambda e: e.tensor_copy(bvb[0:1, :], bvr[0:1, :]),
         reads=["bvr"], writes=["bvb"], tag="bv_cast")
    a.op("vector", lambda e: e.memset(onesr[0:1, :], 1.0), writes=["onesr"])

    def proj_fm(wd_, bd_, out_dram, n2_dram, pname):
        emit_weight_cast(a, wd_, wsb, FCH, "wsb")
        a.dma("sync", bcol[:, :], bd_[:, :], reads=["bcol"], writes=["bcol"],
              tag=f"{pname}_bld")
        for tg in range(4):
            for oc in range(FCH):
                pb_ = oc % 2
                for dc in range(FCH):
                    a.op("tensor", lambda e, oc=oc, dc=dc, tg=tg, pb_=pb_:
                         e.matmul(pps[pb_][:, :],
                                  wsb[:, dc, oc * 128:(oc + 1) * 128],
                                  xT[:, dc, tg * 512:(tg + 1) * 512],
                                  start=(dc == 0), stop=(dc == FCH - 1)),
                         reads=[f"wsb/{dc}"] + xt_keys(tg),
                         writes=[f"pps/{pb_}"], tag=f"{pname}_mm")
                a.op("scalar", lambda e, oc=oc, pb_=pb_: e.activation(
                    qf[pb_][:, :], pps[pb_][:, :], AF.Identity,
                    bias=bcol[:, oc:oc + 1], scale=np.float32(1.0)),
                    reads=[f"pps/{pb_}", "bcol"], writes=[f"qf/{pb_}"],
                    tag=f"{pname}_ev")
                a.op("vector", lambda e, oc=oc, tg=tg, pb_=pb_: e.tensor_copy(
                    outT[:, oc, tg * 512:(tg + 1) * 512], qf[pb_][:, :]),
                    reads=[f"qf/{pb_}"], writes=[f"oT/{oc}/{tg}"],
                    tag=f"{pname}_cast")
                a.op("scalar", lambda e, pb_=pb_: e.activation(
                    sq[pb_][:, :], qf[pb_][:, :], AF.Square,
                    bias=np.float32(0.0), scale=np.float32(1.0)),
                    reads=[f"qf/{pb_}"], writes=[f"sqb/{pb_}"],
                    tag=f"{pname}_sq")
                a.op("tensor", lambda e, oc=oc, pb_=pb_: e.matmul(
                    nps[:, :], ehf[:, oc, :], sq[pb_][:, :],
                    start=(oc == 0), stop=(oc == FCH - 1)),
                    reads=["ehf", f"sqb/{pb_}"], writes=["nps"],
                    tag=f"{pname}_emm")
            a.op("vector", lambda e, tg=tg: e.tensor_copy(
                n2sb[:, tg * 512:(tg + 1) * 512], nps[:, :]),
                reads=["nps"], writes=[f"n2sb/{tg}"], tag=f"{pname}_n2ev")
        a.dma("gpsimd", out_dram.rearrange("(c p) r -> p c r", p=128),
              outT[:, :, :],
              reads=[f"oT/{oc}/{tg}" for oc in range(FCH) for tg in range(4)],
              writes=[f"{pname}_st", "outTbuf"], tag=f"{pname}_st")
        a.dma("gpsimd", n2_dram[:, :], n2sb[:, :],
              reads=[f"n2sb/{tg}" for tg in range(4)],
              writes=[f"{pname}_st2", "n2buf"], tag=f"{pname}_st2")

    proj_fm(wqd, bqd, qt_o, qn2_o, "q")
    proj_fm(wkd, bkd, kt_o, kn2_o, "k")

    emit_weight_cast(a, wvd, wsb, FCH, "wsb")
    a.op("vector", lambda e: e.memset(vsb[:, :, :].rearrange(
        "p t (h o) -> p t h o", o=65)[:, :, :, 64:65], 1.0),
        writes=["vones"], tag="v_ones")
    for t in range(NTILE):
        for vh in range(2):
            pb_ = (t * 2 + vh) % 2
            for dc in range(FCH):
                a.op("tensor", lambda e, t=t, vh=vh, dc=dc, pb_=pb_: e.matmul(
                    pps[pb_][:, :], xT[:, dc, t * 128:(t + 1) * 128],
                    wsb[:, dc, vh * 512:(vh + 1) * 512],
                    start=(dc == 0), stop=False),
                    reads=[f"wsb/{dc}", f"xT/{t}/{dc}"], writes=[f"pps/{pb_}"],
                    tag="v_mm")
            a.op("tensor", lambda e, vh=vh, pb_=pb_: e.matmul(
                pps[pb_][:, :], onesr[0:1, :], bvb[0:1, vh * 512:(vh + 1) * 512],
                start=False, stop=True),
                reads=["onesr", "bvb"], writes=[f"pps/{pb_}"], tag="v_bias")
            a.op("vector", lambda e, t=t, vh=vh, pb_=pb_: e.tensor_copy(
                vsb[:, t, :].rearrange("p (h o) -> p h o", o=65)
                [:, vh * 8:(vh + 1) * 8, 0:64],
                pps[pb_][:, :].rearrange("p (h d) -> p h d", d=64)),
                reads=[f"pps/{pb_}", "vones"], writes=[f"vev/{t}/{vh}"],
                tag="v_ev")
    a.dma("gpsimd", v_o.rearrange("(t p) v -> p t v", p=128), vsb[:, :, :],
          reads=[f"vev/{t}/{vh}" for t in range(NTILE) for vh in range(2)],
          writes=["v_st"], tag="v_st")

    a.finish(["q_st", "q_st2", "k_st", "k_st2", "v_st"])
    return a.build()



def build_c1(scale=float(D) ** -0.5, causal=False, name="c1"):
    a = ASB(name)
    xd = a.dram_in("xs", [ROWS, C])
    qtd = a.dram_in("qt", [C, ROWS], BF16)
    qn2d = a.dram_in("qn2", [16, ROWS])
    ktd = a.dram_in("ktg", [16, 64, B, T], BF16)
    kn2d = a.dram_in("kn2g", [16, B * T])
    vgd = a.dram_in("vg", [B, T, 16 * 65], BF16)
    wod = a.dram_in("wo", [C, C])
    bod = a.dram_in("bor", [1, C])
    maskd = a.dram_in("masks", [T // 128, 128, 512], BF16) if causal else None
    xo = a.dram_out("xo", [ROWS, C])

    qt = a.sb("qtsb", [128, FCH, ROWS], BF16)
    qn2 = a.sb("qn2sb", [16, ROWS], F32)
    kn2 = a.sb("kn2sb", [16, T], F32)
    vsb = a.sb("vsb", [128, T // 128, 16 * 65], BF16)
    kbuf = a.sb("kbuf", [128, T], BF16)
    wosb = a.sb("wosb", [128, FCH, C], BF16)
    borow = a.sb("borow", [1, C], F32)
    bob = a.sb("bob", [1, C], BF16)
    onesr = a.sb("onesr", [1, 128], BF16)
    onesrf = a.sb("onesrf", [1, 128], F32)
    ones64 = a.sb("ones64", [1, 64], F32)
    yT = a.sb("yT", [128, FCH, ROWS], BF16)
    pbuf = [a.sb(f"pbuf_{i}", [128, 512], BF16) for i in range(4)]
    masks = a.sb("masksb", [128, T // 128, 512], BF16) if causal else None
    mm_sc = a.sb("mm_sc", [16, 4], F32)
    kmx = a.sb("kmx", [16, 4], F32)
    crow = a.sb("crow", [1, 64], F32)
    cb = a.sb("cbsb", [128, 64], F32)
    rrow = a.sb("rrow", [1, 512], F32)
    rbs = a.sb("rbs", [64, 512], F32)
    xbuf = [a.sb(f"xbuf_{i}", [128, C], F32) for i in range(2)]
    sps = [a.ps(f"sps_{i}", [128, 512], F32) for i in range(4)]
    avps = a.ps("avps", [128, 512], F32)
    rbps = a.ps("rbps", [128, 512], F32)
    cbounce = a.dram_tmp("cbounce", [16, 4])

    a.dma("sync", qt[:, :, :], qtd.rearrange("(c p) r -> p c r", p=128),
          writes=["qt"], tag="ld_qt")
    a.dma("sync", qn2[:, :], qn2d[:, :], writes=["qn2"], tag="ld_qn2")

    if causal:
        a.dma("sync", masks[:, :, :], maskd.rearrange("j p q -> p j q"),
              writes=["masks"], tag="ld_masks")
    a.op("vector", lambda e: e.memset(onesr[0:1, :], 1.0), writes=["onesr"])
    a.op("vector", lambda e: e.memset(onesrf[0:1, :], 1.0), writes=["onesrf"])
    a.op("vector", lambda e: e.memset(ones64[0:1, :], 1.0), writes=["ones64"])
    emit_weight_cast(a, wod, wosb, FCH, "wo")
    a.dma("sync", borow[0:1, :], bod[0:1, :], writes=["bo"], tag="ld_bo")
    a.op("vector", lambda e: e.tensor_copy(bob[0:1, :], borow[0:1, :]),
         reads=["bo"], writes=["bob"], tag="bo_cast")

    for b in range(B):
        a.op("vector", lambda e, b=b: e.reduce_max(
            mm_sc[:, b:b + 1], qn2[:, b * TPC:(b + 1) * TPC], AX.X),
            reads=["qn2"], writes=[f"qm/{b}"], tag="qmax")
        a.dma("sync", kn2[:, :], kn2d[:, b * T:(b + 1) * T],
              reads=["kn2"], writes=["kn2"], tag="ld_kn2")
        a.op("vector", lambda e, b=b: e.reduce_max(
            kmx[:, b:b + 1], kn2[:, :], AX.X),
            reads=["kn2"], writes=[f"km/{b}"], tag="kmax")
    a.op("vector", lambda e: e.tensor_tensor(mm_sc[:, :], mm_sc[:, :],
                                             kmx[:, :], ALU.mult),
         reads=[f"qm/{b}" for b in range(B)] + [f"km/{b}" for b in range(B)],
         writes=["prod"], tag="cprod")
    a.op("scalar", lambda e: e.activation(mm_sc[:, :], mm_sc[:, :], AF.Sqrt,
                                          bias=np.float32(0.0),
                                          scale=np.float32(1.0)),
         reads=["prod"], writes=["csqrt"], tag="csqrt")
    a.op("vector", lambda e: e.tensor_scalar(
        mm_sc[:, :], mm_sc[:, :], float(-scale * 1.02),
        -0.01, ALU.mult, ALU.add),
        reads=["csqrt"], writes=["cneg"], tag="cneg")
    a.dma("gpsimd", cbounce[:, :], mm_sc[:, :], reads=["cneg"],
          writes=["cb1"], tag="c_bounce1")
    a.dma("sync", crow[0:1, :], cbounce.rearrange("h b -> () (h b)"),
          reads=["cb1"], writes=["cb2"], tag="c_bounce2")
    a.op("tensor", lambda e: e.matmul(rbps[:, 0:64], onesrf[0:1, :],
                                      crow[0:1, :], start=True, stop=True),
         reads=["cb2", "onesrf"], writes=["rbps"], tag="c_bcast")
    a.op("vector", lambda e: e.tensor_copy(cb[:, :], rbps[:, 0:64]),
         reads=["rbps"], writes=["cb"], tag="c_ev")

    NJ = T // 128
    for b in range(B):
        a.dma("sync", vsb[:, :, :],
              vgd[b, :, :].rearrange("(kt p) v -> p kt v", p=128),
              writes=["vsb"], tag="v_ld")
        for h in range(H):
            par = h % 2
            a.dma("sync", kbuf[par * 64:(par + 1) * 64, :], ktd[h, :, b, :],
                  writes=["kbuf"], tag="k_ld")
            LAG = 3
            for j in range(NJ + LAG):
                if j < NJ:
                    pj = j % 4
                    a.op("tensor", lambda e, b=b, h=h, j=j, pj=pj, par=par:
                         e.matmul(sps[pj][:, :],
                                  kbuf[par * 64:(par + 1) * 64,
                                       j * 128:(j + 1) * 128],
                                  qt[par * 64:(par + 1) * 64, h // 2,
                                     b * 512:(b + 1) * 512],
                                  start=True, stop=True),
                         reads=["kbuf", "qt"], writes=[f"sps/{pj}"],
                         tag="s_mm")
                    a.op("scalar", lambda e, b=b, h=h, pj=pj: e.activation(
                        pbuf[pj][:, :], sps[pj][:, :], AF.Exp,
                        bias=cb[:, h * B + b:h * B + b + 1],
                        scale=np.float32(scale)),
                        reads=[f"sps/{pj}", "cb"], writes=[f"pb/{pj}"],
                        tag="exp")
                    if causal:
                        a.op("gpsimd", lambda e, j=j, pj=pj: e.tensor_tensor(
                            pbuf[pj][:, :], pbuf[pj][:, :], masks[:, j, :],
                            ALU.mult),
                            reads=[f"pb/{pj}", "masks"], writes=[f"pb/{pj}"],
                            tag="mask")
                if j >= LAG:
                    ja = j - LAG
                    pa = ja % 4
                    a.op("tensor", lambda e, h=h, ja=ja, pa=pa: e.matmul(
                        avps[0:65, :], vsb[:, ja, h * 65:(h + 1) * 65],
                        pbuf[pa][:, :], start=(ja == 0), stop=(ja == NJ - 1)),
                        reads=["vsb", f"pb/{pa}"], writes=["avps"],
                        tag="av_mm")
            a.op("vector", lambda e: e.tensor_scalar(
                rrow[0:1, :], avps[64:65, :], 1e-37, None,
                ALU.max),
                reads=["avps"], writes=["rrow"], tag="rho_max")
            a.op("vector", lambda e: e.reciprocal(rrow[0:1, :], rrow[0:1, :]),
                 reads=["rrow"], writes=["rrow"], tag="rho_rec")
            a.op("tensor", lambda e: e.matmul(rbps[0:64, :], ones64[0:1, :],
                                              rrow[0:1, :], start=True,
                                              stop=True),
                 reads=["rrow", "ones64"], writes=["rbps"], tag="rho_bcast")
            a.op("vector", lambda e: e.tensor_copy(rbs[:, :], rbps[0:64, :]),
                 reads=["rbps"], writes=["rbs"], tag="rho_ev")
            a.op("vector", lambda e, b=b, h=h: e.tensor_tensor(
                yT[(h % 2) * 64:(h % 2) * 64 + 64, h // 2,
                   b * 512:(b + 1) * 512],
                avps[0:64, :], rbs[:, :], ALU.mult),
                reads=["avps", "rbs"], writes=[f"y/{h}/{b}"], tag="y_ev")
    ykeys = [f"y/{h}/{b}" for h in range(H) for b in range(B)]
    for t in range(NTILE):
        xb_ = t % 2
        a.dma("sync", xbuf[xb_][:, :], xd[t * 128:(t + 1) * 128, :],
              reads=[f"xb/{xb_}"], writes=[f"xb/{xb_}"], tag="x_ld")
        for dh in range(2):
            pj = dh % 2
            for hc in range(FCH):
                a.op("tensor", lambda e, t=t, dh=dh, hc=hc, pj=pj: e.matmul(
                    sps[pj][:, :], yT[:, hc, t * 128:(t + 1) * 128],
                    wosb[:, hc, dh * 512:(dh + 1) * 512],
                    start=(hc == 0), stop=False),
                    reads=ykeys + [f"wo/{hc}"], writes=[f"sps/{pj}"],
                    tag="o_mm")
            a.op("tensor", lambda e, dh=dh, pj=pj: e.matmul(
                sps[pj][:, :], onesr[0:1, :], bob[0:1, dh * 512:(dh + 1) * 512],
                start=False, stop=True),
                reads=["onesr", "bob"], writes=[f"sps/{pj}"], tag="o_bias")
            a.op("vector", lambda e, dh=dh, pj=pj, xb_=xb_:
                 e.scalar_tensor_tensor(
                     xbuf[xb_][:, dh * 512:(dh + 1) * 512], sps[pj][:, :],
                     1.0, xbuf[xb_][:, dh * 512:(dh + 1) * 512],
                     ALU.mult, ALU.add),
                 reads=[f"sps/{pj}", f"xb/{xb_}"], writes=[f"xb/{xb_}"],
                 tag="o_resid")
        a.dma("gpsimd", xo[t * 128:(t + 1) * 128, :], xbuf[xb_][:, :],
              reads=[f"xb/{xb_}"], writes=[f"xb/{xb_}", "xo_st"],
              tag="x_st")
    a.finish(["xo_st"])
    return a.build()



def build_a0():
    a = ASB("a0")
    xd = a.dram_in("xs", [ROWS, C])
    wdd = a.dram_in("wd", [C, 128])
    wukd = a.dram_in("wukT", [128, FCH, 128])
    wuvd = a.dram_in("wuv", [128, C])
    wqd = a.dram_in("wq", [C, C])
    identd = a.dram_in("ident", [128, 128], BF16)
    lat_o = a.dram_out("lat", [128, ROWS], BF16)
    qat_o = a.dram_out("qat", [16 * 128, ROWS], BF16)
    v_o = a.dram_out("vv", [ROWS, 16 * 65], BF16)
    latmax_o = a.dram_out("latmax", [1, B])
    qamax_o = a.dram_out("qamax", [1, 64])

    xT = emit_stream_prologue(a, xd, identd)
    wsb = a.sb("wsb", [128, FCH, C], BF16)
    wdsb = a.sb("wdsb", [128, FCH, 128], BF16)
    wuk = a.sb("wuk", [128, FCH, 128], BF16)
    wuv = a.sb("wuv_sb", [128, C], BF16)
    onesf = a.sb("onesf", [128, 1], F32)
    latT = a.sb("latT", [128, ROWS], BF16)
    qtb = a.sb("qtb", [128, FCH, ROWS], BF16)
    qac = [a.sb(f"qac_{i}", [128, 512], BF16) for i in range(2)]
    vsb = a.sb("vsb", [128, NTILE, 16 * 65], BF16)
    qf = [a.sb(f"qf_{i}", [128, 512], F32) for i in range(2)]
    sq = [a.sb(f"sq_{i}", [128, 512], F32) for i in range(2)]
    nrow = a.sb("nrow", [1, 512], F32)
    latmax = a.sb("latmax_sb", [1, B], F32)
    qamax = a.sb("qamax_sb", [1, 64], F32)
    pps = [a.ps(f"pps_{i}", [128, 512], F32) for i in range(2)]
    nps = a.ps("nps", [1, 512], F32)

    emit_weight_cast(a, wdd, wdsb, FCH, "wdsb", width=128)
    wukf = a.sb("wukf", [128, FCH, 128], F32)
    a.dma("sync", wukf[:, :, :], wukd[:, :, :], writes=["wukf"], tag="wuk_ld")
    a.op("vector", lambda e: e.tensor_copy(wuk[:, :, :], wukf[:, :, :]),
         reads=["wukf"], writes=["wuk"], tag="wuk_cast")
    wuvf = a.sb("wuvf", [128, C], F32)
    a.dma("sync", wuvf[:, :], wuvd[:, :], writes=["wuvf"], tag="wuv_ld")
    a.op("vector", lambda e: e.tensor_copy(wuv[:, :], wuvf[:, :]),
         reads=["wuvf"], writes=["wuv"], tag="wuv_cast")
    a.op("vector", lambda e: e.memset(onesf[:, 0:1], 1.0), writes=["onesf"])

    for tg in range(4):
        pb_ = tg % 2
        for dc in range(FCH):
            a.op("tensor", lambda e, tg=tg, dc=dc, pb_=pb_: e.matmul(
                pps[pb_][:, :], wdsb[:, dc, :],
                xT[:, dc, tg * 512:(tg + 1) * 512],
                start=(dc == 0), stop=(dc == FCH - 1)),
                reads=[f"wdsb/{dc}"] + xt_keys(tg), writes=[f"pps/{pb_}"],
                tag="lat_mm")
        a.op("scalar", lambda e, pb_=pb_: e.activation(
            qf[pb_][:, :], pps[pb_][:, :], AF.Identity, bias=np.float32(0.0),
            scale=np.float32(1.0)),
            reads=[f"pps/{pb_}"], writes=[f"qf/{pb_}"], tag="lat_ev")
        a.op("vector", lambda e, tg=tg, pb_=pb_: e.tensor_copy(
            latT[:, tg * 512:(tg + 1) * 512], qf[pb_][:, :]),
            reads=[f"qf/{pb_}"], writes=[f"lat/{tg}"], tag="lat_cast")
        a.op("scalar", lambda e, pb_=pb_: e.activation(
            sq[pb_][:, :], qf[pb_][:, :], AF.Square, bias=np.float32(0.0),
            scale=np.float32(1.0)),
            reads=[f"qf/{pb_}"], writes=[f"sqb/{pb_}"], tag="lat_sq")
        a.op("tensor", lambda e, pb_=pb_: e.matmul(
            nps[0:1, :], onesf[:, 0:1], sq[pb_][:, :], start=True, stop=True),
            reads=["onesf", f"sqb/{pb_}"], writes=["nps"], tag="lat_nmm")
        a.op("vector", lambda e: e.tensor_copy(nrow[0:1, :], nps[0:1, :]),
             reads=["nps"], writes=["nrow"], tag="lat_nev")
        a.op("vector", lambda e, tg=tg: e.reduce_max(
            latmax[0:1, tg:tg + 1], nrow[0:1, :], AX.X),
            reads=["nrow"], writes=[f"lm/{tg}", "latmax"], tag="lat_max")
    a.dma("gpsimd", lat_o[:, :], latT[:, :],
          reads=[f"lat/{tg}" for tg in range(4)], writes=["lat_st"],
          tag="lat_st")
    a.dma("gpsimd", latmax_o[0:1, :], latmax[0:1, :],
          reads=[f"lm/{tg}" for tg in range(4)], writes=["lm_st"],
          tag="lm_st")

    emit_weight_cast(a, wqd, wsb, FCH, "wsb")
    for tg in range(4):
        for oc in range(FCH):
            pb_ = oc % 2
            for dc in range(FCH):
                a.op("tensor", lambda e, oc=oc, dc=dc, tg=tg, pb_=pb_:
                     e.matmul(pps[pb_][:, :],
                              wsb[:, dc, oc * 128:(oc + 1) * 128],
                              xT[:, dc, tg * 512:(tg + 1) * 512],
                              start=(dc == 0), stop=(dc == FCH - 1)),
                     reads=[f"wsb/{dc}"] + xt_keys(tg),
                     writes=[f"pps/{pb_}"], tag="q_mm")
            a.op("vector", lambda e, oc=oc, tg=tg, pb_=pb_: e.tensor_copy(
                qtb[:, oc, tg * 512:(tg + 1) * 512], pps[pb_][:, :]),
                reads=[f"pps/{pb_}"], writes=[f"qtb/{oc}/{tg}"], tag="q_cast")

    for h in range(H):
        par = h % 2
        for tg in range(4):
            pb_ = (h * 4 + tg) % 2
            a.op("tensor", lambda e, h=h, tg=tg, pb_=pb_, par=par: e.matmul(
                pps[pb_][:, :], wuk[par * 64:(par + 1) * 64, h // 2, :],
                qtb[par * 64:(par + 1) * 64, h // 2,
                    tg * 512:(tg + 1) * 512],
                start=True, stop=True),
                reads=["wuk", f"qtb/{h // 2}/{tg}"], writes=[f"pps/{pb_}"],
                tag="qa_mm")
            a.op("scalar", lambda e, pb_=pb_: e.activation(
                qf[pb_][:, :], pps[pb_][:, :], AF.Identity,
                bias=np.float32(0.0), scale=np.float32(1.0)),
                reads=[f"pps/{pb_}"], writes=[f"qf/{pb_}"], tag="qa_ev")
            a.op("vector", lambda e, h=h, tg=tg, pb_=pb_: e.tensor_copy(
                qac[pb_][:, :], qf[pb_][:, :]),
                reads=[f"qf/{pb_}"], writes=[f"qac/{pb_}"], tag="qa_cast")
            a.dma("gpsimd",
                  qat_o[h * 128:(h + 1) * 128, tg * 512:(tg + 1) * 512],
                  qac[pb_][:, :], reads=[f"qac/{pb_}"],
                  writes=[f"qac/{pb_}", "qa_st"], tag="qa_st")
            a.op("scalar", lambda e, pb_=pb_: e.activation(
                sq[pb_][:, :], qf[pb_][:, :], AF.Square, bias=np.float32(0.0),
                scale=np.float32(1.0)),
                reads=[f"qf/{pb_}"], writes=[f"sqb/{pb_}"], tag="qa_sq")
            a.op("tensor", lambda e, pb_=pb_: e.matmul(
                nps[0:1, :], onesf[:, 0:1], sq[pb_][:, :], start=True,
                stop=True),
                reads=["onesf", f"sqb/{pb_}"], writes=["nps"], tag="qa_nmm")
            a.op("vector", lambda e: e.tensor_copy(nrow[0:1, :], nps[0:1, :]),
                 reads=["nps"], writes=["nrow"], tag="qa_nev")
            a.op("vector", lambda e, h=h, tg=tg: e.reduce_max(
                qamax[0:1, h * B + tg:h * B + tg + 1], nrow[0:1, :], AX.X),
                reads=["nrow"], writes=[f"qam/{h}/{tg}", "qamax"],
                tag="qa_max")
    a.dma("gpsimd", qamax_o[0:1, :], qamax[0:1, :],
          reads=[f"qam/{h}/{tg}" for h in range(H) for tg in range(4)],
          writes=["qam_st"], tag="qam_st")

    a.op("vector", lambda e: e.memset(vsb[:, :, :].rearrange(
        "p t (h o) -> p t h o", o=65)[:, :, :, 64:65], 1.0),
        writes=["vones"], tag="v_ones")
    for t in range(NTILE):
        for vh in range(2):
            pb_ = (t * 2 + vh) % 2
            a.op("tensor", lambda e, t=t, vh=vh, pb_=pb_: e.matmul(
                pps[pb_][:, :], latT[:, t * 128:(t + 1) * 128],
                wuv[:, vh * 512:(vh + 1) * 512], start=True, stop=True),
                reads=["wuv"] + [f"lat/{tg}" for tg in range(4)],
                writes=[f"pps/{pb_}"], tag="v_mm")
            a.op("vector", lambda e, t=t, vh=vh, pb_=pb_: e.tensor_copy(
                vsb[:, t, :].rearrange("p (h o) -> p h o", o=65)
                [:, vh * 8:(vh + 1) * 8, 0:64],
                pps[pb_][:, :].rearrange("p (h d) -> p h d", d=64)),
                reads=[f"pps/{pb_}", "vones"], writes=[f"vev/{t}/{vh}"],
                tag="v_ev")
    a.dma("gpsimd", v_o.rearrange("(t p) v -> p t v", p=128), vsb[:, :, :],
          reads=[f"vev/{t}/{vh}" for t in range(NTILE) for vh in range(2)],
          writes=["v_st"], tag="v_st")

    a.finish(["lat_st", "lm_st", "qa_st", "qam_st", "v_st"])
    return a.build()



def build_a1(scale=float(D) ** -0.5):
    a = ASB("a1")
    xd = a.dram_in("xs", [ROWS, C])
    qatd = a.dram_in("qat", [16 * 128, ROWS], BF16)
    latgd = a.dram_in("latg", [128, B * T], BF16)
    vgd = a.dram_in("vg", [B, T, 16 * 65], BF16)
    qamaxd = a.dram_in("qamax", [1, 64])
    latmax8 = a.dram_in("latmax8", [1, NCORES * B])
    wod = a.dram_in("wo", [C, C])
    maskd = a.dram_in("masks", [T // 128, 128, 512], BF16)
    xo = a.dram_out("xo", [ROWS, C])

    latsb = a.sb("latsb", [128, B * T], BF16)
    qab = a.sb("qab", [128, H, 512], BF16)
    vsb = a.sb("vsb", [128, T // 128, 16 * 65], BF16)
    masks = a.sb("masksb", [128, T // 128, 512], BF16)
    wosb = a.sb("wosb", [128, FCH, C], BF16)
    onesr = a.sb("onesr", [1, 128], BF16)
    onesrf = a.sb("onesrf", [1, 128], F32)
    ones64 = a.sb("ones64", [1, 64], F32)
    yT = a.sb("yTb", [128, FCH, 512], BF16)
    pbuf = [a.sb(f"pbuf_{i}", [128, 512], BF16) for i in range(4)]
    qamax = a.sb("qamax_sb", [1, 64], F32)
    lm8 = a.sb("lm8", [1, NCORES * B], F32)
    lmb = a.sb("lmb", [1, 64], F32)
    crow = a.sb("crow", [1, 64], F32)
    cb = a.sb("cbsb", [128, 64], F32)
    rrow = a.sb("rrow", [1, 512], F32)
    rbs = a.sb("rbs", [64, 512], F32)
    xbuf = [a.sb(f"xbuf_{i}", [128, C], F32) for i in range(2)]
    sps = [a.ps(f"sps_{i}", [128, 512], F32) for i in range(4)]
    avps = a.ps("avps", [128, 512], F32)
    rbps = a.ps("rbps", [128, 512], F32)

    a.dma("sync", latsb[:, :], latgd[:, :], writes=["latsb"], tag="ld_lat")
    a.dma("sync", masks[:, :, :], maskd.rearrange("j p q -> p j q"),
          writes=["masks"], tag="ld_masks")
    a.dma("sync", qamax[0:1, :], qamaxd[0:1, :], writes=["qamax"],
          tag="ld_qam")
    a.dma("sync", lm8[0:1, :], latmax8[0:1, :], writes=["lm8"], tag="ld_lm8")
    a.op("vector", lambda e: e.memset(onesr[0:1, :], 1.0), writes=["onesr"])
    a.op("vector", lambda e: e.memset(onesrf[0:1, :], 1.0), writes=["onesrf"])
    a.op("vector", lambda e: e.memset(ones64[0:1, :], 1.0), writes=["ones64"])
    emit_weight_cast(a, wod, wosb, FCH, "wo")

    for b_ in range(B):
        a.op("vector", lambda e, b_=b_: e.reduce_max(
            lmb[0:1, b_:b_ + 1],
            lm8[0:1, :].rearrange("p (c b) -> p c b", b=B)[:, :, b_],
            AX.X), reads=["lm8"], writes=[f"lmx/{b_}"], tag="lm_max")
    for b_ in range(B):
        a.op("vector", lambda e, b_=b_: e.tensor_scalar(
            crow[0:1, :].rearrange("p (h b) -> p h b", b=B)[:, :, b_],
            ones64[0:1, 0:16], lmb[0:1, b_:b_ + 1], None, ALU.mult),
            reads=[f"lmx/{b_}", "ones64"], writes=["lmbcast"], tag="lm_bc")
    a.op("vector", lambda e: e.tensor_tensor(crow[0:1, :], crow[0:1, :],
                                             qamax[0:1, :], ALU.mult),
         reads=["lmbcast", "qamax"], writes=["cprod"], tag="cprod")
    a.op("scalar", lambda e: e.activation(crow[0:1, :], crow[0:1, :], AF.Sqrt,
                                          bias=np.float32(0.0),
                                          scale=np.float32(1.0)),
         reads=["cprod"], writes=["csqrt"], tag="csqrt")
    a.op("vector", lambda e: e.tensor_scalar(
        crow[0:1, :], crow[0:1, :], float(-scale * 1.02), -0.01,
        ALU.mult, ALU.add),
        reads=["csqrt"], writes=["cneg"], tag="cneg")
    a.op("tensor", lambda e: e.matmul(rbps[:, 0:64], onesrf[0:1, :],
                                      crow[0:1, :], start=True, stop=True),
         reads=["cneg", "onesrf"], writes=["rbps"], tag="c_bcast")
    a.op("vector", lambda e: e.tensor_copy(cb[:, :], rbps[:, 0:64]),
         reads=["rbps"], writes=["cb"], tag="c_ev")

    NJ = T // 128
    for b in range(B):
        a.dma("sync", vsb[:, :, :],
              vgd[b, :, :].rearrange("(kt p) v -> p kt v", p=128),
              writes=["vsb"], tag="v_ld")
        a.dma("sync", qab[:, :, :],
              qatd.rearrange("(h p) r -> p h r", p=128)
              [:, :, b * 512:(b + 1) * 512],
              writes=["qab"], tag="qa_ld")
        for h in range(H):
            LAG = 3
            for j in range(NJ + LAG):
                if j < NJ:
                    pj = j % 4
                    a.op("tensor", lambda e, b=b, h=h, j=j, pj=pj: e.matmul(
                        sps[pj][:, :],
                        latsb[:, b * T + j * 128:b * T + (j + 1) * 128],
                        qab[:, h, :], start=True, stop=True),
                        reads=["latsb", "qab"], writes=[f"sps/{pj}"],
                        tag="s_mm")
                    a.op("scalar", lambda e, b=b, h=h, pj=pj: e.activation(
                        pbuf[pj][:, :], sps[pj][:, :], AF.Exp,
                        bias=cb[:, h * B + b:h * B + b + 1],
                        scale=np.float32(scale)),
                        reads=[f"sps/{pj}", "cb"], writes=[f"pb/{pj}"],
                        tag="exp")
                    a.op("gpsimd", lambda e, j=j, pj=pj: e.tensor_tensor(
                        pbuf[pj][:, :], pbuf[pj][:, :], masks[:, j, :],
                        ALU.mult),
                        reads=[f"pb/{pj}", "masks"], writes=[f"pb/{pj}"],
                        tag="mask")
                if j >= LAG:
                    ja = j - LAG
                    pa = ja % 4
                    a.op("tensor", lambda e, h=h, ja=ja, pa=pa: e.matmul(
                        avps[0:65, :], vsb[:, ja, h * 65:(h + 1) * 65],
                        pbuf[pa][:, :], start=(ja == 0), stop=(ja == NJ - 1)),
                        reads=["vsb", f"pb/{pa}"], writes=["avps"],
                        tag="av_mm")
            a.op("vector", lambda e: e.tensor_scalar(
                rrow[0:1, :], avps[64:65, :], 1e-37, None, ALU.max),
                reads=["avps"], writes=["rrow"], tag="rho_max")
            a.op("vector", lambda e: e.reciprocal(rrow[0:1, :], rrow[0:1, :]),
                 reads=["rrow"], writes=["rrow"], tag="rho_rec")
            a.op("tensor", lambda e: e.matmul(rbps[0:64, :], ones64[0:1, :],
                                              rrow[0:1, :], start=True,
                                              stop=True),
                 reads=["rrow", "ones64"], writes=["rbps"], tag="rho_bcast")
            a.op("vector", lambda e: e.tensor_copy(rbs[:, :], rbps[0:64, :]),
                 reads=["rbps"], writes=["rbs"], tag="rho_ev")
            a.op("vector", lambda e, h=h: e.tensor_tensor(
                yT[(h % 2) * 64:(h % 2) * 64 + 64, h // 2, :],
                avps[0:64, :], rbs[:, :], ALU.mult),
                reads=["avps", "rbs"], writes=[f"y/{h}"], tag="y_ev")
        ykeys = [f"y/{h}" for h in range(H)]
        for i in range(4):
            t = b * 4 + i
            xb_ = i % 2
            a.dma("sync", xbuf[xb_][:, :], xd[t * 128:(t + 1) * 128, :],
                  reads=[f"xb/{xb_}"], writes=[f"xb/{xb_}"], tag="x_ld")
            for dh in range(2):
                pj = dh % 2
                for hc in range(FCH):
                    a.op("tensor", lambda e, i=i, dh=dh, hc=hc, pj=pj:
                         e.matmul(sps[pj][:, :],
                                  yT[:, hc, i * 128:(i + 1) * 128],
                                  wosb[:, hc, dh * 512:(dh + 1) * 512],
                                  start=(hc == 0), stop=(hc == FCH - 1)),
                         reads=ykeys + [f"wo/{hc}"], writes=[f"sps/{pj}"],
                         tag="o_mm")
                a.op("vector", lambda e, dh=dh, pj=pj, xb_=xb_:
                     e.scalar_tensor_tensor(
                         xbuf[xb_][:, dh * 512:(dh + 1) * 512], sps[pj][:, :],
                         1.0, xbuf[xb_][:, dh * 512:(dh + 1) * 512],
                         ALU.mult, ALU.add),
                     reads=[f"sps/{pj}", f"xb/{xb_}"], writes=[f"xb/{xb_}"],
                     tag="o_resid")
            a.dma("gpsimd", xo[t * 128:(t + 1) * 128, :], xbuf[xb_][:, :],
                  reads=[f"xb/{xb_}"], writes=[f"xb/{xb_}", "xo_st"],
                  tag="x_st")
    a.finish(["xo_st"])
    return a.build()



EPS_B = 1e-6
SQEPS2 = 2.0 * math.sqrt(EPS_B)


def build_b0():
    a = ASB("b0")
    xd = a.dram_in("xs", [ROWS, C])
    wqkvd = a.dram_in("wqkv", [C, 3 * C])
    bqc_d = a.dram_in("bqc", [128, FCH])
    bkc_d = a.dram_in("bkc", [128, FCH])
    bvr_d = a.dram_in("bvr", [1, C])
    ehd = a.dram_in("eh", [128, FCH, 16])
    identd = a.dram_in("ident", [128, 128], BF16)
    qaug_o = a.dram_out("qaug", [16, 68, ROWS], BF16)
    kaug_o = a.dram_out("kaug", [16, 68, ROWS], BF16)
    v_o = a.dram_out("vv", [ROWS, 16 * 65], BF16)

    xT = emit_stream_prologue(a, xd, identd)
    wsb = a.sb("wsb", [128, FCH, C], BF16)
    bcol = a.sb("bcol", [128, FCH], F32)
    ehf = a.sb("ehf", [128, FCH, 16], F32)
    bvr = a.sb("bvr_sb", [1, C], F32)
    bvb = a.sb("bvb", [1, C], BF16)
    onesr = a.sb("onesr", [1, 128], BF16)
    qf = [a.sb(f"qf_{i}", [128, 512], F32) for i in range(2)]
    sq = [a.sb(f"sq_{i}", [128, 512], F32) for i in range(2)]
    qb = [a.sb(f"qb_{i}", [128, 512], BF16) for i in range(2)]
    n2b = a.sb("n2b", [16, ROWS], BF16)
    vsb = a.sb("vsb", [128, NTILE, 16 * 65], BF16)
    consts = a.sb("consts", [1, ROWS], BF16)
    pps = [a.ps(f"pps_{i}", [128, 512], F32) for i in range(2)]
    nps = a.ps("nps", [16, 512], F32)

    a.dma("sync", ehf[:, :, :], ehd[:, :, :], writes=["ehf"], tag="ld_eh")
    a.dma("sync", bvr[0:1, :], bvr_d[0:1, :], writes=["bvr"], tag="ld_bv")
    a.op("vector", lambda e: e.tensor_copy(bvb[0:1, :], bvr[0:1, :]),
         reads=["bvr"], writes=["bvb"], tag="bv_cast")
    a.op("vector", lambda e: e.memset(onesr[0:1, :], 1.0), writes=["onesr"])

    def proj_aug(col_off, bd_, out_dram, scale_cast, nrm_scale, c64,
                 pname):
        emit_weight_cast(a, wqkvd[:, col_off:col_off + C], wsb, FCH, "wsb")
        a.dma("sync", bcol[:, :], bd_[:, :], reads=["bcol"], writes=["bcol"],
              tag=f"{pname}_bld")
        for tg in range(4):
            for oc in range(FCH):
                pb_ = oc % 2
                for dc in range(FCH):
                    a.op("tensor", lambda e, oc=oc, dc=dc, tg=tg, pb_=pb_:
                         e.matmul(pps[pb_][:, :],
                                  wsb[:, dc, oc * 128:(oc + 1) * 128],
                                  xT[:, dc, tg * 512:(tg + 1) * 512],
                                  start=(dc == 0), stop=(dc == FCH - 1)),
                         reads=[f"wsb/{dc}"] + xt_keys(tg),
                         writes=[f"pps/{pb_}"], tag=f"{pname}_mm")
                a.op("scalar", lambda e, oc=oc, pb_=pb_: e.activation(
                    qf[pb_][:, :], pps[pb_][:, :], AF.Identity,
                    bias=bcol[:, oc:oc + 1], scale=np.float32(1.0)),
                    reads=[f"pps/{pb_}", "bcol"], writes=[f"qf/{pb_}"],
                    tag=f"{pname}_ev")
                a.op("vector", lambda e, pb_=pb_: e.tensor_scalar(
                    qb[pb_][:, :], qf[pb_][:, :], float(scale_cast), None,
                    ALU.mult),
                    reads=[f"qf/{pb_}"], writes=[f"qb/{pb_}"],
                    tag=f"{pname}_cast")
                for hh in range(2):
                    h = 2 * oc + hh
                    a.dma("gpsimd",
                          out_dram[h, 0:64, tg * 512:(tg + 1) * 512],
                          qb[pb_][hh * 64:(hh + 1) * 64, :],
                          reads=[f"qb/{pb_}"], writes=[f"qb/{pb_}", "aug_st"],
                          tag=f"{pname}_st")
                a.op("scalar", lambda e, pb_=pb_: e.activation(
                    sq[pb_][:, :], qf[pb_][:, :], AF.Square,
                    bias=np.float32(0.0), scale=np.float32(1.0)),
                    reads=[f"qf/{pb_}"], writes=[f"sqb/{pb_}"],
                    tag=f"{pname}_sq")
                a.op("tensor", lambda e, oc=oc, pb_=pb_: e.matmul(
                    nps[:, :], ehf[:, oc, :], sq[pb_][:, :],
                    start=(oc == 0), stop=(oc == FCH - 1)),
                    reads=["ehf", f"sqb/{pb_}"], writes=["nps"],
                    tag=f"{pname}_emm")
            a.op("vector", lambda e, tg=tg: e.tensor_scalar(
                n2b[:, tg * 512:(tg + 1) * 512], nps[:, :],
                float(nrm_scale), None, ALU.mult),
                reads=["nps"], writes=[f"n2b/{tg}"], tag=f"{pname}_n2")
        nrow_idx = 64 if c64 is None else 65
        for h in range(H):
            a.dma("gpsimd", out_dram[h, nrow_idx:nrow_idx + 1, :],
                  n2b[h:h + 1, :],
                  reads=[f"n2b/{tg}" for tg in range(4)],
                  writes=["aug_st"], tag=f"{pname}_nst")
        crow_idx = 65 if c64 is None else 64
        cval = 1.0 if c64 is None else c64
        for val, row in ((cval, crow_idx), (SQEPS2, 66), (0.0, 67)):
            a.op("vector", lambda e, val=val: e.memset(consts[0:1, :], val),
                 writes=["consts"], tag="c_set")
            for h in range(H):
                a.dma("gpsimd", out_dram[h, row:row + 1, :], consts[0:1, :],
                      reads=["consts"], writes=["aug_st"], tag=f"{pname}_cst")

    proj_aug(0, bqc_d, qaug_o, 4.0, 1.0, None, "q")
    proj_aug(C, bkc_d, kaug_o, -2.0, 4.0, 4.0, "k")

    emit_weight_cast(a, wqkvd[:, 2 * C:3 * C], wsb, FCH, "wsb")
    a.op("vector", lambda e: e.memset(vsb[:, :, :].rearrange(
        "p t (h o) -> p t h o", o=65)[:, :, :, 64:65], 1.0),
        writes=["vones"], tag="v_ones")
    for t in range(NTILE):
        for vh in range(2):
            pb_ = (t * 2 + vh) % 2
            for dc in range(FCH):
                a.op("tensor", lambda e, t=t, vh=vh, dc=dc, pb_=pb_: e.matmul(
                    pps[pb_][:, :], xT[:, dc, t * 128:(t + 1) * 128],
                    wsb[:, dc, vh * 512:(vh + 1) * 512],
                    start=(dc == 0), stop=False),
                    reads=[f"wsb/{dc}", f"xT/{t}/{dc}"], writes=[f"pps/{pb_}"],
                    tag="v_mm")
            a.op("tensor", lambda e, vh=vh, pb_=pb_: e.matmul(
                pps[pb_][:, :], onesr[0:1, :], bvb[0:1, vh * 512:(vh + 1) * 512],
                start=False, stop=True),
                reads=["onesr", "bvb"], writes=[f"pps/{pb_}"], tag="v_bias")
            a.op("vector", lambda e, t=t, vh=vh, pb_=pb_: e.tensor_copy(
                vsb[:, t, :].rearrange("p (h o) -> p h o", o=65)
                [:, vh * 8:(vh + 1) * 8, 0:64],
                pps[pb_][:, :].rearrange("p (h d) -> p h d", d=64)),
                reads=[f"pps/{pb_}", "vones"], writes=[f"vev/{t}/{vh}"],
                tag="v_ev")
    a.dma("gpsimd", v_o.rearrange("(t p) v -> p t v", p=128), vsb[:, :, :],
          reads=[f"vev/{t}/{vh}" for t in range(NTILE) for vh in range(2)],
          writes=["v_st"], tag="v_st")

    a.finish(["aug_st", "v_st"])
    return a.build()



def build_b1(debug=False):
    a = ASB("b1")
    xd = a.dram_in("xs", [ROWS, C])
    qaugd = a.dram_in("qaug", [16, 68, ROWS], BF16)
    kaugd = a.dram_in("kaugg", [16, 68, B, T], BF16)
    vgd = a.dram_in("vg", [B, T, 16 * 65], BF16)
    wod = a.dram_in("wo", [C, C])
    bod = a.dram_in("bor", [1, C])
    maskd = a.dram_in("masks", [T // 128, 128, 512], BF16)
    identd = a.dram_in("ident", [128, 128], BF16)
    xo = a.dram_out("xo", [ROWS, C])

    qaug = a.sb("qaugsb", [68, 16, 512], BF16)
    vsb = a.sb("vsb", [128, T // 128, 16 * 65], BF16)
    kbuf = a.sb("kbuf", [68, T], BF16)
    ident = a.sb("identsb", [128, 128], BF16)
    wosb = a.sb("wosb", [128, FCH, C], BF16)
    borow = a.sb("borow", [1, C], F32)
    bob = a.sb("bob", [1, C], BF16)
    onesr = a.sb("onesr", [1, 128], BF16)
    onesrf = a.sb("onesrf", [1, 128], F32)
    ones64 = a.sb("ones64", [1, 64], F32)
    yT = a.sb("yTb", [128, FCH, 512], BF16)
    masks = a.sb("masksb", [128, T // 128, 512], BF16)
    sbig = a.sb("sbig", [128, T // 128, 256], F32)
    d2 = [a.sb(f"d2_{i}", [128, 256], F32) for i in range(4)]
    rc = [a.sb(f"rc_{i}", [128, 256], F32) for i in range(4)]
    pbuf = [a.sb(f"pbuf_{i}", [128, 256], BF16) for i in range(4)]
    colmax = a.sb("colmax", [128, 1], F32)
    cmrow = a.sb("cmrow", [1, 128], F32)
    cneg = a.sb("cneg1", [1, 1], F32)
    cb = a.sb("cbsb", [128, 1], F32)
    rrow = a.sb("rrow", [1, 256], F32)
    rbs = a.sb("rbs", [64, 256], F32)
    xbuf = [a.sb(f"xbuf_{i}", [128, C], F32) for i in range(2)]
    sps = [a.ps(f"sps_{i}", [128, 512], F32) for i in range(4)]
    avps = a.ps("avps", [128, 256], F32)
    tps = a.ps("tps", [128, 128], BF16)
    ops2 = [a.ps("ops2_0", [128, 512], F32)]
    cmb = a.sb("cmb", [128, 128], BF16)

    a.dma("sync", masks[:, :, :], maskd.rearrange("j p q -> p j q"),
          writes=["masks"], tag="ld_masks")
    a.dma("sync", ident[:], identd[:, :], writes=["ident"], tag="ld_ident")
    a.op("vector", lambda e: e.memset(onesr[0:1, :], 1.0), writes=["onesr"])
    a.op("vector", lambda e: e.memset(onesrf[0:1, :], 1.0), writes=["onesrf"])
    a.op("vector", lambda e: e.memset(ones64[0:1, :], 1.0), writes=["ones64"])
    a.op("vector", lambda e: e.memset(cmb[:, :], 0.0), writes=["cmb"],
         tag="init")
    a.op("vector", lambda e: e.memset(yT[:, :, :], 0.0),
         writes=[f"y/{h}" for h in range(H)], tag="init")
    emit_weight_cast(a, wod, wosb, FCH, "wo")
    a.dma("sync", borow[0:1, :], bod[0:1, :], writes=["bo"], tag="ld_bo")
    a.op("vector", lambda e: e.tensor_copy(bob[0:1, :], borow[0:1, :]),
         reads=["bo"], writes=["bob"], tag="bo_cast")

    NJ = T // 128
    if debug:
        a._cdbg = a.dram_out("dbg_cd", [B * H * 2, 1])
    for b in range(B):
        a.dma("sync", vsb[:, :, :],
              vgd[b, :, :].rearrange("(kt p) v -> p kt v", p=128),
              writes=["vsb"], tag="v_ld")
        a.dma("sync", qaug[:, :, :],
              qaugd.rearrange("h a r -> a h r")
              [:, :, b * 512:(b + 1) * 512],
              writes=["qaug"], tag="qa_ld")
        for h in range(H):
            a.dma("sync", kbuf[:, :], kaugd[h, :, b, :],
                  writes=["kbuf"], tag="k_ld")
            for qh in range(2):
                qsl = slice(qh * 256, (qh + 1) * 256)
                for j in range(NJ):
                    pj = j % 4
                    a.op("tensor", lambda e, j=j, pj=pj, h=h, qh=qh: e.matmul(
                        sps[pj][:, 0:256], kbuf[0:64, j * 128:(j + 1) * 128],
                        qaug[0:64, h, qh * 256:(qh + 1) * 256],
                        start=True, stop=True),
                        reads=["kbuf", "qaug"], writes=[f"sps/{pj}"],
                        tag="dot_mm")
                    a.op("tensor", lambda e, j=j, pj=pj, h=h, qh=qh: e.matmul(
                        sps[pj][:, 256:512], kbuf[0:68, j * 128:(j + 1) * 128],
                        qaug[0:68, h, qh * 256:(qh + 1) * 256],
                        start=True, stop=True),
                        reads=["kbuf", "qaug"], writes=[f"sps/{pj}"],
                        tag="den_mm")
                    a.op("scalar", lambda e, pj=pj: e.activation(
                        d2[pj][:, :], sps[pj][:, 0:256], AF.Square,
                        bias=np.float32(0.0), scale=np.float32(0.25)),
                        reads=[f"sps/{pj}"], writes=[f"d2/{pj}"], tag="sq")
                    a.op("vector", lambda e, pj=pj: e.reciprocal(
                        rc[pj][:, :], sps[pj][:, 256:512]),
                        reads=[f"sps/{pj}", f"d2/{pj}"], writes=[f"rc/{pj}"],
                        tag="recip")
                    a.op("gpsimd", lambda e, j=j, pj=pj: e.tensor_tensor(
                        sbig[:, j, :], d2[pj][:, :], rc[pj][:, :], ALU.mult),
                        reads=[f"d2/{pj}", f"rc/{pj}"], writes=[f"sb/{j}"],
                        tag="smul")
                a.op("vector", lambda e: e.reduce_max(
                    colmax[:, 0:1],
                    sbig[:, :, :].rearrange("p j q -> p (j q)"), AX.X),
                    reads=[f"sb/{j}" for j in range(NJ)], writes=["colmax"],
                    tag="cmax")
                a.op("vector", lambda e: e.tensor_copy(cmb[:, 0:1],
                                                       colmax[:, 0:1]),
                     reads=["colmax"], writes=["cmb"], tag="cm_cast")
                a.op("tensor", lambda e: e.transpose(
                    tps[:, 0:128], cmb[:, 0:128], ident[:]),
                    reads=["cmb", "ident"], writes=["tps"], tag="cm_tr")
                a.op("vector", lambda e: e.tensor_copy(cmrow[0:1, :],
                                                       tps[0:1, 0:128]),
                     reads=["tps"], writes=["cmrow"], tag="cm_ev")
                a.op("vector", lambda e: e.reduce_max(
                    cneg[0:1, 0:1], cmrow[0:1, :], AX.X),
                    reads=["cmrow"], writes=["cneg"], tag="cm_red")
                a.op("vector", lambda e: e.tensor_scalar(
                    cneg[0:1, 0:1], cneg[0:1, 0:1], -1.0, None, ALU.mult),
                    reads=["cneg"], writes=["cneg"], tag="cm_neg")
                a.op("tensor", lambda e: e.matmul(
                    ops2[0][:, 0:1], onesrf[0:1, :], cneg[0:1, 0:1],
                    start=True, stop=True),
                    reads=["cneg", "onesrf"], writes=["ops2/0"], tag="c_bc")
                a.op("vector", lambda e: e.tensor_copy(cb[:, 0:1],
                                                       ops2[0][:, 0:1]),
                     reads=["ops2/0"], writes=["cb"], tag="c_ev")
                if debug:
                    idx = (b * H + h) * 2 + qh
                    a.dma("gpsimd", a._cdbg[idx:idx + 1, 0:1], cneg[0:1, 0:1],
                          reads=["cneg"], writes=["cdbg_st"], tag="cdbg")
                LAG = 3
                for j in range(NJ + LAG):
                    if j < NJ:
                        pj = j % 4
                        a.op("scalar", lambda e, j=j, pj=pj: e.activation(
                            pbuf[pj][:, :], sbig[:, j, :], AF.Exp,
                            bias=cb[:, 0:1], scale=np.float32(1.0)),
                            reads=[f"sb/{j}", "cb"], writes=[f"pb/{pj}"],
                            tag="exp")
                        a.op("gpsimd", lambda e, j=j, pj=pj, qh=qh:
                             e.tensor_tensor(
                                 pbuf[pj][:, :], pbuf[pj][:, :],
                                 masks[:, j, qh * 256:(qh + 1) * 256],
                                 ALU.mult),
                             reads=[f"pb/{pj}", "masks"], writes=[f"pb/{pj}"],
                             tag="mask")
                    if j >= LAG:
                        ja = j - LAG
                        pa = ja % 4
                        a.op("tensor", lambda e, h=h, ja=ja, pa=pa: e.matmul(
                            avps[0:65, :], vsb[:, ja, h * 65:(h + 1) * 65],
                            pbuf[pa][:, :], start=(ja == 0),
                            stop=(ja == NJ - 1)),
                            reads=["vsb", f"pb/{pa}"], writes=["avps"],
                            tag="av_mm")
                a.op("vector", lambda e: e.tensor_scalar(
                    rrow[0:1, :], avps[64:65, :], 1e-37, None, ALU.max),
                    reads=["avps"], writes=["rrow"], tag="rho_max")
                a.op("vector", lambda e: e.reciprocal(rrow[0:1, :],
                                                      rrow[0:1, :]),
                     reads=["rrow"], writes=["rrow"], tag="rho_rec")
                a.op("tensor", lambda e: e.matmul(
                    ops2[0][0:64, 0:256], ones64[0:1, :], rrow[0:1, :],
                    start=True, stop=True),
                    reads=["rrow", "ones64"], writes=["ops2/0"], tag="rho_bc")
                a.op("vector", lambda e: e.tensor_copy(rbs[:, :],
                                                       ops2[0][0:64, 0:256]),
                     reads=["ops2/0"], writes=["rbs"], tag="rho_ev")
                a.op("vector", lambda e, h=h, qsl=qsl: e.tensor_tensor(
                    yT[(h % 2) * 64:(h % 2) * 64 + 64, h // 2, qsl],
                    avps[0:64, :], rbs[:, :], ALU.mult),
                    reads=["avps", "rbs"], writes=[f"y/{h}"], tag="y_ev")
        ykeys = [f"y/{h}" for h in range(H)]
        for i in range(4):
            t = b * 4 + i
            xb_ = i % 2
            a.dma("sync", xbuf[xb_][:, :], xd[t * 128:(t + 1) * 128, :],
                  reads=[f"xb/{xb_}"], writes=[f"xb/{xb_}"], tag="x_ld")
            for dh in range(2):
                for hc in range(FCH):
                    a.op("tensor", lambda e, i=i, dh=dh, hc=hc:
                         e.matmul(ops2[0][:, :],
                                  yT[:, hc, i * 128:(i + 1) * 128],
                                  wosb[:, hc, dh * 512:(dh + 1) * 512],
                                  start=(hc == 0), stop=False),
                         reads=ykeys + [f"wo/{hc}"], writes=["ops2/0"],
                         tag="o_mm")
                a.op("tensor", lambda e, dh=dh: e.matmul(
                    ops2[0][:, :], onesr[0:1, :],
                    bob[0:1, dh * 512:(dh + 1) * 512],
                    start=False, stop=True),
                    reads=["onesr", "bob"], writes=["ops2/0"],
                    tag="o_bias")
                a.op("vector", lambda e, dh=dh, xb_=xb_:
                     e.scalar_tensor_tensor(
                         xbuf[xb_][:, dh * 512:(dh + 1) * 512],
                         ops2[0][:, :], 1.0,
                         xbuf[xb_][:, dh * 512:(dh + 1) * 512],
                         ALU.mult, ALU.add),
                     reads=["ops2/0", f"xb/{xb_}"], writes=[f"xb/{xb_}"],
                     tag="o_resid")
            a.dma("gpsimd", xo[t * 128:(t + 1) * 128, :], xbuf[xb_][:, :],
                  reads=[f"xb/{xb_}"], writes=[f"xb/{xb_}", "xo_st"],
                  tag="x_st")
    if debug:
        sdbg = a.dram_out("dbg_s", [128, T // 128, 256])
        ydbg = a.dram_out("dbg_y", [128, FCH, 512], BF16)
        a.dma("gpsimd", sdbg[:, :, :], sbig[:, :, :],
              reads=[f"sb/{j}" for j in range(T // 128)], writes=["dbg_st"],
              tag="dbg")
        a.dma("gpsimd", ydbg[:, :, :], yT[:, :, :],
              reads=[f"y/{h}" for h in range(H)], writes=["dbg_st"], tag="dbg")
        a.finish(["xo_st", "dbg_st", "cdbg_st"])
    else:
        a.finish(["xo_st"])
    return a.build()



def build_g():
    a = ASB("g")
    xd = a.dram_in("xs", [ROWS, C])
    wqd = a.dram_in("wqT", [C, C])
    wkd = a.dram_in("wkT", [C, C])
    wvd = a.dram_in("wvT", [C, C])
    wod = a.dram_in("woT", [C, C])
    wwd = a.dram_in("wwT", [C, 16])
    wrd = a.dram_in("wrT", [C, 16])
    bqd = a.dram_in("bqc", [128, FCH])
    bkd = a.dram_in("bkc", [128, FCH])
    bvd = a.dram_in("bvc", [128, FCH])
    bod = a.dram_in("bor", [1, C])
    bwd = a.dram_in("bwc", [16, 1])
    brd = a.dram_in("brc", [16, 1])
    ehd = a.dram_in("eh", [128, FCH, 16])
    seld = a.dram_in("sel", [16, FCH, 128])
    identd = a.dram_in("ident", [128, 128], BF16)
    xo = a.dram_out("xo", [ROWS, C])

    xT = emit_stream_prologue(a, xd, identd, nbuf=1)
    xln_t = a._xln_tensor
    qn = xln_t[:, :, :].rearrange("p a b -> p (a b)").rearrange(
        "p (c r) -> p c r", c=FCH)
    wsb = a.sb("wsb", [128, FCH, C], BF16)
    wsmall = a.sb("wsmall", [128, FCH, 16], BF16)
    bcol = a.sb("bcol", [128, FCH], F32)
    bcol16 = a.sb("bcol16", [16, 1], F32)
    ehb = a.sb("ehb", [128, FCH, 16], BF16)
    ehf = a.sb("ehf", [128, FCH, 16], F32)
    self_ = a.sb("self_", [16, FCH, 128], F32)
    selb = a.sb("selb", [16, FCH, 128], BF16)
    ee = a.sb("ee", [128, 128], BF16)
    onesr = a.sb("onesr", [1, 128], BF16)
    borow = a.sb("borow", [1, C], F32)
    bob = a.sb("bob", [1, C], BF16)
    kn = a.sb("kn", [128, FCH, ROWS], BF16)
    vw = a.sb("vw", [128, FCH, ROWS], BF16)
    wrows = a.sb("wrows", [16, ROWS], BF16)
    rrows = a.sb("rrows", [16, ROWS], BF16)
    grows = a.sb("grows", [16, 512], BF16)
    qf = [a.sb("qf_0", [128, 512], F32)]
    sqb = [a.sb("sqb_0", [128, 512], BF16)]
    rn = [a.sb("rn_0", [128, 512], F32)]
    prodb = a.sb("prodb", [128, 512], BF16)
    pr2 = a.sb("pr2", [128, 512], BF16)
    outacc = a.sb("outacc", [128, FCH, 512], BF16)
    yTb = a.sb("yTb", [128, FCH, 512], BF16)
    xbuf = [a.sb("xbuf_0", [128, C], F32)]
    pps = [a.ps(f"pps_{i}", [128, 512], F32) for i in range(2)]
    bps = a.ps("bps", [128, 512], F32)
    nsqps = a.ps("nsqps", [128, 512], F32)
    gps = a.ps("gps", [16, 512], F32)
    ops2 = a.ps("ops2", [128, 512], F32)

    a.op("sync", lambda e: e.nop(),
         reads=[f"xT/{t}/{c}" for t in range(NTILE) for c in range(FCH)],
         writes=["xlnfree"], tag="xln_fence")

    a.dma("sync", ehf[:, :, :], ehd[:, :, :], writes=["ehf"], tag="ld_eh")
    a.op("vector", lambda e: e.tensor_copy(ehb[:, :, :], ehf[:, :, :]),
         reads=["ehf"], writes=["ehb"], tag="eh_cast")
    a.dma("sync", self_[:, :, :], seld[:, :, :], writes=["self"], tag="ld_sel")
    a.op("vector", lambda e: e.tensor_copy(selb[:, :, :], self_[:, :, :]),
         reads=["self"], writes=["selb"], tag="sel_cast")
    a.op("vector", lambda e: e.memset(ee[:, :], 0.0), writes=["ee"], tag="ee0")
    a.op("vector", lambda e: e.memset(ee[0:64, 0:64], 1.0),
         reads=["ee"], writes=["ee"], tag="ee1")
    a.op("vector", lambda e: e.memset(ee[64:128, 64:128], 1.0),
         reads=["ee"], writes=["ee"], tag="ee2")
    a.op("vector", lambda e: e.memset(onesr[0:1, :], 1.0), writes=["onesr"])
    a.dma("sync", borow[0:1, :], bod[0:1, :], writes=["bo"], tag="ld_bo")
    a.op("vector", lambda e: e.tensor_copy(bob[0:1, :], borow[0:1, :]),
         reads=["bo"], writes=["bob"], tag="bo_cast")

    for name, wdram, bdram, dst in (("w", wwd, bwd, wrows),
                                    ("r", wrd, brd, rrows)):
        emit_weight_cast(a, wdram, wsmall, FCH, "wsmall", width=16)
        a.dma("sync", bcol16[:, 0:1], bdram[:, :], reads=["bcol16"],
              writes=["bcol16"], tag=f"{name}_bld")
        for tg in range(4):
            for dc in range(FCH):
                a.op("tensor", lambda e, tg=tg, dc=dc: e.matmul(
                    gps[:, :], wsmall[:, dc, :],
                    xT[:, dc, tg * 512:(tg + 1) * 512],
                    start=(dc == 0), stop=(dc == FCH - 1)),
                    reads=[f"wsmall/{dc}"] + xt_keys(tg), writes=["gps"],
                    tag=f"{name}_mm")
            a.op("scalar", lambda e, tg=tg, dst=dst: e.activation(
                dst[:, tg * 512:(tg + 1) * 512], gps[:, :], AF.Sigmoid,
                bias=bcol16[:, 0:1], scale=np.float32(1.0)),
                reads=["gps", "bcol16"], writes=[f"{name}rows/{tg}"],
                tag=f"{name}_ev")
    wrkeys = [f"wrows/{tg}" for tg in range(4)]
    rrkeys = [f"rrows/{tg}" for tg in range(4)]

    def proj_norm(wdram, bdram, dst, dkey, pname):
        emit_weight_cast(a, wdram, wsb, FCH, "wsb")
        a.dma("sync", bcol[:, :], bdram[:, :], reads=["bcol"],
              writes=["bcol"], tag=f"{pname}_bld")
        for tg in range(4):
            for oc in range(FCH):
                pb_ = 0
                for dc in range(FCH):
                    a.op("tensor", lambda e, oc=oc, dc=dc, tg=tg, pb_=pb_:
                         e.matmul(pps[pb_][:, :],
                                  wsb[:, dc, oc * 128:(oc + 1) * 128],
                                  xT[:, dc, tg * 512:(tg + 1) * 512],
                                  start=(dc == 0), stop=(dc == FCH - 1)),
                         reads=[f"wsb/{dc}"] + xt_keys(tg),
                         writes=[f"pps/{pb_}"], tag=f"{pname}_mm")
                a.op("scalar", lambda e, oc=oc, pb_=pb_: e.activation(
                    qf[pb_][:, :], pps[pb_][:, :], AF.Identity,
                    bias=bcol[:, oc:oc + 1], scale=np.float32(1.0)),
                    reads=[f"pps/{pb_}", "bcol"], writes=[f"qf/{pb_}"],
                    tag=f"{pname}_ev")
                a.op("scalar", lambda e, pb_=pb_: e.activation(
                    sqb[pb_][:, :], qf[pb_][:, :], AF.Square,
                    bias=np.float32(0.0), scale=np.float32(1.0)),
                    reads=[f"qf/{pb_}"], writes=[f"sqb/{pb_}"],
                    tag=f"{pname}_sq")
                a.op("tensor", lambda e, pb_=pb_: e.matmul(
                    nsqps[:, :], ee[:, :], sqb[pb_][:, :],
                    start=True, stop=True),
                    reads=["ee", f"sqb/{pb_}"], writes=["nsqps"],
                    tag=f"{pname}_eemm")
                a.op("vector", lambda e, pb_=pb_: e.tensor_scalar(
                    rn[pb_][:, :], nsqps[:, :], 1e-24, None, ALU.max),
                    reads=["nsqps"], writes=[f"rn/{pb_}"], tag=f"{pname}_clip")
                a.op("scalar", lambda e, pb_=pb_: e.activation(
                    rn[pb_][:, :], rn[pb_][:, :], AF.Ln,
                    bias=np.float32(0.0), scale=np.float32(1.0)),
                    reads=[f"rn/{pb_}"], writes=[f"rn/{pb_}"],
                    tag=f"{pname}_ln")
                a.op("scalar", lambda e, pb_=pb_: e.activation(
                    rn[pb_][:, :], rn[pb_][:, :], AF.Exp,
                    bias=np.float32(0.0), scale=np.float32(-0.5)),
                    reads=[f"rn/{pb_}"], writes=[f"rn/{pb_}"],
                    tag=f"{pname}_exp")
                a.op("vector", lambda e, oc=oc, tg=tg, pb_=pb_, dst=dst:
                     e.tensor_tensor(dst[:, oc, tg * 512:(tg + 1) * 512],
                                     qf[pb_][:, :], rn[pb_][:, :], ALU.mult),
                     reads=[f"qf/{pb_}", f"rn/{pb_}", "xlnfree"],
                     writes=[f"{dkey}/{oc}/{tg}"], tag=f"{pname}_n")

    proj_norm(wqd, bqd, qn, "qn", "q")
    proj_norm(wkd, bkd, kn, "kn", "k")

    emit_weight_cast(a, wvd, wsb, FCH, "wsb")
    a.dma("sync", bcol[:, :], bvd[:, :], reads=["bcol"], writes=["bcol"],
          tag="v_bld")
    for tg in range(4):
        for oc in range(FCH):
            pb_ = 0
            for dc in range(FCH):
                a.op("tensor", lambda e, oc=oc, dc=dc, tg=tg, pb_=pb_:
                     e.matmul(pps[pb_][:, :],
                              wsb[:, dc, oc * 128:(oc + 1) * 128],
                              xT[:, dc, tg * 512:(tg + 1) * 512],
                              start=(dc == 0), stop=(dc == FCH - 1)),
                     reads=[f"wsb/{dc}"] + xt_keys(tg),
                     writes=[f"pps/{pb_}"], tag="v_mm")
            a.op("scalar", lambda e, oc=oc, pb_=pb_: e.activation(
                qf[pb_][:, :], pps[pb_][:, :], AF.Identity,
                bias=bcol[:, oc:oc + 1], scale=np.float32(1.0)),
                reads=[f"pps/{pb_}", "bcol"], writes=[f"qf/{pb_}"],
                tag="v_ev")
            a.op("tensor", lambda e, oc=oc, tg=tg: e.matmul(
                bps[:, :], selb[:, oc, :],
                wrows[:, tg * 512:(tg + 1) * 512],
                start=True, stop=True),
                reads=["selb"] + wrkeys, writes=["bps"], tag="v_selmm")
            a.op("vector", lambda e, oc=oc, tg=tg, pb_=pb_: e.tensor_tensor(
                vw[:, oc, tg * 512:(tg + 1) * 512], qf[pb_][:, :],
                bps[:, :], ALU.mult),
                reads=[f"qf/{pb_}", "bps"], writes=[f"vw/{oc}/{tg}"],
                tag="v_n")

    qnkeys = [f"qn/{oc}/{tg}" for oc in range(FCH) for tg in range(4)]
    knkeys = [f"kn/{oc}/{tg}" for oc in range(FCH) for tg in range(4)]
    vwkeys = [f"vw/{oc}/{tg}" for oc in range(FCH) for tg in range(4)]
    for s in range(B):
        scols = slice(s * 512, (s + 1) * 512)
        for s2 in range(B):
            s2cols = slice(s2 * 512, (s2 + 1) * 512)
            for oc in range(FCH):
                a.op("vector", lambda e, oc=oc, scols=scols, s2cols=s2cols:
                     e.tensor_tensor(prodb[:, :], qn[:, oc, scols],
                                     kn[:, oc, s2cols], ALU.mult),
                     reads=qnkeys + knkeys, writes=["prodb"], tag="g_tt")
                a.op("tensor", lambda e, oc=oc: e.matmul(
                    gps[:, :], ehb[:, oc, :], prodb[:, :],
                    start=(oc == 0), stop=(oc == FCH - 1)),
                    reads=["ehb", "prodb"], writes=["gps"], tag="g_mm")
            a.op("vector", lambda e: e.tensor_copy(grows[:, :], gps[:, :]),
                 reads=["gps"], writes=["grows"], tag="g_ev")
            for oc in range(FCH):
                a.op("tensor", lambda e, oc=oc: e.matmul(
                    bps[:, :], selb[:, oc, :], grows[:, :],
                    start=True, stop=True),
                    reads=["selb", "grows"], writes=["bps"], tag="g_selmm")
                if s2 == 0:
                    a.op("vector", lambda e, oc=oc, s2cols=s2cols:
                         e.tensor_tensor(outacc[:, oc, :],
                                         vw[:, oc, s2cols], bps[:, :],
                                         ALU.mult),
                         reads=vwkeys + ["bps"], writes=[f"oa/{oc}"],
                         tag="g_out0")
                else:
                    a.op("vector", lambda e, oc=oc, s2cols=s2cols:
                         e.tensor_tensor(pr2[:, :], vw[:, oc, s2cols],
                                         bps[:, :], ALU.mult),
                         reads=vwkeys + ["bps"], writes=["pr2"], tag="g_outm")
                    a.op("vector", lambda e, oc=oc: e.tensor_add(
                        outacc[:, oc, :], outacc[:, oc, :], pr2[:, :]),
                        reads=["pr2", f"oa/{oc}"], writes=[f"oa/{oc}"],
                        tag="g_outa")
        for oc in range(FCH):
            a.op("tensor", lambda e, oc=oc, scols=scols: e.matmul(
                bps[:, :], selb[:, oc, :], rrows[:, scols],
                start=True, stop=True),
                reads=["selb"] + rrkeys, writes=["bps"], tag="r_selmm")
            a.op("vector", lambda e, oc=oc: e.tensor_tensor(
                yTb[:, oc, :], outacc[:, oc, :], bps[:, :], ALU.mult),
                reads=[f"oa/{oc}", "bps"], writes=[f"y/{oc}"], tag="r_gate")
        emit_weight_cast(a, wod, wsb, FCH, "wsb") if s == 0 else None
        ykeys = [f"y/{oc}" for oc in range(FCH)]
        for i in range(4):
            t = s * 4 + i
            xb_ = 0
            a.dma("sync", xbuf[xb_][:, :], xd[t * 128:(t + 1) * 128, :],
                  reads=[f"xb/{xb_}"], writes=[f"xb/{xb_}"], tag="x_ld")
            for dh in range(2):
                for hc in range(FCH):
                    a.op("tensor", lambda e, i=i, dh=dh, hc=hc: e.matmul(
                        ops2[:, :], yTb[:, hc, i * 128:(i + 1) * 128],
                        wsb[:, hc, dh * 512:(dh + 1) * 512],
                        start=(hc == 0), stop=False),
                        reads=ykeys + [f"wsb/{hc}"], writes=["ops2"],
                        tag="o_mm")
                a.op("tensor", lambda e, dh=dh: e.matmul(
                    ops2[:, :], onesr[0:1, :],
                    bob[0:1, dh * 512:(dh + 1) * 512],
                    start=False, stop=True),
                    reads=["onesr", "bob"], writes=["ops2"], tag="o_bias")
                a.op("vector", lambda e, dh=dh, xb_=xb_:
                     e.scalar_tensor_tensor(
                         xbuf[xb_][:, dh * 512:(dh + 1) * 512], ops2[:, :],
                         1.0, xbuf[xb_][:, dh * 512:(dh + 1) * 512],
                         ALU.mult, ALU.add),
                     reads=["ops2", f"xb/{xb_}"], writes=[f"xb/{xb_}"],
                     tag="o_resid")
            a.dma("gpsimd", xo[t * 128:(t + 1) * 128, :], xbuf[xb_][:, :],
                  reads=[f"xb/{xb_}"], writes=[f"xb/{xb_}", "xo_st"],
                  tag="x_st")
    a.finish(["xo_st"])
    return a.build()



def build_f1(debug=False):
    a = ASB("f1")
    xd = a.dram_in("xs", [ROWS, C])
    t1d = a.dram_in("t1r", [HCH, 128, C], BF16)
    t2d = a.dram_in("t2", [HID, C], BF16)
    mwd = a.dram_in("mw", [1, 2])
    cwd = a.dram_in("convw", [128, HCH, 3])
    cbd = a.dram_in("convb", [128, HCH])
    halod = a.dram_in("halox", [8, C])
    identd = a.dram_in("ident", [128, 128], BF16)
    xo = a.dram_out("xo", [ROWS, C])

    ident = a.sb("identsb", [128, 128], BF16)
    xt_ = [a.sb(f"xt_{i}", [128, C], F32) for i in range(2)]
    bn6 = a.sb("bn6p", [128, 12], F32)
    mv = a.sb("mvp", [128, 2], F32)
    lncol = a.sb("lncol", [128, 2], F32)
    ylt = a.sb("ylt", [128, C], F32)
    yi = a.sb("yi", [128, C], I32)
    nsb = a.sb("nsb", [128, NTILE, C], BF16)
    nT = a.sb("nT", [128, FCH, ROWS], BF16)
    mxc = a.sb("mxc", [128, NTILE], F32)
    scol = a.sb("scol", [128, NTILE], F32)
    hbcol = a.sb("hbcol", [128, NTILE], F32)
    mwcol = a.sb("mwcol", [128, 2], F32)
    mwsb = a.sb("mwsb", [1, 2], F32)
    onesrf = a.sb("onesrf", [1, 128], F32)
    hx = a.sb("hx", [8, C], F32)
    hbn = a.sb("hbn", [8, 12], F32)
    hmv = a.sb("hmv", [8, 2], F32)
    hln = a.sb("hln", [8, 2], F32)
    hy = a.sb("hy", [8, C], F32)
    hyi = a.sb("hyi", [8, C], I32)
    hnb = a.sb("hnb", [8, C], BF16)
    hmx = a.sb("hmx", [8, 1], F32)
    hsc = a.sb("hsc", [8, 1], F32)
    hhb = a.sb("hhb", [8, 1], F32)
    nTh = a.sb("nTh", [128, FCH, 8], BF16)
    hrow = a.sb("hrow", [1, 8], F32)
    srow = a.sb("srow", [1, ROWS], F32)
    s2row = a.sb("s2row", [1, 1024], F32)
    hh8 = a.sb("hh8", [128, 8], F32)
    sbb = a.sb("sbb", [128, 1024], F32)
    s2bb = a.sb("s2bb", [128, 1024], F32)
    shalo = a.sb("shalo", [128, 8], F32)
    t1b = [a.sb(f"t1b_{i}", [128, C], BF16) for i in range(2)]
    t2b = [a.sb(f"t2b_{i}", [128, 512], BF16) for i in range(2)]
    cw = a.sb("cwsb", [128, HCH, 3], F32)
    cbc = a.sb("cbsb2", [128, HCH], F32)
    hbuf = a.sb("hbuf", [128, 2, 514], F32)
    cvb = a.sb("cvb", [128, 512], F32)
    shbuf = a.sb("shbuf", [128, HCH, 1024], BF16)
    mh = a.sb("mh", [128, 1024], F32)
    mhb = a.sb("mhb", [128, 1024], BF16)
    mx2 = a.sb("mx2", [128, 8], F32)
    s2col = a.sb("s2col", [128, 8], F32)
    s2inv = a.sb("s2inv", [128, 8], F32)
    xbuf = a.sb("xbuf_0", [128, C], F32)
    bounce = a.dram_tmp("bounce", [128, NTILE])
    bounce2 = a.dram_tmp("bounce2", [128, 8])
    hps = [a.ps(f"hps_{i}", [128, 512], F32) for i in range(2)]
    hhps = a.ps("hhps", [128, 8], F32)
    bcps = a.ps("bcps", [128, 512], F32)
    ops = [a.ps(f"ops_{i}", [128, 512], F32) for i in range(2)]
    tpbanks = [a.ps("tpA", [128, 512], BF16), a.ps("tpB", [128, 512], BF16)]
    tps = tpbanks[0]

    a.dma("sync", ident[:], identd[:, :], writes=["ident"], tag="ld_ident")
    a.dma("sync", cw[:, :, :], cwd[:, :, :], writes=["cw"], tag="ld_cw")
    a.dma("sync", cbc[:, :], cbd[:, :], writes=["cb"], tag="ld_cb")
    a.dma("sync", mwsb[0:1, :], mwd[0:1, :], writes=["mw"], tag="ld_mw")
    a.op("vector", lambda e: e.memset(onesrf[0:1, :], 1.0), writes=["onesrf"])
    a.op("tensor", lambda e: e.matmul(bcps[:, 0:2], onesrf[0:1, :],
                                      mwsb[0:1, 0:2], start=True, stop=True),
         reads=["onesrf", "mw"], writes=["bcps"], tag="mw_bc")
    a.op("vector", lambda e: e.tensor_copy(mwcol[:, :], bcps[:, 0:2]),
         reads=["bcps"], writes=["mwcol"], tag="mw_ev")

    for t in range(NTILE):
        b = t % 2
        a.dma("sync", xt_[b][:, :], xd[t * 128:(t + 1) * 128, :],
              reads=[f"xt/{b}"], writes=[f"xt/{b}"], tag="x_ld")
        for s in range(2):
            a.op("vector", lambda e, s=s, b=b: e.bn_stats(
                bn6[:, s * 6:(s + 1) * 6], xt_[b][:, s * 512:(s + 1) * 512]),
                reads=[f"xt/{b}"], writes=[f"bn6/{s}"], tag="bn")
        a.op("vector", lambda e: e.bn_aggr(mv[:, 0:2], bn6[:, 0:12]),
             reads=["bn6/0", "bn6/1"], writes=["mv"], tag="bnag")
        a.op("vector", lambda e: e.tensor_scalar(
            lncol[:, 0:1], mv[:, 1:2], EPS_LN, None, ALU.add),
            reads=["mv"], writes=["lntmp"], tag="ln_eps")
        a.op("scalar", lambda e: e.activation(
            lncol[:, 0:1], lncol[:, 0:1], AF.Ln, bias=np.float32(0.0),
            scale=np.float32(1.0)),
            reads=["lntmp"], writes=["lnl"], tag="ln_ln")
        a.op("scalar", lambda e: e.activation(
            lncol[:, 0:1], lncol[:, 0:1], AF.Exp, bias=np.float32(0.0),
            scale=np.float32(-0.5)),
            reads=["lnl"], writes=["lnsc"], tag="ln_exp")
        a.op("vector", lambda e: e.tensor_tensor(
            lncol[:, 1:2], mv[:, 0:1], lncol[:, 0:1], ALU.mult),
            reads=["mv", "lnsc"], writes=["lnb0"], tag="ln_b0")
        a.op("vector", lambda e: e.tensor_scalar(
            lncol[:, 1:2], lncol[:, 1:2], -1.0, None, ALU.mult),
            reads=["lnb0"], writes=["lnbv"], tag="ln_b1")
        a.op("scalar", lambda e, t=t, b=b: e.activation(
            ylt[:, :], xt_[b][:, :], AF.Identity, bias=lncol[:, 1:2],
            scale=lncol[:, 0:1]),
            reads=[f"xt/{b}", "lnsc", "lnbv"], writes=["ylt"], tag="ln_ap")
        a.op("vector", lambda e, t=t: e.reduce_max(
            mxc[:, t:t + 1], ylt[:, :], AX.X, apply_absolute_value=True),
            reads=["ylt"], writes=[f"mx/{t}"], tag="q_max")
        a.op("vector", lambda e, t=t: e.tensor_scalar(
            scol[:, t:t + 1], mxc[:, t:t + 1], 1e-5, None, ALU.max),
            reads=[f"mx/{t}"], writes=[f"mxc/{t}"], tag="q_clip")
        a.op("vector", lambda e, t=t: e.tensor_scalar(
            hbcol[:, t:t + 1], scol[:, t:t + 1], mwcol[:, 0:1],
            float(1.0 / 127), ALU.mult, ALU.mult),
            reads=[f"mxc/{t}", "mwcol"], writes=[f"hb/{t}"], tag="q_hb")
        a.op("vector", lambda e, t=t: e.reciprocal(
            scol[:, t:t + 1], scol[:, t:t + 1]),
            reads=[f"mxc/{t}"], writes=[f"sc0/{t}"], tag="q_rec")
        a.op("vector", lambda e, t=t: e.tensor_scalar(
            scol[:, t:t + 1], scol[:, t:t + 1], 127.0, None, ALU.mult),
            reads=[f"sc0/{t}"], writes=[f"sc/{t}"], tag="q_s")
        a.op("vector", lambda e, t=t: e.tensor_scalar(
            ylt[:, :], ylt[:, :], scol[:, t:t + 1], None, ALU.mult),
            reads=["ylt", f"sc/{t}"], writes=["ylt"], tag="q_mul")
        a.op("vector", lambda e: e.tensor_copy(yi[:, :], ylt[:, :]),
             reads=["ylt"], writes=["yi"], tag="q_rnd")
        a.op("vector", lambda e, t=t: e.tensor_scalar(
            nsb[:, t, :], yi[:, :], 127.0, -128.0, ALU.min, ALU.max),
            reads=["yi"], writes=[f"n/{t}"], tag="q_clip2")

    emit_transposes(a, nsb, nT, NTILE, ident, lambda t: f"n/{t}",
                    lambda t, c: f"nT/{t}/{c}", banks=tpbanks,
                    bankkeys=["tps", "tpsB"])

    a.dma("sync", hx[:, :], halod[:, :], writes=["hx"], tag="h_ld")
    for s in range(2):
        a.op("vector", lambda e, s=s: e.bn_stats(
            hbn[:, s * 6:(s + 1) * 6], hx[:, s * 512:(s + 1) * 512]),
            reads=["hx"], writes=[f"hbn/{s}"], tag="h_bn")
    a.op("vector", lambda e: e.bn_aggr(hmv[:, 0:2], hbn[:, 0:12]),
         reads=["hbn/0", "hbn/1"], writes=["hmv"], tag="h_bnag")
    a.op("vector", lambda e: e.tensor_scalar(
        hln[:, 0:1], hmv[:, 1:2], EPS_LN, None, ALU.add),
        reads=["hmv"], writes=["hlnt"], tag="h_eps")
    a.op("scalar", lambda e: e.activation(hln[:, 0:1], hln[:, 0:1], AF.Ln,
                                          bias=np.float32(0.0),
                                          scale=np.float32(1.0)),
         reads=["hlnt"], writes=["hlnl"], tag="h_ln")
    a.op("scalar", lambda e: e.activation(hln[:, 0:1], hln[:, 0:1], AF.Exp,
                                          bias=np.float32(0.0),
                                          scale=np.float32(-0.5)),
         reads=["hlnl"], writes=["hlns"], tag="h_exp")
    a.op("vector", lambda e: e.tensor_tensor(hln[:, 1:2], hmv[:, 0:1],
                                             hln[:, 0:1], ALU.mult),
         reads=["hmv", "hlns"], writes=["hlb0"], tag="h_b0")
    a.op("vector", lambda e: e.tensor_scalar(hln[:, 1:2], hln[:, 1:2], -1.0,
                                             None, ALU.mult),
         reads=["hlb0"], writes=["hlbv"], tag="h_b1")
    a.op("scalar", lambda e: e.activation(hy[:, :], hx[:, :], AF.Identity,
                                          bias=hln[:, 1:2],
                                          scale=hln[:, 0:1]),
         reads=["hx", "hlns", "hlbv"], writes=["hy"], tag="h_ap")
    a.op("vector", lambda e: e.reduce_max(hmx[:, 0:1], hy[:, :], AX.X,
                                          apply_absolute_value=True),
         reads=["hy"], writes=["hmx"], tag="h_max")
    a.op("vector", lambda e: e.tensor_scalar(hmx[:, 0:1], hmx[:, 0:1], 1e-5,
                                             None, ALU.max),
         reads=["hmx"], writes=["hmx"], tag="h_clip")
    a.op("vector", lambda e: e.tensor_scalar(
        hhb[:, 0:1], hmx[:, 0:1], mwcol[0:8, 0:1], float(1.0 / 127),
        ALU.mult, ALU.mult),
        reads=["hmx", "mwcol"], writes=["hhb"], tag="h_hb")
    a.op("vector", lambda e: e.reciprocal(hsc[:, 0:1], hmx[:, 0:1]),
         reads=["hmx"], writes=["hsc"], tag="h_rec")
    a.op("vector", lambda e: e.tensor_scalar(hsc[:, 0:1], hsc[:, 0:1], 127.0,
                                             None, ALU.mult),
         reads=["hsc"], writes=["hsc"], tag="h_s")
    a.op("vector", lambda e: e.tensor_scalar(hy[:, :], hy[:, :],
                                             hsc[:, 0:1], None, ALU.mult),
         reads=["hy", "hsc"], writes=["hy"], tag="h_mul")
    a.op("vector", lambda e: e.tensor_copy(hyi[:, :], hy[:, :]),
         reads=["hy"], writes=["hyi"], tag="h_rnd")
    a.op("vector", lambda e: e.tensor_scalar(hnb[:, :], hyi[:, :], 127.0,
                                             -128.0, ALU.min, ALU.max),
         reads=["hyi"], writes=["hnb"], tag="h_q")
    for c in range(FCH):
        a.op("tensor", lambda e, c=c: e.transpose(
            tps[:, 0:8], hnb[:, c * 128:(c + 1) * 128], ident[0:8, 0:8]),
            reads=["hnb", "ident"], writes=["tps"], tag="h_tr")
        a.op("vector", lambda e, c=c: e.tensor_copy(nTh[:, c, :],
                                                    tps[:, 0:8]),
            reads=["tps"], writes=[f"nTh/{c}"], tag="h_trev")
    a.dma("gpsimd", bounce2[0:8, 0:1], hhb[:, 0:1], reads=["hhb"],
          writes=["hbb1"], tag="h_bounce1",
          allow_slow_non_contiguous=True)
    a.dma("sync", hrow[0:1, 0:8], bounce2[0:8, 0:1].rearrange("a () -> () a"),
          reads=["hbb1"], writes=["hrow"], tag="h_bounce2",
          allow_slow_non_contiguous=True)
    a.op("tensor", lambda e: e.matmul(bcps[:, 0:8], onesrf[0:1, :],
                                      hrow[0:1, 0:8], start=True, stop=True),
         reads=["hrow", "onesrf"], writes=["bcps"], tag="h_bc")
    a.op("vector", lambda e: e.tensor_copy(shalo[:, 0:8], bcps[:, 0:8]),
         reads=["bcps"], writes=["shalo"], tag="h_bcev")

    a.dma("gpsimd", bounce[:, :], hbcol[:, :],
          reads=[f"hb/{t}" for t in range(NTILE)], writes=["bb1"],
          tag="s_bounce1")
    a.dma("sync", srow[0:1, :].rearrange("a (t p) -> a t p", p=128),
          bounce.rearrange("p t -> () t p"),
          reads=["bb1"], writes=["srow"], tag="s_bounce2",
          allow_slow_non_contiguous=True)

    for bh in range(2):
        base = bh * 1024
        for g in range(2):
            a.op("tensor", lambda e, g=g: e.matmul(
                bcps[:, :], onesrf[0:1, :],
                srow[0:1, base + g * 512:base + (g + 1) * 512],
                start=True, stop=True),
                reads=["srow", "onesrf"], writes=["bcps"], tag="sb_bc")
            a.op("vector", lambda e, g=g: e.tensor_copy(
                sbb[:, g * 512:(g + 1) * 512], bcps[:, :]),
                reads=["bcps"], writes=["sbb"], tag="sb_ev")
        for hc in range(HCH):
            tb = hc % 2
            a.dma("sync", t1b[tb][:, :], t1d[hc, :, :],
                  reads=[f"t1b/{tb}"], writes=[f"t1b/{tb}"], tag="t1_ld")
            for g in range(2):
                pg = g % 2
                for dc in range(FCH):
                    a.op("tensor", lambda e, hc=hc, dc=dc, g=g, tb=tb, pg=pg,
                         base=base: e.matmul(
                             hps[pg][:, :], t1b[tb][:, dc * 128:(dc + 1) * 128],
                             nT[:, dc, base + g * 512:base + (g + 1) * 512],
                             start=(dc == 0), stop=(dc == FCH - 1)),
                         reads=[f"t1b/{tb}"] +
                         [f"nT/{t}/{dc}" for t in range(bh * 8 + g * 4,
                                                        bh * 8 + g * 4 + 4)],
                         writes=[f"hps/{pg}"], tag="mm1")
                bb = g
                a.op("vector", lambda e, g=g, pg=pg: e.tensor_tensor(
                    hbuf[:, g, 1:513], hps[pg][:, :],
                    sbb[:, g * 512:(g + 1) * 512], ALU.mult),
                    reads=[f"hps/{pg}", "sbb"], writes=[f"hbuf/{g}"],
                    tag="h_evac")
            for dc in range(FCH):
                a.op("tensor", lambda e, hc=hc, dc=dc, tb=tb: e.matmul(
                    hhps[:, 0:8], t1b[tb][:, dc * 128:(dc + 1) * 128],
                    nTh[:, dc, :], start=(dc == 0), stop=(dc == FCH - 1)),
                    reads=[f"t1b/{tb}", f"nTh/{dc}"], writes=["hhps"],
                    tag="mm1h")
            a.op("vector", lambda e: e.tensor_tensor(
                hh8[:, 0:8], hhps[:, 0:8], shalo[:, 0:8], ALU.mult),
                reads=["hhps", "shalo"], writes=["hh"], tag="h_hevac")
            for g in range(2):
                b_glob = bh * 2 + g
                a.op("vector", lambda e, g=g, b_glob=b_glob: e.tensor_copy(
                    hbuf[:, g, 0:1], hh8[:, 2 * b_glob:2 * b_glob + 1]),
                    reads=["hh"], writes=[f"hbuf/{g}"], tag="h_hl")
                a.op("vector", lambda e, g=g, b_glob=b_glob: e.tensor_copy(
                    hbuf[:, g, 513:514], hh8[:, 2 * b_glob + 1:2 * b_glob + 2]),
                    reads=["hh"], writes=[f"hbuf/{g}"], tag="h_hr")
            for g in range(2):
                a.op("vector", lambda e, hc=hc, g=g: e.tensor_scalar(
                    cvb[:, 0:512], hbuf[:, g, 0:512], cw[:, hc, 0:1],
                    cbc[:, hc:hc + 1], ALU.mult, ALU.add),
                    reads=[f"hbuf/{g}", "cw", "cb"], writes=["cvb"],
                    tag="conv0")
                a.op("vector", lambda e, hc=hc, g=g: e.scalar_tensor_tensor(
                    cvb[:, 0:512], hbuf[:, g, 1:513], cw[:, hc, 1:2],
                    cvb[:, 0:512], ALU.mult, ALU.add),
                    reads=[f"hbuf/{g}", "cw", "cvb"], writes=["cvb"],
                    tag="conv1")
                a.op("vector", lambda e, hc=hc, g=g: e.scalar_tensor_tensor(
                    cvb[:, 0:512], hbuf[:, g, 2:514], cw[:, hc, 2:3],
                    cvb[:, 0:512], ALU.mult, ALU.add),
                    reads=[f"hbuf/{g}", "cw", "cvb"], writes=["cvb"],
                    tag="conv2")
                a.op("scalar", lambda e, hc=hc, g=g: e.activation(
                    shbuf[:, hc, g * 512:(g + 1) * 512], cvb[:, 0:512],
                    AF.Silu, bias=np.float32(0.0), scale=np.float32(1.0)),
                    reads=["cvb"], writes=[f"sh/{hc}"], tag="silu")
            if hc == 0:
                a.op("vector", lambda e: e.memset(mh[:, :], 0.0),
                     reads=["mhk"], writes=["mhk"], tag="mh0")
            a.op("vector", lambda e, hc=hc: e.scalar_tensor_tensor(
                ylt[:, :], shbuf[:, hc, :], -1.0, shbuf[:, hc, :],
                ALU.mult, ALU.max),
                reads=[f"sh/{hc}"], writes=["ylt"], tag="mh_abs")
            a.op("vector", lambda e: e.tensor_max(mh[:, :], mh[:, :],
                                                  ylt[:, :]),
                reads=["ylt", "mhk"], writes=["mhk"], tag="mh_acc")
        a.op("vector", lambda e: e.tensor_copy(mhb[:, :], mh[:, :]),
             reads=["mhk"], writes=["mhb"], tag="mh_cast")
        for i in range(8):
            a.op("tensor", lambda e, i=i: e.transpose(
                tps[:, 0:128], mhb[:, i * 128:(i + 1) * 128], ident[:]),
                reads=["mhb", "ident"], writes=["tps"], tag="mh_tr")
            a.op("vector", lambda e, i=i: e.reduce_max(
                mx2[:, i:i + 1], tps[:, 0:128], AX.X),
                reads=["tps"], writes=[f"mx2/{i}"], tag="mh_red")
        mx2keys = [f"mx2/{i}" for i in range(8)]
        a.op("vector", lambda e: e.tensor_scalar(
            mx2[:, :], mx2[:, :], 1e-5, None, ALU.max),
            reads=mx2keys, writes=["mx2c"], tag="s2_clip")
        a.op("vector", lambda e: e.tensor_scalar(
            s2inv[:, :], mx2[:, :], mwcol[:, 1:2], float(1.0 / 127),
            ALU.mult, ALU.mult),
            reads=["mx2c", "mwcol"], writes=["s2inv"], tag="s2_inv")
        a.op("vector", lambda e: e.reciprocal(s2col[:, :], mx2[:, :]),
             reads=["mx2c"], writes=["s2r"], tag="s2_rec")
        a.op("vector", lambda e: e.tensor_scalar(
            s2col[:, :], s2col[:, :], 127.0, None, ALU.mult),
            reads=["s2r"], writes=["s2col"], tag="s2_s")
        a.dma("gpsimd", bounce2[:, 0:8], s2col[:, :], reads=["s2col"],
              writes=["bb2"], tag="s2_bounce1")
        a.dma("sync", s2row[0:1, 0:1024].rearrange("a (t p) -> a t p", p=128),
              bounce2[:, 0:8].rearrange("p t -> () t p"),
              reads=["bb2", "s2row"], writes=["s2row"], tag="s2_bounce2",
              allow_slow_non_contiguous=True)
        for g in range(2):
            a.op("tensor", lambda e, g=g: e.matmul(
                bcps[:, :], onesrf[0:1, :],
                s2row[0:1, g * 512:(g + 1) * 512], start=True, stop=True),
                reads=["s2row", "onesrf"], writes=["bcps"], tag="s2_bc")
            a.op("vector", lambda e, g=g: e.tensor_copy(
                s2bb[:, g * 512:(g + 1) * 512], bcps[:, :]),
                reads=["bcps"], writes=["s2bb"], tag="s2_ev")
        for hc in range(HCH):
            a.op("vector", lambda e, hc=hc: e.tensor_tensor(
                ylt[:, :], shbuf[:, hc, :], s2bb[:, :], ALU.mult),
                reads=[f"sh/{hc}", "s2bb"], writes=["ylt"], tag="q2_mul")
            a.op("vector", lambda e: e.tensor_copy(yi[:, :], ylt[:, :]),
                 reads=["ylt"], writes=["yi"], tag="q2_rnd")
            a.op("vector", lambda e, hc=hc: e.tensor_scalar(
                shbuf[:, hc, :], yi[:, :], 127.0, -128.0, ALU.min, ALU.max),
                reads=["yi"], writes=[f"sh/{hc}"], tag="q2_clip")
        for tg2 in range(4):
            for dh in range(2):
                for hc in range(HCH):
                    tb = hc % 2
                    a.dma("scalar", t2b[tb][:, :],
                          t2d[hc * 128:(hc + 1) * 128,
                              dh * 512:(dh + 1) * 512],
                          reads=[f"t2b/{tb}"], writes=[f"t2b/{tb}"],
                          tag="t2_ld")
                    for i in range(2):
                        tt = tg2 * 2 + i
                        a.op("tensor", lambda e, hc=hc, i=i, tt=tt, tb=tb:
                             e.matmul(ops[i][:, :],
                                      shbuf[:, hc, tt * 128:(tt + 1) * 128],
                                      t2b[tb][:, :],
                                      start=(hc == 0), stop=(hc == HCH - 1)),
                             reads=[f"t2b/{tb}"] +
                             [f"sh/{hcx}" for hcx in range(HCH)],
                             writes=[f"ops/{i}"], tag="mm2")
                for i in range(2):
                    tt = tg2 * 2 + i
                    t_glob = bh * 8 + tt
                    a.dma("sync", xbuf[:, :],
                          xd[t_glob * 128:(t_glob + 1) * 128, :],
                          reads=["xbuf"], writes=["xbuf"], tag="x2_ld")
                    a.op("vector", lambda e, i=i, tt=tt, dh=dh:
                         e.scalar_tensor_tensor(
                             xbuf[:, dh * 512:(dh + 1) * 512], ops[i][:, :],
                             s2inv[:, tt:tt + 1],
                             xbuf[:, dh * 512:(dh + 1) * 512],
                             ALU.mult, ALU.add),
                         reads=[f"ops/{i}", "xbuf", "s2inv"],
                         writes=["xbuf"], tag="o_resid")
                    a.dma("gpsimd",
                          xo[t_glob * 128:(t_glob + 1) * 128,
                             dh * 512:(dh + 1) * 512],
                          xbuf[:, dh * 512:(dh + 1) * 512],
                          reads=["xbuf"], writes=["xbuf", "xo_st"],
                          tag="x_st")
    if debug:
        nd = a.dram_out("dbg_n", [ROWS, C], BF16)
        hd = a.dram_out("dbg_h", [128, 2, 514])
        shd = a.dram_out("dbg_sh", [HID, 1024], BF16)
        s2d = a.dram_out("dbg_s2inv", [128, 8])
        sbd = a.dram_out("dbg_sbb", [128, 1024])
        a.dma("gpsimd", nd.rearrange("(n p) f -> p n f", p=128), nsb[:, :, :],
              reads=[f"n/{t}" for t in range(NTILE)], writes=["dbg"],
              tag="dbg")
        a.dma("gpsimd", hd[:, :, :], hbuf[:, :, :],
              reads=["hbuf/0", "hbuf/1"], writes=["dbg"], tag="dbg")
        a.dma("gpsimd", shd.rearrange("(hc p) f -> p hc f", p=128),
              shbuf[:, :, :],
              reads=[f"sh/{hc}" for hc in range(HCH)], writes=["dbg"],
              tag="dbg")
        a.dma("gpsimd", s2d[:, :], s2inv[:, :], reads=["s2inv"],
              writes=["dbg"], tag="dbg")
        a.dma("gpsimd", sbd[:, :], sbb[:, :], reads=["sbb"], writes=["dbg"],
              tag="dbg")
        a.finish(["xo_st", "dbg"])
    else:
        a.finish(["xo_st"])
    return a.build()



_PROGS = {}


def _prog(name, builder):
    if name not in _PROGS:
        _PROGS[name] = builder()
    return _PROGS[name]


def _consts():
    ident = np.eye(128, dtype=BFNP)
    eh = np.zeros((128, FCH, 16), np.float32)
    for f in range(C):
        eh[f % 128, f // 128, f // 64] = 1.0
    sel = np.zeros((16, FCH, 128), np.float32)
    for oc in range(FCH):
        sel[2 * oc, oc, 0:64] = 1.0
        sel[2 * oc + 1, oc, 64:128] = 1.0
    masks = []
    for c in range(NCORES):
        kk = np.arange(T)[:, None]
        qq = TPC * c + np.arange(TPC)[None, :]
        m = (kk <= qq).astype(BFNP).reshape(T // 128, 128, TPC)
        masks.append(np.ascontiguousarray(m))
    return ident, eh, sel, masks


def _run(prog, maps):
    res = run_bass_kernel_spmd(prog.nc, maps, core_ids=list(range(NCORES)))
    return res.results


def _colsplit(v, n=128):
    v = np.asarray(v, np.float32)
    return np.ascontiguousarray(v.reshape(-1, n).T)


def kernel(**inputs):
    inp = {k: np.asarray(v, np.float32) for k, v in inputs.items()}
    ident, eh, sel, masks = _consts()
    x = np.ascontiguousarray(inp["L0_attn_a_x"], dtype=np.float32)

    w0p = _prog("w0", build_w0)
    m = {
        "w1t": np.ascontiguousarray(inp["L0_ffn_w1"].T),
        "w2t": np.ascontiguousarray(inp["L0_ffn_w2"].T),
        "b1": inp["L0_ffn_b1"].reshape(1, -1),
        "b2": inp["L0_ffn_b2"].reshape(1, -1),
        "v1t": np.ascontiguousarray(inp["L1_ffn_w1"].T),
        "v2t": np.ascontiguousarray(inp["L1_ffn_w2"].T),
    }
    w0 = run_bass_kernel_spmd(w0p.nc, [m], core_ids=[0]).results[0]
    m1, m2, mb1c, b2q = _f0_host_inputs(w0["m1t"], w0["m2t"], w0["mb1"],
                                        w0["b2q1024"])
    t1r = np.ascontiguousarray(
        np.asarray(w0["t1t"]).reshape(FCH, 128, HCH, 128)
        .transpose(2, 1, 0, 3)).reshape(HCH, 128, C)
    t2 = np.ascontiguousarray(np.asarray(w0["t2t"]))
    mw = np.asarray(w0["mw"])
    convw = np.ascontiguousarray(
        inp["L1_ffn_conv_w"].reshape(HID, 3).reshape(HCH, 128, 3)
        .transpose(1, 0, 2))
    convb = _colsplit(inp["L1_ffn_conv_b"])

    def run_f0(x):
        shards = _shard_rows(x)
        maps = [{"xs": s, "m1": m1, "m2": m2, "mb1c": mb1c, "b2q": b2q,
                 "ident": ident} for s in shards]
        res = _run(_prog("f0", build_f0), maps)
        return _unshard_rows([r["xo"] for r in res])

    def run_f1(x):
        shards = _shard_rows(x)
        halos = []
        for c in range(NCORES):
            hh = np.zeros((8, C), np.float32)
            for b in range(B):
                tl = c * TPC - 1
                tr = (c + 1) * TPC
                if tl >= 0:
                    hh[b * 2 + 0] = x[b, tl]
                if tr < T:
                    hh[b * 2 + 1] = x[b, tr]
            halos.append(hh)
        maps = [{"xs": shards[c], "t1r": t1r, "t2": t2, "mw": mw,
                 "convw": convw, "convb": convb, "halox": halos[c],
                 "ident": ident} for c in range(NCORES)]
        res = _run(_prog("f1", build_f1), maps)
        return _unshard_rows([r["xo"] for r in res])

    shards = _shard_rows(x)
    wukc = np.ascontiguousarray(
        inp["L0_attn_a_w_kv_up"][:, :1024].T.reshape(FCH, 128, 128)
        .transpose(1, 0, 2))
    maps = [{"xs": s, "wd": inp["L0_attn_a_w_kv_down"], "wukT": wukc,
             "wuv": np.ascontiguousarray(inp["L0_attn_a_w_kv_up"][:, 1024:]),
             "wq": inp["L0_attn_a_w_q"], "ident": ident} for s in shards]
    res0 = _run(_prog("a0", build_a0), maps)
    latg = np.concatenate([r["lat"].reshape(128, B, TPC) for r in res0],
                          axis=2).reshape(128, B * T)
    vg = np.concatenate([r["vv"].reshape(B, TPC, 16 * 65) for r in res0],
                        axis=1)
    latmax8 = np.concatenate([r["latmax"] for r in res0], axis=1)
    maps = [{"xs": shards[c], "qat": res0[c]["qat"], "latg": latg, "vg": vg,
             "qamax": res0[c]["qamax"], "latmax8": latmax8,
             "wo": inp["L0_attn_a_w_out"], "masks": masks[c]}
            for c in range(NCORES)]
    res = _run(_prog("a1", build_a1), maps)
    x = _unshard_rows([r["xo"] for r in res])

    x = run_f0(x)

    shards = _shard_rows(x)
    bqkv = inp["L0_attn_b_b_qkv"]
    maps = [{"xs": s, "wqkv": inp["L0_attn_b_w_qkv"],
             "bqc": _colsplit(bqkv[:C]), "bkc": _colsplit(bqkv[C:2 * C]),
             "bvr": bqkv[2 * C:].reshape(1, C), "eh": eh, "ident": ident}
            for s in shards]
    res0 = _run(_prog("b0", build_b0), maps)
    kaugg = np.ascontiguousarray(np.concatenate(
        [r["kaug"].reshape(16, 68, B, TPC) for r in res0], axis=3))
    vg = np.concatenate([r["vv"].reshape(B, TPC, 16 * 65) for r in res0],
                        axis=1)
    maps = [{"xs": shards[c], "qaug": res0[c]["qaug"], "kaugg": kaugg,
             "vg": vg, "wo": inp["L0_attn_b_w_out"],
             "bor": inp["L0_attn_b_b_out"].reshape(1, C),
             "masks": masks[c], "ident": ident} for c in range(NCORES)]
    res = _run(_prog("b1", build_b1), maps)
    x = _unshard_rows([r["xo"] for r in res])

    x = run_f0(x)

    shards = _shard_rows(x)
    maps = [{"xs": s,
             "wqT": np.ascontiguousarray(inp["L1_attn_a_Wq_w"].T),
             "wkT": np.ascontiguousarray(inp["L1_attn_a_Wk_w"].T),
             "wvT": np.ascontiguousarray(inp["L1_attn_a_Wv_w"].T),
             "woT": np.ascontiguousarray(inp["L1_attn_a_Wo_w"].T),
             "wwT": np.ascontiguousarray(inp["L1_attn_a_Ww_w"].T),
             "wrT": np.ascontiguousarray(inp["L1_attn_a_Wr_w"].T),
             "bqc": _colsplit(inp["L1_attn_a_Wq_b"]),
             "bkc": _colsplit(inp["L1_attn_a_Wk_b"]),
             "bvc": _colsplit(inp["L1_attn_a_Wv_b"]),
             "bor": inp["L1_attn_a_Wo_b"].reshape(1, C),
             "bwc": inp["L1_attn_a_Ww_b"].reshape(16, 1),
             "brc": inp["L1_attn_a_Wr_b"].reshape(16, 1),
             "eh": eh, "sel": sel, "ident": ident} for s in shards]
    res = _run(_prog("g", build_g), maps)
    x = _unshard_rows([r["xo"] for r in res])

    x = run_f1(x)

    shards = _shard_rows(x)
    maps = [{"xs": s, "wq": inp["L1_attn_b_Wq"], "wk": inp["L1_attn_b_Wk"],
             "wv": inp["L1_attn_b_Wv"],
             "bqc": _colsplit(inp["L1_attn_b_bq"]),
             "bkc": _colsplit(inp["L1_attn_b_bk"]),
             "bvr": inp["L1_attn_b_bv"].reshape(1, C),
             "eh": eh, "ident": ident} for s in shards]
    res0 = _run(_prog("c0", build_c0), maps)
    ktg = np.ascontiguousarray(np.concatenate(
        [r["kt"].reshape(16, 64, B, TPC) for r in res0], axis=3))
    kn2g = np.ascontiguousarray(np.concatenate(
        [r["kn2"].reshape(16, B, TPC) for r in res0], axis=2)
        .reshape(16, B * T))
    vg = np.concatenate([r["vv"].reshape(B, TPC, 16 * 65) for r in res0],
                        axis=1)
    maps = [{"xs": shards[c], "qt": res0[c]["qt"], "qn2": res0[c]["qn2"],
             "ktg": ktg, "kn2g": kn2g, "vg": vg, "wo": inp["L1_attn_b_Wo"],
             "bor": inp["L1_attn_b_bo"].reshape(1, C)}
            for c in range(NCORES)]
    res = _run(_prog("c1", build_c1), maps)
    x = _unshard_rows([r["xo"] for r in res])

    x = run_f1(x)

    return x
